# version 05
# baseline (speedup 1.0000x reference)
import math
import numpy as np
import ml_dtypes
import concourse.bass as bass
import concourse.mybir as mybir
from concourse.bass_utils import run_bass_kernel_spmd

F32 = mybir.dt.float32
BF16 = mybir.dt.bfloat16
AF = mybir.ActivationFunctionType
ALU = mybir.AluOpType
AXX = mybir.AxisListType.X

NT = 4096
DM = 1024
NCH = 8
TB = 512
NTB = NT // TB
HALF = 2048
DH = 4096
NE = 8
DKM = 128
DHEAD = 64
NHEADS = 16
DLAT = 256
CDEC = 1e-4
EPS = 1e-5
L1SCALE = 1.0 / (math.sqrt(1024.0) + 1e-8)

ENGINES = ("tensor", "vector", "scalar", "gpsimd", "sync")

_NMC = [0]


def _nm(name):
    _NMC[0] += 1
    return f"{name}_{_NMC[0]}"


NONE_MARK = (None, 0)


class Sem:
    def __init__(self, name, handle):
        self.name = name
        self.h = handle
        self.count = 0

    def mark(self):
        return (self, self.count)


class Prog:

    def __init__(self, nc):
        self.nc = nc
        self.recs = {e: [] for e in ENGINES}
        self.sems = {}
        self.known = {}

    def sem(self, name):
        if name not in self.sems:
            self.sems[name] = Sem(name, self.nc.alloc_semaphore(name=name))
        return self.sems[name]

    def do(self, eng, fn, after=(), inc=None, inc_n=1, dma=False):
        waits = []
        for a in after:
            if a is None:
                continue
            s, v = (a if isinstance(a, tuple) else (a, a.count))
            if s is None or v <= 0:
                continue
            k = (eng, s.name)
            if v > self.known.get(k, 0):
                waits.append((s, v))
                self.known[k] = v
        tag = None
        if inc is not None:
            inc.count += inc_n
            tag = (inc, inc_n)
        self.recs[eng].append((waits, fn, tag))
        return (inc, inc.count) if inc is not None else None

    def dma(self, eng, dst, src, after=(), inc=None, **kw):
        return self.do(eng, lambda e: e.dma_start(dst, src, **kw),
                       after=after, inc=inc, inc_n=16, dma=True)

    def fence(self):
        vals = [(s, s.count) for s in self.sems.values() if s.count > 0]
        for eng in ENGINES:
            waits = []
            for s, v in vals:
                k = (eng, s.name)
                if v > self.known.get(k, 0):
                    waits.append((s, v))
                    self.known[k] = v
            if waits:
                self.recs[eng].append((waits, None, None))

    def flush(self, name="blk"):
        recs = self.recs
        self.recs = {e: [] for e in ENGINES}
        if not any(recs.values()):
            return
        with self.nc.Block(_nm(name)) as blk:
            def make(rlist):
                def run(e):
                    for waits, fn, tag in rlist:
                        for (s, v) in waits:
                            e.wait_ge(s.h, v)
                        if fn is not None:
                            ins = fn(e)
                            if tag is not None:
                                ins.then_inc(tag[0].h, tag[1])
                return run
            for nm, meth in (("tensor", blk.tensor), ("vector", blk.vector),
                             ("scalar", blk.scalar), ("gpsimd", blk.gpsimd),
                             ("sync", blk.sync)):
                if recs[nm]:
                    meth(make(recs[nm]))



INPUT_SHAPES = {
    "L0_attn_x": (NT, DM),
    "L0_attn_proj_mat": (DHEAD, DKM),
    "L0_attn_Wq": (DM, DM), "L0_attn_bq": (DM,),
    "L0_attn_Wk": (DM, DM), "L0_attn_bk": (DM,),
    "L0_attn_Wv": (DM, DM), "L0_attn_bv": (DM,),
    "L0_attn_Wo": (DM, DM), "L0_attn_bo": (DM,),
    "L0_ffn_Wg": (DM, NE), "L0_ffn_bg": (NE,),
    "L0_ffn_W1": (NE, DM, DH), "L0_ffn_b1": (NE, DH),
    "L0_ffn_W2": (NE, DH, DM), "L0_ffn_b2": (NE, DM),
    "L1_attn_Wq": (DM, DM), "L1_attn_Wk": (DM, DM),
    "L1_attn_Wv": (DM, DM), "L1_attn_Wo": (DM, DM),
    "L1_attn_W_down_kv": (DM, DLAT),
    "L1_attn_W_up_k": (DLAT, DM), "L1_attn_W_up_v": (DLAT, DM),
    "L1_ffn_router_w": (NE, DM), "L1_ffn_router_b": (NE,),
    "L1_ffn_expert_w": (NE, DM, DM), "L1_ffn_expert_b": (NE, DM),
}

PART_INPUTS = {
    "prologue": {"L0_attn_x"},
    "l0ffn": {"L0_ffn_Wg", "L0_ffn_bg", "L0_ffn_W1", "L0_ffn_b1",
              "L0_ffn_W2", "L0_ffn_b2"},
    "l0attn": {"L0_attn_proj_mat", "L0_attn_Wq", "L0_attn_bq", "L0_attn_Wk",
               "L0_attn_bk", "L0_attn_Wv", "L0_attn_bv", "L0_attn_Wo",
               "L0_attn_bo"},
    "l1attn": {"L1_attn_Wq", "L1_attn_Wk", "L1_attn_Wv", "L1_attn_Wo",
               "L1_attn_W_down_kv", "L1_attn_W_up_k", "L1_attn_W_up_v"},
    "l1ffn": {"L1_ffn_router_w", "L1_ffn_router_b", "L1_ffn_expert_w",
              "L1_ffn_expert_b"},
    "l1prep": {"L1_ffn_expert_w"},
    "l0aprep": {"L0_attn_Wq", "L0_attn_Wk", "L0_attn_Wv", "L0_attn_Wo"},
    "final": set(),
}


def declare_io(nc, parts, xio="internal"):
    d = {}
    need = set()
    for part in parts:
        need |= PART_INPUTS[part.replace("_final", "")]
    for name in sorted(need):
        d[name] = nc.dram_tensor(name, list(INPUT_SHAPES[name]), F32,
                                 kind="ExternalInput").ap()
    if any(pt.endswith("_final") or pt == "final" for pt in parts):
        d["out"] = nc.dram_tensor("out", [NT, DM], F32, kind="ExternalOutput").ap()
    if any(pt.startswith("l1prep") or pt.startswith("l1ffn") for pt in parts):
        d["W1T_l1"] = nc.dram_tensor("W1T_l1", [NE, NCH, 128, DM], BF16).ap()
    if any(pt.startswith("l0a") for pt in parts):
        d["WT_l0a"] = nc.dram_tensor("WT_l0a", [4, NCH, 128, DM], BF16).ap()
    if any(pt.startswith("l0a") or pt.startswith("l1attn") for pt in parts):
        d["OT"] = nc.dram_tensor("OT", [NCH, 128, NT], BF16).ap()
    if any(pt.startswith("l1attn") for pt in parts):
        d["QTd"] = nc.dram_tensor("QTd", [NCH, 128, NT], BF16).ap()
        d["KVSd"] = nc.dram_tensor("KVSd", [NCH, 128, NT], BF16).ap()
        d["KRd"] = nc.dram_tensor("KRd", [NCH, 128, NT], BF16).ap()
        d["VRd"] = nc.dram_tensor("VRd", [NCH, 128, NT], BF16).ap()
    if xio == "internal":
        d["XLN"] = nc.dram_tensor("XLN", [NCH, 128, NT], BF16).ap()
        d["XRES"] = nc.dram_tensor("XRES", [NCH, 128, NT], F32).ap()
        d["XLN_o"], d["XRES_o"] = d["XLN"], d["XRES"]
    else:
        if xio in ("in", "inout"):
            d["XLN"] = nc.dram_tensor("XLN", [NCH, 128, NT], BF16,
                                      kind="ExternalInput").ap()
            d["XRES"] = nc.dram_tensor("XRES", [NCH, 128, NT], F32,
                                       kind="ExternalInput").ap()
        if xio in ("out", "inout"):
            d["XLN_o"] = nc.dram_tensor("XLN_o", [NCH, 128, NT], BF16,
                                        kind="ExternalOutput").ap()
            d["XRES_o"] = nc.dram_tensor("XRES_o", [NCH, 128, NT], F32,
                                         kind="ExternalOutput").ap()
        if xio == "in":
            d["XLN_o"], d["XRES_o"] = d["XLN"], d["XRES"]
        if xio == "out":
            d["XLN"], d["XRES"] = d["XLN_o"], d["XRES_o"]
    return d



class Consts:
    pass


def emit_consts(p, io, ctx, need_proj=False):
    nc = p.nc
    c = Consts()
    c.ident_bf = ctx.enter_context(nc.sbuf_tensor(_nm("c_ident_bf"), [128, 128], BF16))
    c.ident_f32 = ctx.enter_context(nc.sbuf_tensor(_nm("c_ident_f32"), [128, 128], F32))
    c.ones_col_bf = ctx.enter_context(nc.sbuf_tensor(_nm("c_ones_col_bf"), [128, 1], BF16))
    c.ones_row_f32 = ctx.enter_context(nc.sbuf_tensor(_nm("c_ones_row_f32"), [1, 128], F32))
    c.sel_bf = ctx.enter_context(nc.sbuf_tensor(_nm("c_sel_bf"), [NE, NE * 128], BF16))
    c.eps_col = ctx.enter_context(nc.sbuf_tensor(_nm("c_eps_col"), [128, 1], F32))
    c.ones_blk = ctx.enter_context(nc.sbuf_tensor(_nm("c_ones_blk"), [128, 128], F32))
    sGP = p.sem("sGP")
    sLD = p.sem("sLD")

    def ident_fill(t):
        def f(g):
            return g.affine_select(out=t[:, :], in_=t[:, :],
                                   compare_op=ALU.not_equal, fill=1.0,
                                   base=0, pattern=[[-1, 128]],
                                   channel_multiplier=1)
        return f

    p.do("gpsimd", lambda g: g.memset(c.ident_bf[:, :], 0.0))
    p.do("gpsimd", ident_fill(c.ident_bf))
    p.do("gpsimd", lambda g: g.memset(c.ident_f32[:, :], 0.0))
    p.do("gpsimd", ident_fill(c.ident_f32))
    p.do("gpsimd", lambda g: g.memset(c.ones_col_bf[:, :], 1.0))
    p.do("gpsimd", lambda g: g.memset(c.ones_row_f32[:, :], 1.0))
    p.do("gpsimd", lambda g: g.memset(c.eps_col[:, :], float(EPS)))
    p.do("gpsimd", lambda g: g.memset(c.ones_blk[:, :], 1.0))
    p.do("gpsimd", lambda g: g.memset(c.sel_bf[:, :], 0.0))
    for e in range(NE):
        def selfill(e=e):
            def f(g):
                return g.affine_select(
                    out=c.sel_bf[:, e * 128:(e + 1) * 128],
                    in_=c.sel_bf[:, e * 128:(e + 1) * 128],
                    compare_op=ALU.not_equal, fill=1.0,
                    base=-e, pattern=[[0, 128]], channel_multiplier=1)
            return f
        p.do("gpsimd", selfill(), inc=(sGP if e == NE - 1 else None))
    c.ready = sGP.mark()
    if need_proj:
        c.projpair = ctx.enter_context(nc.sbuf_tensor(_nm("c_projpair"), [128, 256], BF16))
        p.do("gpsimd", lambda g: g.memset(c.projpair[:, :], 0.0), inc=sGP)
        mz = sGP.mark()
        p.dma("gpsimd", c.projpair[0:64, 0:128], io["L0_attn_proj_mat"][:, :],
              after=[mz], inc=sLD)
        p.dma("gpsimd", c.projpair[64:128, 128:256], io["L0_attn_proj_mat"][:, :],
              after=[mz], inc=sLD)
        c.proj_ready = sLD.mark()
    return c



class EpiBufs:
    pass


def alloc_epi(nc, ctx, final=False):
    A = EpiBufs()
    A.stage = ctx.enter_context(nc.sbuf_tensor(_nm("ep_stage"), [128, 2 * TB], F32))
    A.xbf = ctx.enter_context(nc.sbuf_tensor(_nm("ep_xbf"), [128, 4096], BF16))
    A.sq = ctx.enter_context(nc.sbuf_tensor(_nm("ep_sq"), [128, 2 * TB], BF16))
    A.xr = ctx.enter_context(nc.sbuf_tensor(_nm("ep_xr"), [128, 2 * TB], F32))
    A.rows = ctx.enter_context(nc.sbuf_tensor(_nm("ep_rows"), [1, 2 * TB], F32))
    A.tmp = ctx.enter_context(nc.sbuf_tensor(_nm("ep_tmp"), [128, TB], F32))
    if final:
        A.outst = ctx.enter_context(nc.sbuf_tensor(_nm("ep_outst"), [128, 1024], F32))
    A.prev_done = NONE_MARK
    A.xr_free = [NONE_MARK, NONE_MARK]
    A.st_free = [NONE_MARK, NONE_MARK]
    return A


def emit_epilogue_tb(p, io, c, A, pe0, pe1, tb, add_src, src_after=()):
    sXR = p.sem("sXR")
    sEPs = p.sem("sEPs")
    sEPm = p.sem("sEPm")
    sEPr = p.sem("sEPr")
    sEPd = p.sem("sEPd")
    sST = p.sem("sST")

    ts, te = tb * TB, (tb + 1) * TB
    prev = A.prev_done

    for jc in range(NCH):
        sl = jc % 2
        xrs = A.xr[:, sl * TB:(sl + 1) * TB]
        sts = A.stage[:, sl * TB:(sl + 1) * TB]
        p.dma("sync", xrs, io["XRES"][jc, :, ts:te],
              after=[A.xr_free[sl], prev], inc=sXR)
        xr_done = sXR.mark()
        p.do("vector", add_src(jc, sts, xrs),
             after=list(src_after) + [xr_done, A.st_free[sl], prev], inc=sEPs)
        A.xr_free[sl] = sEPs.mark()
        p.do("vector", (lambda jc=jc, sts=sts: (lambda v: v.tensor_copy(
            A.xbf[:, jc * TB:(jc + 1) * TB], sts)))(),
            after=[sEPs.mark()], inc=sEPs)
        p.dma("sync", io["XRES_o"][jc, :, ts:te], sts,
              after=[sEPs.mark()], inc=sST)
        A.st_free[sl] = sST.mark()
    xbf_done = sEPs.mark()

    for jc in range(NCH):
        p.do("tensor", (lambda jc=jc: (lambda t: t.matmul(
            pe0[0:1, :], c.ones_col_bf[:, :],
            A.xbf[:, jc * TB:(jc + 1) * TB],
            start=(jc == 0), stop=(jc == NCH - 1))))(),
            after=[xbf_done, c.ready, prev], inc=(sEPm if jc == NCH - 1 else None))
    sq_mm = NONE_MARK
    for jc in range(NCH):
        slot = jc % 2
        p.do("scalar", (lambda jc=jc, slot=slot: (lambda s: s.activation(
            A.sq[:, slot * TB:(slot + 1) * TB],
            A.xbf[:, jc * TB:(jc + 1) * TB], AF.Square)))(),
            after=[xbf_done, sq_mm], inc=sEPr)
        sq_done = sEPr.mark()
        p.do("tensor", (lambda jc=jc, slot=slot: (lambda t: t.matmul(
            pe1[0:1, :], c.ones_col_bf[:, :],
            A.sq[:, slot * TB:(slot + 1) * TB],
            start=(jc == 0), stop=(jc == NCH - 1))))(),
            after=[sq_done, prev], inc=sEPm)
        if jc >= 1:
            sq_mm = (p.sems["sEPm"], p.sems["sEPm"].count - 1)
    mm_done = sEPm.mark()

    mean = A.rows[0:1, 0:TB]
    var = A.tmp[0:1, 0:TB]
    rr = A.rows[0:1, TB:2 * TB]
    p.do("vector", lambda v: v.tensor_scalar_mul(mean, pe0[0:1, :], 1.0 / DM),
         after=[mm_done], inc=sEPr)
    p.do("scalar", lambda s: s.activation(var, mean, AF.Square),
         after=[sEPr.mark()], inc=sEPr)
    p.do("vector", lambda v: v.scalar_tensor_tensor(
        var, pe1[0:1, :], 1.0 / DM, var, ALU.mult, ALU.subtract),
        after=[sEPr.mark()], inc=sEPr)
    p.do("scalar", lambda s: s.activation(var, var, AF.Ln,
                                          bias=c.eps_col[0:1, 0:1]),
         after=[sEPr.mark()], inc=sEPr)
    p.do("scalar", lambda s: s.activation(rr, var, AF.Exp, scale=-0.5),
         after=[sEPr.mark()], inc=sEPr)
    rows_done = sEPr.mark()

    p.do("tensor", lambda t: t.matmul(pe0[:, :], c.ones_row_f32[:, :], mean,
                                      start=True, stop=True),
         after=[rows_done], inc=sEPm)
    p.do("tensor", lambda t: t.matmul(pe1[:, :], c.ones_row_f32[:, :], rr,
                                      start=True, stop=True),
         after=[rows_done], inc=sEPm)
    bc_done = sEPm.mark()

    for jc in range(NCH):
        p.do("vector", (lambda jc=jc: (lambda v: v.tensor_sub(
            A.tmp[:, :], A.xbf[:, jc * TB:(jc + 1) * TB], pe0[:, :])))(),
            after=[bc_done], inc=sEPd)
        p.do("vector", (lambda jc=jc: (lambda v: v.tensor_mul(
            A.xbf[:, jc * TB:(jc + 1) * TB], A.tmp[:, :], pe1[:, :])))(),
            after=[sEPd.mark()], inc=sEPd)
        p.dma("sync", io["XLN_o"][jc, :, ts:te], A.xbf[:, jc * TB:(jc + 1) * TB],
              after=[sEPd.mark()], inc=sST)
    A.prev_done = sEPd.mark()
    return A.prev_done


def emit_final_tb(p, io, c, A, pe0, pe1, tb, add_src4, src_after=()):
    sXR = p.sem("sXR")
    sEPs = p.sem("sEPs")
    sEPd = p.sem("sEPd")
    sEPt = p.sem("sEPt")
    sST = p.sem("sST")

    ts = tb * TB
    prev = A.prev_done
    dma_done = NONE_MARK
    for t4 in range(4):
        for jc in range(NCH):
            sl = jc % 2
            xrs = A.xr[:, sl * TB:sl * TB + 128]
            sts = A.stage[:, sl * TB:sl * TB + 128]
            p.dma("sync", xrs,
                  io["XRES"][jc, :, ts + t4 * 128:ts + (t4 + 1) * 128],
                  after=[A.xr_free[sl], prev], inc=sXR)
            xr_done = sXR.mark()
            p.do("vector", add_src4(jc, t4, sts, xrs),
                 after=list(src_after) + [xr_done, A.st_free[sl], prev],
                 inc=sEPs)
            A.xr_free[sl] = sEPs.mark()
            st_done = sEPs.mark()
            bk = pe0 if jc % 2 == 0 else pe1
            p.do("tensor", (lambda bk=bk, sts=sts: (lambda t: t.transpose(
                bk[:, 0:128], sts, c.ident_f32[:, :])))(),
                after=[st_done, c.ready,
                       (p.sems["sEPd"], p.sems["sEPd"].count - 1)], inc=sEPt)
            A.st_free[sl] = sEPt.mark()
            p.do("vector", (lambda jc=jc, bk=bk: (lambda v: v.tensor_copy(
                A.outst[:, jc * 128:(jc + 1) * 128], bk[:, 0:128])))(),
                after=[sEPt.mark(), dma_done], inc=sEPd)
        p.dma("sync", io["out"][ts + t4 * 128: ts + (t4 + 1) * 128, :],
              A.outst[:, :], after=[sEPd.mark()], inc=sST)
        dma_done = sST.mark()
    A.prev_done = sEPd.mark()
    return A.prev_done



def emit_prologue(p, io, c):
    nc = p.nc
    sLD = p.sem("sLD")
    sPR = p.sem("sPR")
    sPT = p.sem("sPT")
    sPV = p.sem("sPV")
    sST = p.sem("sST")

    with (
        nc.sbuf_tensor(_nm("pr_x"), [128, 2 * DM], F32) as xt,
        nc.sbuf_tensor(_nm("pr_sq"), [128, DM], F32) as xsq,
        nc.sbuf_tensor(_nm("pr_ln"), [128, DM], BF16) as xln,
        nc.sbuf_tensor(_nm("pr_rows"), [128, 12], F32) as rows,
        nc.sbuf_tensor(_nm("pr_stb"), [128, 4 * DM], BF16) as stb,
        nc.sbuf_tensor(_nm("pr_stf"), [128, 2 * DM], F32) as stf,
        nc.psum_tensor(_nm(f"pr_pb"), [128, 1024], BF16) as pb,
        nc.psum_tensor(_nm(f"pr_pf0"), [128, 512], F32) as pf0,
        nc.psum_tensor(_nm(f"pr_pf1"), [128, 512], F32) as pf1,
    ):
        slot_free = [NONE_MARK, NONE_MARK]
        xln_free = NONE_MARK
        xsq_free = NONE_MARK
        pb_free = NONE_MARK
        pf_free = NONE_MARK
        stb_free = NONE_MARK
        stf_free = NONE_MARK
        row_free = [NONE_MARK, NONE_MARK]

        for tt in range(NT // 128):
            slot = tt % 2
            grp, sub = tt // 4, tt % 4
            xs = xt[:, slot * DM:(slot + 1) * DM]
            p.dma("sync", xs, io["L0_attn_x"][tt * 128:(tt + 1) * 128, :],
                  after=[slot_free[slot]], inc=sLD)
            ld = sLD.mark()

            rbase = (tt % 2) * 6
            s_ = rows[:, rbase + 0:rbase + 1]
            m_ = rows[:, rbase + 1:rbase + 2]
            s2 = rows[:, rbase + 2:rbase + 3]
            mq = rows[:, rbase + 3:rbase + 4]
            vr = rows[:, rbase + 4:rbase + 5]
            rc = rows[:, rbase + 5:rbase + 6]
            p.do("vector", (lambda xs=xs, s_=s_: (lambda v: v.reduce_sum(
                s_, xs, AXX)))(), after=[ld, row_free[tt % 2]], inc=sPR)
            p.do("vector", (lambda s_=s_, m_=m_: (lambda v: v.tensor_scalar_mul(
                m_, s_, 1.0 / DM)))(), after=[sPR.mark()], inc=sPR)
            p.do("scalar", (lambda xs=xs: (lambda s: s.activation(
                xsq[:, :], xs, AF.Square)))(), after=[ld, xsq_free], inc=sPR)
            p.do("vector", (lambda s2=s2: (lambda v: v.reduce_sum(
                s2, xsq[:, :], AXX)))(), after=[sPR.mark()], inc=sPR)
            xsq_free = sPR.mark()
            p.do("scalar", (lambda m_=m_, mq=mq: (lambda s: s.activation(
                mq, m_, AF.Square)))(), after=[sPR.mark()], inc=sPR)
            p.do("vector", (lambda s2=s2, mq=mq, vr=vr: (lambda v: v.scalar_tensor_tensor(
                vr, s2, 1.0 / DM, mq, ALU.mult, ALU.subtract)))(),
                after=[sPR.mark()], inc=sPR)
            p.do("scalar", (lambda vr=vr: (lambda s: s.activation(
                vr, vr, AF.Ln, bias=c.eps_col[:, 0:1])))(),
                after=[sPR.mark()], inc=sPR)
            p.do("scalar", (lambda vr=vr, rc=rc: (lambda s: s.activation(
                rc, vr, AF.Exp, scale=-0.5)))(),
                after=[sPR.mark()], inc=sPR)
            p.do("vector", (lambda xs=xs, m_=m_, rc=rc: (lambda v: v.tensor_scalar(
                xln[:, :], xs, m_, rc, ALU.subtract, ALU.mult)))(),
                after=[sPR.mark(), xln_free], inc=sPR)
            ln_done = sPR.mark()
            row_free[tt % 2] = ln_done

            for cc in range(NCH):
                p.do("tensor", (lambda cc=cc: (lambda t: t.transpose(
                    pb[:, cc * 128:(cc + 1) * 128],
                    xln[:, cc * 128:(cc + 1) * 128], c.ident_bf[:, :])))(),
                    after=[ln_done, pb_free, c.ready],
                    inc=(sPT if cc == NCH - 1 else None))
            xln_free = sPT.mark()
            p.do("vector", (lambda sub=sub: (lambda v: v.tensor_copy(
                stb[:, sub * DM:(sub + 1) * DM], pb[:, :])))(),
                after=[sPT.mark(), stb_free], inc=sPV)
            pb_free = sPV.mark()

            for cc in range(NCH):
                pf = pf0 if cc % 2 == 0 else pf1
                off = (cc // 2) * 128
                p.do("tensor", (lambda pf=pf, off=off, cc=cc, xs=xs: (lambda t: t.transpose(
                    pf[:, off:off + 128], xs[:, cc * 128:(cc + 1) * 128],
                    c.ident_f32[:, :])))(),
                    after=[ld, pf_free, c.ready],
                    inc=(sPT if cc >= NCH - 2 else None))
            slot_free[slot] = sPT.mark()
            p.do("vector", lambda v: v.tensor_copy(stf[:, 0:512], pf0[:, :]),
                 after=[sPT.mark(), stf_free], inc=sPV)
            p.do("vector", lambda v: v.tensor_copy(stf[:, 512:1024], pf1[:, :]),
                 after=[sPT.mark(), stf_free], inc=sPV)
            pf_free = sPV.mark()
            for cc in range(NCH):
                off = 512 * (cc % 2) + (cc // 2) * 128
                p.dma("sync", io["XRES_o"][cc, :, tt * 128:(tt + 1) * 128],
                      stf[:, off:off + 128], after=[sPV.mark()], inc=sST)
            stf_free = sST.mark()

            if sub == 3:
                for cc in range(NCH):
                    src = stb.rearrange("p (s c x) -> p s c x", s=4, c=NCH)[:, :, cc, :]
                    dst = io["XLN_o"][cc, :, grp * 512:(grp + 1) * 512]
                    dst = dst.rearrange("p (s x) -> p s x", s=4)
                    p.dma("sync", dst, src, after=[sPV.mark()], inc=sST)
                stb_free = sST.mark()
        p.fence()
        p.flush("prologue")



def emit_l0ffn(p, io, c, final=False):
    nc = p.nc
    sXL = p.sem("sXL")
    sWT = p.sem("sWT")
    sHP = p.sem("sHP")
    sHS = p.sem("sHS")
    sHM = p.sem("sHM")
    sYP = p.sem("sYP")
    sYE = p.sem("sYE")
    sGT = p.sem("sGT")
    sGE = p.sem("sGE")
    sGP = p.sem("sGP")
    sLD = p.sem("sLD")

    from contextlib import ExitStack
    with ExitStack() as ctx:
        xh = ctx.enter_context(nc.sbuf_tensor(_nm("f_xh"), [128, NCH * HALF], BF16))
        hb = ctx.enter_context(nc.sbuf_tensor(_nm("f_hb"), [128, NCH * HALF], BF16))
        wa = ctx.enter_context(nc.sbuf_tensor(_nm("f_wa"), [128, 16384], BF16))
        wb = ctx.enter_context(nc.sbuf_tensor(_nm("f_wb"), [128, 16384], BF16))
        yb = ctx.enter_context(nc.sbuf_tensor(_nm("f_yb"), [128, NCH * HALF], BF16))
        wg = ctx.enter_context(nc.sbuf_tensor(_nm("f_wg"), [128, NCH * NE], BF16))
        bgsb = ctx.enter_context(nc.sbuf_tensor(_nm("f_bg"), [NE, 1], F32))
        b1all = ctx.enter_context(nc.sbuf_tensor(_nm("f_b1"), [32, NE * 128], F32))
        b1T = ctx.enter_context(nc.sbuf_tensor(_nm("f_b1T"), [128, 32 * NE], F32))
        b2sb = ctx.enter_context(nc.sbuf_tensor(_nm("f_b2"), [NE, DM], BF16))
        glog = ctx.enter_context(nc.sbuf_tensor(_nm("f_glog"), [NE, TB], F32))
        gt = ctx.enter_context(nc.sbuf_tensor(_nm("f_gt"), [128, 2 * NE], F32))
        gms = ctx.enter_context(nc.sbuf_tensor(_nm("f_gms"), [128, NE], F32))
        grows = ctx.enter_context(nc.sbuf_tensor(_nm("f_grows"), [128, 8], F32))
        mt = ctx.enter_context(nc.sbuf_tensor(_nm("f_mt"), [128, 16 * NE], BF16))
        mT = ctx.enter_context(nc.sbuf_tensor(_nm("f_mT"), [NE, HALF], BF16))
        mbc = ctx.enter_context(nc.sbuf_tensor(_nm("f_mbc"), [128, HALF], BF16))
        ph0 = ctx.enter_context(nc.psum_tensor(_nm(f"f_ph0"), [128, 512], F32))
        ph1 = ctx.enter_context(nc.psum_tensor(_nm(f"f_ph1"), [128, 512], F32))
        py0 = ctx.enter_context(nc.psum_tensor(_nm(f"f_py0"), [128, 512], F32))
        py1 = ctx.enter_context(nc.psum_tensor(_nm(f"f_py1"), [128, 512], F32))
        pg = ctx.enter_context(nc.psum_tensor(_nm(f"f_pg"), [128, 512], F32))
        ptb = ctx.enter_context(nc.psum_tensor(_nm(f"f_ptb"), [128, 1024], BF16))
        pe0 = ctx.enter_context(nc.psum_tensor(_nm(f"f_pe0"), [128, 512], F32))
        pe1 = ctx.enter_context(nc.psum_tensor(_nm(f"f_pe1"), [128, 512], F32))
        A = alloc_epi(nc, ctx, final=final)

        p.dma("gpsimd", wg.rearrange("p (cc e) -> p cc e", cc=NCH),
              io["L0_ffn_Wg"].rearrange("(cc p) e -> p cc e", p=128),
              inc=sLD, allow_slow_non_contiguous=True)
        p.dma("gpsimd", bgsb[:, :],
              io["L0_ffn_bg"].rearrange("(e one) -> e one", one=1), inc=sLD)
        for e in range(NE):
            p.dma("sync", b1all[0:32, e * 128:(e + 1) * 128],
                  io["L0_ffn_b1"][e].rearrange("(cc pp) -> cc pp", pp=128),
                  inc=sLD)
        p.dma("gpsimd", b2sb[:, :], io["L0_ffn_b2"][:, :], inc=sLD)
        ld1 = sLD.mark()
        for e in range(NE):
            p.do("tensor", (lambda e=e: (lambda t: t.transpose(
                pg[:, e * 32:(e + 1) * 32], b1all[0:32, e * 128:(e + 1) * 128],
                c.ident_f32[0:32, 0:32])))(),
                after=[ld1, c.ready], inc=(sGT if e == NE - 1 else None))
        b1Tv = b1T.rearrange("p (hc e) -> p hc e", e=NE)
        for e in range(NE):
            p.do("vector", (lambda e=e: (lambda v: v.tensor_copy(
                b1Tv[:, :, e], pg[:, e * 32:(e + 1) * 32])))(),
                after=[sGT.mark()], inc=sGE)
        pg_free = sGE.mark()
        b1T_ready = sGE.mark()

        phases = [(e, q) for e in range(NE) for q in range(4)]

        def load_phase(kk, after):
            e, q = phases[kk]
            buf = wa if kk % 2 == 0 else wb
            w1src = io["L0_ffn_W1"][e].rearrange("(cc pp) h -> pp cc h", pp=128)
            w1src = w1src[:, :, q * 1024:(q + 1) * 1024]
            p.dma("gpsimd", buf[:, 0:8192].rearrange("p (cc h) -> p cc h", cc=NCH),
                  w1src, after=after, inc=sWT)
            w2src = io["L0_ffn_W2"][e][q * 1024:(q + 1) * 1024, :]
            w2src = w2src.rearrange("(hl pp) j -> pp hl j", pp=128)
            p.dma("gpsimd", buf[:, 8192:16384].rearrange("p (hl j) -> p hl j", hl=NCH),
                  w2src, after=after, inc=sWT)
            return sWT.count

        wt_ready = {}
        xh_free = NONE_MARK
        hb_free = NONE_MARK
        mT_free = NONE_MARK
        mbc_free = NONE_MARK
        yb_free = NONE_MARK
        ptb_free = NONE_MARK

        for half in range(2):
            for cc in range(NCH):
                p.dma("sync", xh[:, cc * HALF:(cc + 1) * HALF],
                      io["XLN"][cc, :, half * HALF:(half + 1) * HALF],
                      after=[xh_free], inc=sXL)
            xh_done = sXL.mark()

            if half == 0:
                wt_ready[0] = load_phase(0, [])
                wt_ready[1] = load_phase(1, [])

            glog_free = NONE_MARK
            for tq in range(4):
                for cc in range(NCH):
                    p.do("tensor", (lambda cc=cc, tq=tq: (lambda t: t.matmul(
                        pg[0:NE, :], wg[:, cc * NE:(cc + 1) * NE],
                        xh[:, cc * HALF + tq * TB: cc * HALF + (tq + 1) * TB],
                        start=(cc == 0), stop=(cc == NCH - 1))))(),
                        after=[xh_done, pg_free, ld1],
                        inc=(sGT if cc == NCH - 1 else None))
                p.do("vector", (lambda tq=tq: (lambda v: v.tensor_scalar_add(
                    glog[:, :], pg[0:NE, :], bgsb[:, 0:1])))(),
                    after=[sGT.mark(), glog_free], inc=sGE)
                pg_free = sGE.mark()
                for tl in range(4):
                    tc = tq * 4 + tl
                    p.do("tensor", (lambda tl=tl: (lambda t: t.transpose(
                        pg[:, 0:NE], glog[:, tl * 128:(tl + 1) * 128],
                        c.ident_f32[0:NE, 0:NE])))(),
                        after=[sGE.mark(), pg_free], inc=sGT)
                    p.do("vector", (lambda tc=tc: (lambda v: v.tensor_copy(
                        gt[:, (tc % 2) * NE:(tc % 2 + 1) * NE], pg[:, 0:NE])))(),
                        after=[sGT.mark()], inc=sGE)
                    pg_free = sGE.mark()
                    gtc = gt[:, (tc % 2) * NE:(tc % 2 + 1) * NE]
                    mx = grows[:, 0:1]; nmx = grows[:, 1:2]
                    ss = grows[:, 2:3]; rs = grows[:, 3:4]
                    p.do("vector", (lambda gtc=gtc, mx=mx: (lambda v: v.reduce_max(
                        mx, gtc, AXX)))(), after=[sGE.mark()], inc=sGE)
                    p.do("vector", (lambda mx=mx, nmx=nmx: (lambda v: v.tensor_scalar_mul(
                        nmx, mx, -1.0)))(), after=[sGE.mark()], inc=sGE)
                    p.do("scalar", (lambda gtc=gtc, nmx=nmx: (lambda s: s.activation(
                        gms[:, :], gtc, AF.Exp, bias=nmx)))(),
                        after=[sGE.mark()], inc=sGE)
                    p.do("vector", (lambda ss=ss: (lambda v: v.reduce_sum(
                        ss, gms[:, :], AXX)))(), after=[sGE.mark()], inc=sGE)
                    p.do("vector", (lambda ss=ss, rs=rs: (lambda v: v.reciprocal(
                        rs, ss)))(), after=[sGE.mark()], inc=sGE)
                    p.do("vector", (lambda gtc=gtc, mx=mx: (lambda v: v.tensor_single_scalar(
                        gms[:, :], gtc, mx, ALU.is_ge)))(),
                        after=[sGE.mark()], inc=sGE)
                    p.do("vector", (lambda tc=tc, rs=rs: (lambda v: v.tensor_scalar(
                        mt[:, tc * NE:(tc + 1) * NE], gms[:, :], rs, 0.5,
                        ALU.mult, ALU.mult)))(),
                        after=[sGE.mark()], inc=sGE)
                    p.do("tensor", (lambda tc=tc: (lambda t: t.transpose(
                        ptb[0:NE, (tc % 8) * 128:(tc % 8 + 1) * 128],
                        mt[:, tc * NE:(tc + 1) * NE], c.ident_bf[:, :])))(),
                        after=[sGE.mark(), ptb_free], inc=sGT)
                    if tc % 8 == 7:
                        p.do("vector", (lambda tc=tc: (lambda v: v.tensor_copy(
                            mT[0:NE, (tc - 7) * 128:(tc + 1) * 128], ptb[0:NE, :])))(),
                            after=[sGT.mark()], inc=sGE)
                        ptb_free = sGE.mark()
                glog_free = sGT.mark()
            mT_done = sGE.mark()

            yg = [A.prev_done] if half > 0 else []
            p.do("gpsimd", lambda g: g.memset(yb[:, :], 0.0),
                 after=yg + [yb_free], inc=sGP)
            y_clear = sGP.mark()

            for e in range(NE):
                for tq in range(4):
                    p.do("tensor", (lambda e=e, tq=tq: (lambda t: t.matmul(
                        pg[:, :], c.sel_bf[:, e * 128:(e + 1) * 128],
                        mT[0:NE, tq * TB:(tq + 1) * TB], start=True, stop=True)))(),
                        after=[mT_done, pg_free, mbc_free], inc=sGT)
                    p.do("vector", (lambda tq=tq: (lambda v: v.tensor_copy(
                        mbc[:, tq * TB:(tq + 1) * TB], pg[:, :])))(),
                        after=[sGT.mark()], inc=sGE)
                    pg_free = sGE.mark()
                mbc_done = sGE.mark()

                for q in range(4):
                    kk = e * 4 + q
                    buf = wa if kk % 2 == 0 else wb
                    wready = (sWT, wt_ready[kk])
                    hfree = {0: hb_free, 1: hb_free}

                    for hc in range(NCH):
                        for tq in range(4):
                            g = hc * 4 + tq
                            bank = [ph0, ph1][g % 2]
                            for dc in range(NCH):
                                p.do("tensor", (lambda dc=dc, hc=hc, tq=tq, bank=bank, buf=buf: (lambda t: t.matmul(
                                    bank[:, :],
                                    buf[:, dc * 1024 + hc * 128: dc * 1024 + (hc + 1) * 128],
                                    xh[:, dc * HALF + tq * TB: dc * HALF + (tq + 1) * TB],
                                    start=(dc == 0), stop=(dc == NCH - 1))))(),
                                    after=[wready, xh_done, hfree[g % 2]],
                                    inc=(sHP if dc == NCH - 1 else None))
                            mm_done = sHP.mark()
                            bias_ap = b1T[:, (q * 8 + hc) * NE + e:(q * 8 + hc) * NE + e + 1]
                            p.do("scalar", (lambda hc=hc, tq=tq, bank=bank, bias_ap=bias_ap: (lambda s: s.activation(
                                hb[:, hc * HALF + tq * TB: hc * HALF + (tq + 1) * TB],
                                bank[:, :], AF.Silu, bias=bias_ap)))(),
                                after=[mm_done, b1T_ready], inc=sHS)
                            hfree[g % 2] = sHS.mark()
                            p.do("vector", (lambda hc=hc, tq=tq: (lambda v: v.tensor_mul(
                                hb[:, hc * HALF + tq * TB: hc * HALF + (tq + 1) * TB],
                                hb[:, hc * HALF + tq * TB: hc * HALF + (tq + 1) * TB],
                                mbc[:, tq * TB:(tq + 1) * TB])))(),
                                after=[sHS.mark(), mbc_done], inc=sHM)
                    hm_all = sHM.mark()

                    ye_base = sYE.count
                    for oc in range(NCH):
                        for tq in range(4):
                            g = oc * 4 + tq
                            bank = [py0, py1][g % 2]
                            bank_guard = (sYE, ye_base + g - 1)
                            is_first_phase = (e == 0 and q == 0)
                            for hl in range(NCH):
                                p.do("tensor", (lambda hl=hl, oc=oc, tq=tq, bank=bank, buf=buf, ifp=is_first_phase: (lambda t: t.matmul(
                                    bank[:, :],
                                    buf[:, 8192 + hl * 1024 + oc * 128: 8192 + hl * 1024 + (oc + 1) * 128],
                                    hb[:, hl * HALF + tq * TB: hl * HALF + (tq + 1) * TB],
                                    start=(hl == 0),
                                    stop=(hl == NCH - 1 and not ifp))))(),
                                    after=[hm_all, wready, bank_guard],
                                    inc=(sYP if (hl == NCH - 1 and not is_first_phase) else None))
                            if is_first_phase:
                                p.do("tensor", (lambda oc=oc, tq=tq, bank=bank: (lambda t: t.matmul(
                                    bank[:, :], b2sb[:, oc * 128:(oc + 1) * 128],
                                    mT[0:NE, tq * TB:(tq + 1) * TB],
                                    start=False, stop=True)))(),
                                    after=[mT_done, ld1], inc=sYP)
                            p.do("vector", (lambda oc=oc, tq=tq, bank=bank: (lambda v: v.tensor_add(
                                yb[:, oc * HALF + tq * TB: oc * HALF + (tq + 1) * TB],
                                bank[:, :],
                                yb[:, oc * HALF + tq * TB: oc * HALF + (tq + 1) * TB])))(),
                                after=[sYP.mark(), y_clear], inc=sYE)
                    hb_free = sYP.mark()
                    if kk + 2 < 32:
                        wt_ready[kk + 2] = load_phase(kk + 2, [sYP.mark()])
                    elif half == 0:
                        wt_ready[kk + 2 - 32] = load_phase(kk + 2 - 32, [sYP.mark()])
            xh_free = sHP.mark()
            mT_free = sYP.mark()
            mbc_free = sHM.mark()

            y_done = sYE.mark()
            for tbl in range(4):
                tb = half * 4 + tbl
                if not final:
                    def add_src(jc, dst, xr, tbl=tbl):
                        def f(v):
                            return v.tensor_add(
                                dst,
                                yb[:, jc * HALF + tbl * TB: jc * HALF + (tbl + 1) * TB],
                                xr)
                        return f
                    emit_epilogue_tb(p, io, c, A, pe0, pe1, tb, add_src,
                                     src_after=[y_done])
                else:
                    def add_src4(jc, t4, dst, xr, tbl=tbl):
                        def f(v):
                            return v.tensor_add(
                                dst,
                                yb[:, jc * HALF + tbl * TB + t4 * 128:
                                   jc * HALF + tbl * TB + (t4 + 1) * 128],
                                xr)
                        return f
                    emit_final_tb(p, io, c, A, pe0, pe1, tb, add_src4,
                                  src_after=[y_done])
            yb_free = A.prev_done
        p.fence()
        p.flush("l0ffn")



def emit_wprep(p, io, c, jobs, name):
    nc = p.nc
    sLD = p.sem("sLD")
    sPT = p.sem("sPT")
    sPV = p.sem("sPV")
    sST = p.sem("sST")
    with (
        nc.sbuf_tensor(_nm("wp_in"), [128, 2 * DM], BF16) as win,
        nc.sbuf_tensor(_nm("wp_out"), [128, DM], BF16) as wout,
        nc.psum_tensor(_nm(f"wp_ptb"), [128, 1024], BF16) as ptb,
    ):
        slot_free = [NONE_MARK, NONE_MARK]
        ptb_free = NONE_MARK
        wout_free = NONE_MARK
        k = 0
        for (srcm, dstm) in jobs:
            for rcq in range(NCH):
                sl = k % 2
                p.dma("gpsimd", win[:, sl * DM:(sl + 1) * DM],
                      srcm[rcq * 128:(rcq + 1) * 128, :],
                      after=[slot_free[sl]], inc=sLD)
                ld = sLD.mark()
                for cc in range(NCH):
                    p.do("tensor", (lambda cc=cc, sl=sl: (lambda t: t.transpose(
                        ptb[:, cc * 128:(cc + 1) * 128],
                        win[:, sl * DM + cc * 128: sl * DM + (cc + 1) * 128],
                        c.ident_bf[:, :])))(),
                        after=[ld, ptb_free, c.ready],
                        inc=(sPT if cc == NCH - 1 else None))
                slot_free[sl] = sPT.mark()
                p.do("vector", lambda v: v.tensor_copy(wout[:, :], ptb[:, :]),
                     after=[sPT.mark(), wout_free], inc=sPV)
                ptb_free = sPV.mark()
                dst = dstm[:, :, rcq * 128:(rcq + 1) * 128]
                dst = dst.rearrange("cc p x -> p cc x")
                src = wout.rearrange("p (cc x) -> p cc x", cc=NCH)
                p.dma("sync", dst, src, after=[sPV.mark()], inc=sST)
                wout_free = sST.mark()
                k += 1
    p.fence()
    p.flush(name)


def emit_l1prep(p, io, c):
    jobs = [(io["L1_ffn_expert_w"][e], io["W1T_l1"][e]) for e in range(NE)]
    emit_wprep(p, io, c, jobs, "l1prep")


def emit_l0aprep(p, io, c):
    jobs = [(io["L0_attn_" + w], io["WT_l0a"][m])
            for m, w in enumerate(("Wq", "Wk", "Wv", "Wo"))]
    emit_wprep(p, io, c, jobs, "l0aprep")



SQI128 = 1.0 / math.sqrt(128.0)


def emit_l0attn(p, io, c, final=False):
    nc = p.nc
    sXL = p.sem("sXL")
    sWT = p.sem("sWT")
    sMM = p.sem("sMM")
    sEV = p.sem("sEV")
    sAC = p.sem("sAC")
    sOT = p.sem("sOT")
    sLD = p.sem("sLD")

    from contextlib import ExitStack
    with ExitStack() as ctx:
        xh = ctx.enter_context(nc.sbuf_tensor(_nm("a_xh"), [128, NCH * NT], BF16))
        kvs = ctx.enter_context(nc.sbuf_tensor(_nm("a_kvs"), [128, 16 * 128], BF16))
        bqkv = ctx.enter_context(nc.sbuf_tensor(_nm("a_bqkv"), [128, 4 * NCH], F32))
        pA = ctx.enter_context(nc.psum_tensor(_nm(f"a_pA"), [128, 512], F32))
        pB = ctx.enter_context(nc.psum_tensor(_nm(f"a_pB"), [128, 512], F32))
        pk0 = ctx.enter_context(nc.psum_tensor(_nm(f"a_pk0"), [128, 512], F32))
        pk1 = ctx.enter_context(nc.psum_tensor(_nm(f"a_pk1"), [128, 512], F32))
        ptb = ctx.enter_context(nc.psum_tensor(_nm(f"a_ptb"), [128, 1024], BF16))
        pe0 = ctx.enter_context(nc.psum_tensor(_nm(f"a_pe0"), [128, 512], F32))
        pe1 = ctx.enter_context(nc.psum_tensor(_nm(f"a_pe1"), [128, 512], F32))
        A = alloc_epi(nc, ctx, final=final)
        ctx12 = ctx.enter_context(ExitStack())
        wpA = ctx12.enter_context(nc.sbuf_tensor(_nm("a_wpA"), [128, 3072], BF16))
        wpB = ctx12.enter_context(nc.sbuf_tensor(_nm("a_wpB"), [128, 3072], BF16))
        kT2 = ctx12.enter_context(nc.sbuf_tensor(_nm("a_kT2"), [128, NT], BF16))
        vT2 = ctx12.enter_context(nc.sbuf_tensor(_nm("a_vT2"), [128, NT], BF16))
        qT2 = kT2
        ktm = ctx12.enter_context(nc.sbuf_tensor(_nm("a_ktm"), [128, 32 * 512], BF16))
        vtm = ctx12.enter_context(nc.sbuf_tensor(_nm("a_vtm"), [128, 32 * 128], BF16))
        kvt = ctx12.enter_context(nc.sbuf_tensor(_nm("a_kvt"), [128, 512], BF16))
        qpT = ctx12.enter_context(nc.sbuf_tensor(_nm("a_qpT"), [128, 4 * NT], BF16))
        csA = ctx12.enter_context(nc.sbuf_tensor(_nm("a_csA"), [128, 512], F32))
        csB = ctx12.enter_context(nc.sbuf_tensor(_nm("a_csB"), [128, 512], F32))
        otst = ctx12.enter_context(nc.sbuf_tensor(_nm("a_otst"), [128, 2 * 512], BF16))


        for cc in range(NCH):
            p.dma("sync", xh[:, cc * NT:(cc + 1) * NT], io["XLN"][cc, :, :],
                  inc=sXL)
        xh_done = sXL.mark()
        for m, w in enumerate(("bq", "bk", "bv", "bo")):
            for cc in range(NCH):
                p.dma("gpsimd", bqkv[:, m * NCH + cc:m * NCH + cc + 1],
                      io["L0_attn_" + w].rearrange("(cc pp) -> cc pp", pp=128)
                      [cc].rearrange("(pp one) -> pp one", one=1), inc=sLD)
        bias_done = sLD.mark()

        def load_pair(hp, buf, after, mats=(0, 1, 2)):
            for i, m in enumerate(mats):
                src = io["WT_l0a"][m][:, :, hp * 128:(hp + 1) * 128]
                src = src.rearrange("cc p x -> p cc x")
                dst = buf[:, i * 1024:(i + 1) * 1024]
                dst = dst.rearrange("p (cc x) -> p cc x", cc=NCH)
                p.dma("gpsimd", dst, src, after=after, inc=sWT)
            return sWT.count

        wt_ready = {}
        wt_ready[0] = load_pair(0, wpA, [])
        wt_ready[1] = load_pair(1, wpB, [])

        pair_mm_end = {}
        ev_free = {"pA": NONE_MARK, "pB": NONE_MARK, "pk0": NONE_MARK,
                   "pk1": NONE_MARK, "ptb": NONE_MARK}
        ktm_free = NONE_MARK
        vtm_free = NONE_MARK
        kT2_free = NONE_MARK
        vT2_free = NONE_MARK
        kvt_free = NONE_MARK

        for hp in range(NHEADS // 2):
            buf = wpA if hp % 2 == 0 else wpB
            wready = (sWT, wt_ready[hp])

            for mi, (dstb, bcol, dfree) in enumerate(
                    ((kT2, 1, kT2_free), (vT2, 2, vT2_free))):
                for tbq in range(NTB):
                    bank = [pA, pB][tbq % 2]
                    bkey = "pA" if tbq % 2 == 0 else "pB"
                    for cc in range(NCH):
                        p.do("tensor", (lambda cc=cc, tbq=tbq, bank=bank, buf=buf, mi=mi: (lambda t: t.matmul(
                            bank[:, :],
                            buf[:, (mi + 1) * 1024 + cc * 128:(mi + 1) * 1024 + (cc + 1) * 128],
                            xh[:, cc * NT + tbq * TB: cc * NT + (tbq + 1) * TB],
                            start=(cc == 0), stop=(cc == NCH - 1))))(),
                            after=[wready, xh_done, ev_free[bkey], dfree],
                            inc=(sMM if cc == NCH - 1 else None))
                        if cc == 0:
                            dfree = None
                    p.do("vector", (lambda tbq=tbq, bank=bank, dstb=dstb, bcol=bcol, hp=hp: (lambda v: v.tensor_scalar_add(
                        dstb[:, tbq * TB:(tbq + 1) * TB], bank[:, :],
                        bqkv[:, bcol * NCH + hp:bcol * NCH + hp + 1])))(),
                        after=[sMM.mark(), bias_done], inc=sEV)
                    ev_free[bkey] = sEV.mark()
            kv_proj_done = sEV.mark()

            for tt in range(32):
                for h in range(2):
                    g = tt * 2 + h
                    bank = [pk0, pk1][g % 2]
                    bkey = "pk0" if g % 2 == 0 else "pk1"
                    base = h * 64
                    p.do("tensor", (lambda tt=tt, h=h, bank=bank, base=base: (lambda t: t.matmul(
                        bank[0:128, 0:128],
                        kT2[base:base + 64, tt * 128:(tt + 1) * 128],
                        c.projpair[base:base + 64, h * 128:(h + 1) * 128],
                        start=True, stop=True)))(),
                        after=[kv_proj_done, ev_free[bkey], c.proj_ready,
                               ktm_free], inc=sMM)
                    mmm = sMM.mark()
                    p.do("scalar", (lambda bank=bank: (lambda s: s.activation(
                        csA[:, 0:128], bank[0:128, 0:128], AF.Sin,
                        scale=0.125)))(), after=[mmm, sEV.mark()], inc=sAC)
                    p.do("scalar", (lambda bank=bank: (lambda s: s.activation(
                        csB[:, 0:128], bank[0:128, 0:128], AF.Sin,
                        scale=0.0625)))(), after=[sAC.mark()], inc=sAC)
                    ev_free[bkey] = sAC.mark()
                    p.do("scalar", lambda s: s.activation(
                        csB[:, 0:128], csB[:, 0:128], AF.Square),
                        after=[sAC.mark()], inc=sAC)
                    trig = sAC.mark()
                    koff = tt * 512 + h * 256
                    p.do("vector", (lambda koff=koff: (lambda v: v.tensor_scalar(
                        ktm[:, koff:koff + 128], csB[:, 0:128],
                        -2.0 * SQI128, SQI128, ALU.mult, ALU.add)))(),
                        after=[trig], inc=sEV)
                    p.do("vector", (lambda koff=koff: (lambda v: v.tensor_scalar_mul(
                        ktm[:, koff + 128:koff + 256], csA[:, 0:128],
                        SQI128)))(), after=[trig], inc=sEV)
            ktrig_done = sEV.mark()
            kT2_free = sMM.mark()

            for ttg in range(4):
                for tl in range(8):
                    tt = ttg * 8 + tl
                    p.do("tensor", (lambda tt=tt, tl=tl: (lambda t: t.transpose(
                        ptb[:, tl * 128:(tl + 1) * 128],
                        vT2[:, tt * 128:(tt + 1) * 128], c.ident_bf[:, :])))(),
                        after=[kv_proj_done, ev_free["ptb"], c.ready, vtm_free],
                        inc=(sMM if tl == 7 else None))
                p.do("vector", (lambda ttg=ttg: (lambda v: v.tensor_copy(
                    vtm[:, ttg * 1024:(ttg + 1) * 1024], ptb[:, :])))(),
                    after=[sMM.mark()], inc=sEV)
                ev_free["ptb"] = sEV.mark()
            vtr_done = sEV.mark()
            vT2_free = sMM.mark()

            for h in range(2):
                bank = [pk0, pk1][h]
                for tt in range(32):
                    p.do("tensor", (lambda tt=tt, h=h, bank=bank: (lambda t: t.matmul(
                        bank[0:64, 0:256],
                        vtm[:, tt * 128 + h * 64: tt * 128 + (h + 1) * 64],
                        ktm[:, tt * 512 + h * 256: tt * 512 + (h + 1) * 256],
                        start=(tt == 0), stop=(tt == 31))))(),
                        after=[ktrig_done, vtr_done,
                               (ev_free["pk0"] if h == 0 else ev_free["pk1"])],
                        inc=(sMM if tt == 31 else None))
            kv_done = sMM.mark()
            ktm_free = kv_done
            vtm_free = kv_done
            for h in range(2):
                bank = [pk0, pk1][h]
                p.do("vector", (lambda h=h, bank=bank: (lambda v: v.tensor_copy(
                    kvt[0:64, h * 256:(h + 1) * 256], bank[0:64, 0:256])))(),
                    after=[kv_done, kvt_free], inc=sEV)
                ev_free["pk0" if h == 0 else "pk1"] = sEV.mark()
            kvt_done = sEV.mark()
            for h in range(2):
                for mc in range(2):
                    p.do("tensor", (lambda h=h, mc=mc: (lambda t: t.transpose(
                        ptb[:, 512 + (h * 2 + mc) * 64: 512 + (h * 2 + mc + 1) * 64],
                        kvt[0:64, h * 256 + mc * 128: h * 256 + (mc + 1) * 128],
                        c.ident_bf[0:64, 0:64])))(),
                        after=[kvt_done, ev_free["ptb"], c.ready], inc=sMM)
            p.do("vector", (lambda hp=hp: (lambda v: v.tensor_copy(
                kvs[:, hp * 256:(hp + 1) * 256], ptb[:, 512:512 + 256])))(),
                after=[sMM.mark()], inc=sEV)
            ev_free["ptb"] = sEV.mark()
            kvt_free = sEV.mark()
            pair_mm_end[hp] = sMM.mark()
            if hp + 2 < 8:
                wt_ready[hp + 2] = load_pair(
                    hp + 2, wpA if hp % 2 == 0 else wpB, [pair_mm_end[hp]])
        kvs_done = sEV.mark()

        wt_ready2 = {}
        wt_ready2[0] = load_pair(0, wpA, [pair_mm_end[6]], mats=(0,))
        wt_ready2[1] = load_pair(1, wpB, [pair_mm_end[7]], mats=(0,))
        qT2_free = pair_mm_end[7]
        qpT_free = NONE_MARK
        otst_free = [NONE_MARK, NONE_MARK]
        pass2_end = {}
        for hp in range(8):
            buf = wpA if hp % 2 == 0 else wpB
            wready = (sWT, wt_ready2[hp])
            qfree = qT2_free
            for tbq in range(NTB):
                bank = [pA, pB][tbq % 2]
                bkey = "pA" if tbq % 2 == 0 else "pB"
                for cc in range(NCH):
                    p.do("tensor", (lambda cc=cc, tbq=tbq, bank=bank, buf=buf: (lambda t: t.matmul(
                        bank[:, :],
                        buf[:, cc * 128:(cc + 1) * 128],
                        xh[:, cc * NT + tbq * TB: cc * NT + (tbq + 1) * TB],
                        start=(cc == 0), stop=(cc == NCH - 1))))(),
                        after=[wready, xh_done, ev_free[bkey], qfree],
                        inc=(sMM if cc == NCH - 1 else None))
                    if cc == 0:
                        qfree = None
                p.do("vector", (lambda tbq=tbq, bank=bank, hp=hp: (lambda v: v.tensor_scalar_add(
                    qT2[:, tbq * TB:(tbq + 1) * TB], bank[:, :],
                    bqkv[:, 0 * NCH + hp:0 * NCH + hp + 1])))(),
                    after=[sMM.mark(), bias_done], inc=sEV)
                ev_free[bkey] = sEV.mark()
            q_proj_done = sEV.mark()

            for h in range(2):
                for tbq in range(NTB):
                    g = h * NTB + tbq
                    bank = [pk0, pk1][g % 2]
                    bkey = "pk0" if g % 2 == 0 else "pk1"
                    p.do("tensor", (lambda h=h, tbq=tbq, bank=bank: (lambda t: t.matmul(
                        bank[:, :],
                        c.projpair[:, h * 128:(h + 1) * 128],
                        qT2[:, tbq * TB:(tbq + 1) * TB],
                        start=True, stop=True)))(),
                        after=[q_proj_done, ev_free[bkey], c.proj_ready,
                               qpT_free], inc=sMM)
                    mmm = sMM.mark()
                    p.do("scalar", (lambda bank=bank: (lambda s: s.activation(
                        csA[:, :], bank[:, :], AF.Sin, scale=0.125)))(),
                        after=[mmm, sEV.mark()], inc=sAC)
                    p.do("scalar", (lambda bank=bank: (lambda s: s.activation(
                        csB[:, :], bank[:, :], AF.Sin, scale=0.0625)))(),
                        after=[sAC.mark()], inc=sAC)
                    ev_free[bkey] = sAC.mark()
                    p.do("scalar", lambda s: s.activation(
                        csB[:, :], csB[:, :], AF.Square),
                        after=[sAC.mark()], inc=sAC)
                    trig = sAC.mark()
                    co = (h * 2 + 0) * NT + tbq * TB
                    so = (h * 2 + 1) * NT + tbq * TB
                    p.do("vector", (lambda co=co: (lambda v: v.tensor_scalar(
                        qpT[:, co:co + TB], csB[:, :],
                        -2.0 * SQI128, SQI128, ALU.mult, ALU.add)))(),
                        after=[trig], inc=sEV)
                    p.do("vector", (lambda so=so: (lambda v: v.tensor_scalar_mul(
                        qpT[:, so:so + TB], csA[:, :], SQI128)))(),
                        after=[trig], inc=sEV)
            qtrig_done = sEV.mark()
            qT2_free = sMM.mark()

            for h in range(2):
                for tbq in range(NTB):
                    g = h * NTB + tbq
                    bank = [pk0, pk1][g % 2]
                    bkey = "pk0" if g % 2 == 0 else "pk1"
                    for mc in range(2):
                        p.do("tensor", (lambda h=h, tbq=tbq, mc=mc, hp=hp, bank=bank: (lambda t: t.matmul(
                            bank[0:64, 0:512],
                            kvs[:, hp * 256 + (h * 2 + mc) * 64:
                                hp * 256 + (h * 2 + mc + 1) * 64],
                            qpT[:, (h * 2 + mc) * NT + tbq * TB:
                                (h * 2 + mc) * NT + (tbq + 1) * TB],
                            start=(mc == 0), stop=(mc == 1))))(),
                            after=[qtrig_done, kvs_done, ev_free[bkey]],
                            inc=(sMM if mc == 1 else None))
                    mmm = sMM.mark()
                    sl = g % 2
                    p.do("vector", (lambda bank=bank, sl=sl: (lambda v: v.tensor_copy(
                        otst[0:64, sl * 512:(sl + 1) * 512],
                        bank[0:64, 0:512])))(),
                        after=[mmm, otst_free[sl]], inc=sEV)
                    ev_free[bkey] = sEV.mark()
                    p.dma("sync", io["OT"][hp, h * 64:(h + 1) * 64,
                                           tbq * TB:(tbq + 1) * TB],
                          otst[0:64, sl * 512:(sl + 1) * 512],
                          after=[sEV.mark()], inc=sOT)
                    otst_free[sl] = sOT.mark()
            qpT_free = sMM.mark()
            pass2_end[hp] = sMM.mark()
            if hp + 2 < 8:
                wt_ready2[hp + 2] = load_pair(
                    hp + 2, wpA if hp % 2 == 0 else wpB, [pass2_end[hp]],
                    mats=(0,))
        ot_all = sOT.mark()

        p.flush("l0attn_p12")
        ctx12.close()
        woT = ctx.enter_context(nc.sbuf_tensor(_nm("a_woT"), [128, NCH * 1024], BF16))
        otb = ctx.enter_context(nc.sbuf_tensor(_nm("a_otb"), [128, 2 * NCH * 512], BF16))
        ybuf = ctx.enter_context(nc.sbuf_tensor(_nm("a_ybuf"), [128, NCH * 512], BF16))
        src = io["WT_l0a"][3].rearrange("cc p x -> p cc x")
        p.dma("gpsimd", woT.rearrange("p (cc x) -> p cc x", cc=NCH), src,
              after=[pass2_end[7]], inc=sWT)
        wo_ready = (sWT, sWT.count)
        otb_free = [NONE_MARK, NONE_MARK]
        ybuf_free = NONE_MARK
        for tb in range(NTB):
            sl = tb % 2
            for cc in range(NCH):
                p.dma("sync", otb[:, sl * 4096 + cc * TB: sl * 4096 + (cc + 1) * TB],
                      io["OT"][cc, :, tb * TB:(tb + 1) * TB],
                      after=[otb_free[sl], ot_all], inc=sXL)
            ot_tb = sXL.mark()
            for jc in range(NCH):
                bank = [pA, pB][jc % 2]
                bkey = "pA" if jc % 2 == 0 else "pB"
                for cc in range(NCH):
                    p.do("tensor", (lambda jc=jc, cc=cc, sl=sl, bank=bank: (lambda t: t.matmul(
                        bank[:, :],
                        woT[:, cc * 1024 + jc * 128: cc * 1024 + (jc + 1) * 128],
                        otb[:, sl * 4096 + cc * TB: sl * 4096 + (cc + 1) * TB],
                        start=(cc == 0), stop=(cc == NCH - 1))))(),
                        after=[wo_ready, ot_tb, ev_free[bkey], ybuf_free],
                        inc=(sMM if cc == NCH - 1 else None))
                p.do("vector", (lambda jc=jc, bank=bank: (lambda v: v.tensor_scalar_add(
                    ybuf[:, jc * TB:(jc + 1) * TB], bank[:, :],
                    bqkv[:, 3 * NCH + jc:3 * NCH + jc + 1])))(),
                    after=[sMM.mark(), bias_done], inc=sEV)
                ev_free[bkey] = sEV.mark()
            ybuf_free = None
            y_done = sEV.mark()
            otb_free[sl] = sMM.mark()

            if not final:
                def add_src(jc, dst, xr):
                    def f(v):
                        return v.tensor_add(dst, ybuf[:, jc * TB:(jc + 1) * TB], xr)
                    return f
                m = emit_epilogue_tb(p, io, c, A, pe0, pe1, tb, add_src,
                                     src_after=[y_done])
            else:
                def add_src4(jc, t4, dst, xr):
                    def f(v):
                        return v.tensor_add(
                            dst,
                            ybuf[:, jc * TB + t4 * 128: jc * TB + (t4 + 1) * 128],
                            xr)
                    return f
                m = emit_final_tb(p, io, c, A, pe0, pe1, tb, add_src4,
                                  src_after=[y_done])
            ybuf_free = m
    p.fence()
    p.flush("l0attn")



def emit_l1attn(p, io, c, final=False):
    nc = p.nc
    sXL = p.sem("sXL")
    sWT = p.sem("sWT")
    sMM = p.sem("sMM")
    sEV = p.sem("sEV")
    sAC = p.sem("sAC")
    sOT = p.sem("sOT")
    sLD = p.sem("sLD")
    sGP = p.sem("sGP")

    from contextlib import ExitStack
    with ExitStack() as ctx:
        epk = ctx.enter_context(nc.sbuf_tensor(_nm("t_epk"), [128, 32], F32))
        enk = ctx.enter_context(nc.sbuf_tensor(_nm("t_enk"), [128, 32], F32))
        Eneg = ctx.enter_context(nc.sbuf_tensor(_nm("t_Eneg"), [128, NT], BF16))
        Epos = ctx.enter_context(nc.sbuf_tensor(_nm("t_Epos"), [128, NT], BF16))
        Ff = ctx.enter_context(nc.sbuf_tensor(_nm("t_Ff"), [128, 4 * 512], F32))
        pS0 = ctx.enter_context(nc.psum_tensor(_nm(f"t_pS0"), [128, 512], F32))
        pS1 = ctx.enter_context(nc.psum_tensor(_nm(f"t_pS1"), [128, 512], F32))
        po = [ctx.enter_context(nc.psum_tensor(_nm(f"t_po{i}"), [128, 512], F32))
              for i in range(4)]
        pn = ctx.enter_context(nc.psum_tensor(_nm(f"t_pn"), [128, 512], F32))
        ptb = ctx.enter_context(nc.psum_tensor(_nm(f"t_ptb"), [128, 1024], BF16))
        A = alloc_epi(nc, ctx, final=final)

        with nc.sbuf_tensor(_nm("t_iot"), [128, NT], F32) as iot:
            p.do("gpsimd", lambda g: g.iota(
                epk[:, :], [[128, 32]], base=0, channel_multiplier=1,
                allow_small_or_imprecise_dtypes=True), inc=sGP)
            m1 = sGP.mark()
            p.do("scalar", lambda s: s.activation(enk[:, :], epk[:, :], AF.Exp,
                                                  scale=-CDEC),
                 after=[m1], inc=sAC)
            p.do("scalar", lambda s: s.activation(epk[:, :], epk[:, :], AF.Exp,
                                                  scale=CDEC),
                 after=[m1, sAC.mark()], inc=sAC)
            p.do("gpsimd", lambda g: g.iota(
                iot[:, :], [[1, NT]], base=0, channel_multiplier=0,
                allow_small_or_imprecise_dtypes=True), inc=sGP)
            m2 = sGP.mark()
            p.do("scalar", lambda s: s.activation(Eneg[:, :], iot[:, :], AF.Exp,
                                                  scale=-CDEC),
                 after=[m2, sAC.mark()], inc=sAC)
            p.do("vector", lambda v: v.tensor_scalar_mul(Eneg[:, :], Eneg[:, :],
                                                         float(L1SCALE)),
                 after=[sAC.mark()], inc=sEV)
            p.do("scalar", lambda s: s.activation(Epos[:, :], iot[:, :], AF.Exp,
                                                  scale=CDEC),
                 after=[m2, sAC.mark()], inc=sAC)
            p.do("vector", lambda v: v.tensor_scalar_mul(Epos[:, :], Epos[:, :],
                                                         float(L1SCALE)),
                 after=[sAC.mark()], inc=sEV)
            for o in range(4):
                p.do("gpsimd", (lambda o=o: (lambda g: g.iota(
                    Ff[:, o * 512:(o + 1) * 512], [[-1, 512]], base=o * 128,
                    channel_multiplier=1,
                    allow_small_or_imprecise_dtypes=True)))(), inc=sGP)
            m3 = sGP.mark()
            p.do("scalar", lambda s: s.activation(Ff[:, :], Ff[:, :], AF.Relu),
                 after=[m3, sAC.mark()], inc=sAC)
            p.do("scalar", lambda s: s.activation(Ff[:, :], Ff[:, :], AF.Exp,
                                                  scale=-2.0 * CDEC),
                 after=[sAC.mark()], inc=sAC)
            tables_done_ac = sAC.mark()
            tables_done_ev = sEV.mark()

            with ExitStack() as ctxA:
                xh = ctxA.enter_context(nc.sbuf_tensor(_nm("t_xh"), [128, NCH * NT], BF16))
                stg = ctxA.enter_context(nc.sbuf_tensor(_nm("t_stg"), [128, NT], BF16))
                for cc in range(NCH):
                    p.dma("sync", xh[:, cc * NT:(cc + 1) * NT], io["XLN"][cc, :, :],
                          inc=sXL)
                xh_done = sXL.mark()

                with nc.sbuf_tensor(_nm("t_wq"), [128, 8192], BF16) as wq:
                    p.dma("gpsimd", wq.rearrange("p (cc j) -> p cc j", cc=NCH),
                          io["L1_attn_Wq"].rearrange("(cc pp) j -> pp cc j", pp=128),
                          inc=sWT)
                    wq_rdy = (sWT, sWT.count)
                    stg_free = NONE_MARK
                    for jc in range(NCH):
                        for tbq in range(NTB):
                            bank = [pS0, pS1][tbq % 2]
                            for cc in range(NCH):
                                p.do("tensor", (lambda jc=jc, tbq=tbq, bank=bank, wq=wq, cc=cc: (lambda t: t.matmul(
                                    bank[:, :],
                                    wq[:, cc * 1024 + jc * 128: cc * 1024 + (jc + 1) * 128],
                                    xh[:, cc * NT + tbq * TB: cc * NT + (tbq + 1) * TB],
                                    start=(cc == 0), stop=(cc == NCH - 1))))(),
                                    after=[wq_rdy, xh_done, (sEV, sEV.count - 1)],
                                    inc=(sMM if cc == NCH - 1 else None))
                            p.do("vector", (lambda tbq=tbq, bank=bank: (lambda v: v.tensor_copy(
                                stg[:, tbq * TB:(tbq + 1) * TB], bank[:, :])))(),
                                after=[sMM.mark(), stg_free], inc=sEV)
                        p.dma("sync", io["QTd"][jc, :, :], stg[:, :],
                              after=[sEV.mark()], inc=sOT)
                        stg_free = sOT.mark()
                    qt_mm_end = sMM.mark()

                with (
                    nc.sbuf_tensor(_nm("t_wk"), [128, 8192], BF16) as wk,
                    nc.sbuf_tensor(_nm("t_wv"), [128, 8192], BF16) as wv,
                ):
                    p.dma("gpsimd", wk.rearrange("p (cc j) -> p cc j", cc=NCH),
                          io["L1_attn_Wk"].rearrange("(cc pp) j -> pp cc j", pp=128),
                          after=[qt_mm_end], inc=sWT)
                    p.dma("gpsimd", wv.rearrange("p (cc j) -> p cc j", cc=NCH),
                          io["L1_attn_Wv"].rearrange("(cc pp) j -> pp cc j", pp=128),
                          inc=sWT)
                    wkv_rdy = (sWT, sWT.count)
                    for jc in range(NCH):
                        for tbq in range(NTB):
                            bank = [pS0, pS1][tbq % 2]
                            for i in range(16):
                                wbuf = wk if i < 8 else wv
                                cc = i % 8
                                p.do("tensor", (lambda jc=jc, tbq=tbq, bank=bank, wbuf=wbuf, cc=cc, i=i: (lambda t: t.matmul(
                                    bank[:, :],
                                    wbuf[:, cc * 1024 + jc * 128: cc * 1024 + (jc + 1) * 128],
                                    xh[:, cc * NT + tbq * TB: cc * NT + (tbq + 1) * TB],
                                    start=(i == 0), stop=(i == 15))))(),
                                    after=[wkv_rdy, xh_done, (sEV, sEV.count - 1)],
                                    inc=(sMM if i == 15 else None))
                            p.do("vector", (lambda tbq=tbq, bank=bank: (lambda v: v.tensor_copy(
                                stg[:, tbq * TB:(tbq + 1) * TB], bank[:, :])))(),
                                after=[sMM.mark(), stg_free], inc=sEV)
                        p.dma("sync", io["KVSd"][jc, :, :], stg[:, :],
                              after=[sEV.mark()], inc=sOT)
                        stg_free = sOT.mark()
                    kvs_mm_end = sMM.mark()
                    kvs_st = sOT.mark()

                latT = ctxA.enter_context(nc.sbuf_tensor(_nm("t_latT"), [128, 2 * NT], BF16))
                kvstb = ctxA.enter_context(nc.sbuf_tensor(_nm("t_kvstb"), [128, 2 * 4096], BF16))
                kvstb_free = [NONE_MARK, NONE_MARK]
                with nc.sbuf_tensor(_nm("t_wd"), [128, 2048], BF16) as wd:
                    p.dma("gpsimd", wd.rearrange("p (cc j) -> p cc j", cc=NCH),
                          io["L1_attn_W_down_kv"].rearrange("(cc pp) j -> pp cc j", pp=128),
                          after=[kvs_mm_end], inc=sWT)
                    wd_rdy = (sWT, sWT.count)
                    for tbq in range(NTB):
                        sl = tbq % 2
                        for cc in range(NCH):
                            p.dma("sync", kvstb[:, sl * 4096 + cc * TB: sl * 4096 + (cc + 1) * TB],
                                  io["KVSd"][cc, :, tbq * TB:(tbq + 1) * TB],
                                  after=[kvstb_free[sl], kvs_st], inc=sXL)
                        kv_tb = sXL.mark()
                        for jc in range(2):
                            bank = [pS0, pS1][jc]
                            for cc in range(NCH):
                                p.do("tensor", (lambda jc=jc, tbq=tbq, sl=sl, bank=bank, wd=wd, cc=cc: (lambda t: t.matmul(
                                    bank[:, :],
                                    wd[:, cc * 256 + jc * 128: cc * 256 + (jc + 1) * 128],
                                    kvstb[:, sl * 4096 + cc * TB: sl * 4096 + (cc + 1) * TB],
                                    start=(cc == 0), stop=(cc == NCH - 1))))(),
                                    after=[wd_rdy, kv_tb, (sEV, sEV.count - 2)],
                                    inc=(sMM if cc == NCH - 1 else None))
                            p.do("vector", (lambda jc=jc, tbq=tbq, bank=bank: (lambda v: v.tensor_copy(
                                latT[:, jc * NT + tbq * TB: jc * NT + (tbq + 1) * TB],
                                bank[:, :])))(),
                                after=[sMM.mark()], inc=sEV)
                        kvstb_free[sl] = sMM.mark()
                    lat_done = sEV.mark()
                    lat_mm_end = sMM.mark()

                with (
                    nc.sbuf_tensor(_nm("t_wuk"), [128, 2 * 1024], BF16) as wuk,
                    nc.sbuf_tensor(_nm("t_wuv"), [128, 2 * 1024], BF16) as wuv,
                ):
                    p.dma("gpsimd", wuk.rearrange("p (cc j) -> p cc j", cc=2),
                          io["L1_attn_W_up_k"].rearrange("(cc pp) j -> pp cc j", pp=128),
                          after=[lat_mm_end], inc=sWT)
                    p.dma("gpsimd", wuv.rearrange("p (cc j) -> p cc j", cc=2),
                          io["L1_attn_W_up_v"].rearrange("(cc pp) j -> pp cc j", pp=128),
                          inc=sWT)
                    wu_rdy = (sWT, sWT.count)
                    for wub, dstd in ((wuk, io["KRd"]), (wuv, io["VRd"])):
                        for jc in range(NCH):
                            for tbq in range(NTB):
                                bank = [pS0, pS1][tbq % 2]
                                for cc in range(2):
                                    p.do("tensor", (lambda jc=jc, tbq=tbq, bank=bank, wub=wub, cc=cc: (lambda t: t.matmul(
                                        bank[:, :],
                                        wub[:, cc * 1024 + jc * 128: cc * 1024 + (jc + 1) * 128],
                                        latT[:, cc * NT + tbq * TB: cc * NT + (tbq + 1) * TB],
                                        start=(cc == 0), stop=(cc == 1))))(),
                                        after=[wu_rdy, lat_done, (sEV, sEV.count - 1)],
                                        inc=(sMM if cc == 1 else None))
                                p.do("vector", (lambda tbq=tbq, bank=bank: (lambda v: v.tensor_copy(
                                    stg[:, tbq * TB:(tbq + 1) * TB], bank[:, :])))(),
                                    after=[sMM.mark(), stg_free], inc=sEV)
                            p.dma("sync", dstd[jc, :, :], stg[:, :],
                                  after=[sEV.mark()], inc=sOT)
                            stg_free = sOT.mark()
                    proj_all_mm = sMM.mark()
                    proj_all_st = sOT.mark()
                p.flush("l1attn_proj")

            with ExitStack() as ctxB:
                krh = ctxB.enter_context(nc.sbuf_tensor(_nm("t_krh"), [64, NT], BF16))
                qh_ = ctxB.enter_context(nc.sbuf_tensor(_nm("t_qh"), [64, NT], BF16))
                vrh = ctxB.enter_context(nc.sbuf_tensor(_nm("t_vrh"), [64, NT], BF16))
                qmin = ctxB.enter_context(nc.sbuf_tensor(_nm("t_qmin"), [64, NT], BF16))
                qpl = ctxB.enter_context(nc.sbuf_tensor(_nm("t_qpl"), [64, NT], BF16))
                vph = ctxB.enter_context(nc.sbuf_tensor(_nm("t_vph"), [128, 32 * 65], BF16))
                ptile = ctxB.enter_context(nc.sbuf_tensor(_nm("t_ptile"), [128, 2 * 512], BF16))
                oth = ctxB.enter_context(nc.sbuf_tensor(_nm("t_oth"), [64, NT], BF16))
                rbsb = ctxB.enter_context(nc.sbuf_tensor(_nm("t_rbsb"), [128, 512], F32))
                rrow = ctxB.enter_context(nc.sbuf_tensor(_nm("t_rrow"), [128, 512], F32))

                h_in_free = NONE_MARK
                q_free = NONE_MARK
                vph_free = NONE_MARK
                oth_free = NONE_MARK
                pt_free = [NONE_MARK, NONE_MARK]
                pS_free = [NONE_MARK, NONE_MARK]
                po_free = [NONE_MARK] * 4
                pn_free = NONE_MARK
                ptb_free = tables_done_ev

                for h in range(NHEADS):
                    hc, hr = h // 2, (h % 2) * 64
                    p.dma("sync", krh[0:64, :], io["KRd"][hc, hr:hr + 64, :],
                          after=[h_in_free, proj_all_st], inc=sXL)
                    p.dma("sync", qh_[0:64, :], io["QTd"][hc, hr:hr + 64, :],
                          after=[h_in_free, proj_all_st], inc=sXL)
                    p.dma("sync", vrh[0:64, :], io["VRd"][hc, hr:hr + 64, :],
                          after=[h_in_free, proj_all_st], inc=sXL)
                    h_in = sXL.mark()
                    p.do("vector", lambda v: v.tensor_mul(
                        qmin[0:64, :], qh_[0:64, :], Eneg[0:64, :]),
                        after=[h_in, q_free, tables_done_ev], inc=sEV)
                    p.do("vector", lambda v: v.tensor_mul(
                        qpl[0:64, :], qh_[0:64, :], Epos[0:64, :]),
                        after=[h_in, q_free], inc=sEV)
                    q_done = sEV.mark()

                    for kcg in range(4):
                        for tl in range(8):
                            kc = kcg * 8 + tl
                            p.do("tensor", (lambda kc=kc, tl=tl: (lambda t: t.transpose(
                                ptb[:, tl * 128:tl * 128 + 64],
                                vrh[0:64, kc * 128:(kc + 1) * 128],
                                c.ident_bf[0:64, 0:64])))(),
                                after=[h_in, ptb_free, c.ready, vph_free],
                                inc=(sMM if tl == 7 else None))
                        vv = vph.rearrange("p (kc x) -> p kc x", x=65)
                        p.do("vector", (lambda kcg=kcg, vv=vv: (lambda v: v.tensor_copy(
                            vv[:, kcg * 8:(kcg + 1) * 8, 0:64],
                            ptb.rearrange("p (tl x) -> p tl x", x=128)[:, :, 0:64])))(),
                            after=[sMM.mark()], inc=sEV)
                        ptb_free = sEV.mark()
                    vv = vph.rearrange("p (kc x) -> p kc x", x=65)
                    p.do("gpsimd", lambda g: g.memset(vv[:, :, 64:65], 1.0),
                         after=[sEV.mark()], inc=sGP)
                    vph_done = sGP.mark()

                    for qbg in range(2):
                        for kc in range(32):
                            for qb4 in range(4):
                                sl = (kc * 4 + qb4) % 2
                                bank = [pS0, pS1][sl]
                                qb = qbg * 4 + qb4
                                qlo, qhi = qb * 512, qb * 512 + 511
                                klo, khi = kc * 128, kc * 128 + 127
                                if qlo > khi:
                                    kind = "up"
                                elif qhi < klo:
                                    kind = "lo"
                                else:
                                    kind = "st"
                                rhs = qmin if kind in ("up", "st") else qpl
                                p.do("tensor", (lambda kc=kc, qb=qb, bank=bank, rhs=rhs: (lambda t: t.matmul(
                                    bank[:, :],
                                    krh[0:64, kc * 128:(kc + 1) * 128],
                                    rhs[0:64, qb * TB:(qb + 1) * TB],
                                    start=True, stop=True)))(),
                                    after=[h_in, q_done, pS_free[sl]],
                                    inc=sMM)
                                smm = sMM.mark()
                                if kind == "st":
                                    p.do("vector", (lambda bank=bank, kc=kc: (lambda v: v.tensor_mul(
                                        bank[:, :], bank[:, :],
                                        Ff[:, (kc % 4) * 512:(kc % 4 + 1) * 512])))(),
                                        after=[smm, tables_done_ac], inc=sEV)
                                    smm = sEV.mark()
                                scol = epk if kind in ("up", "st") else enk
                                psl = qb4 % 2
                                p.do("scalar", (lambda bank=bank, scol=scol, kc=kc, psl=psl: (lambda s: s.activation(
                                    ptile[:, psl * 512:(psl + 1) * 512],
                                    bank[:, :], AF.Exp,
                                    scale=scol[:, kc:kc + 1])))(),
                                    after=[smm, pt_free[psl], tables_done_ac],
                                    inc=sAC)
                                pS_free[sl] = sAC.mark()
                                exp_done = sAC.mark()
                                p.do("tensor", (lambda kc=kc, qb4=qb4, psl=psl: (lambda t: t.matmul(
                                    po[qb4][0:65, :],
                                    vph[:, kc * 65:(kc + 1) * 65],
                                    ptile[:, psl * 512:(psl + 1) * 512],
                                    start=(kc == 0), stop=(kc == 31))))(),
                                    after=[exp_done, vph_done, po_free[qb4]],
                                    inc=sMM)
                                pt_free[psl] = sMM.mark()
                                if kc == 0:
                                    po_free[qb4] = NONE_MARK
                        pv_done = sMM.mark()
                        for qb4 in range(4):
                            qb = qbg * 4 + qb4
                            p.do("vector", (lambda qb4=qb4: (lambda v: v.reciprocal(
                                rrow[64:65, 0:512], po[qb4][64:65, :])))(),
                                after=[pv_done, pn_free], inc=sEV)
                            p.do("tensor", (lambda: (lambda t: t.matmul(
                                pn[0:64, :], c.ones_blk[64:65, 0:64],
                                rrow[64:65, 0:512], start=True, stop=True)))(),
                                after=[sEV.mark(), pn_free], inc=sMM)
                            p.do("vector", lambda v: v.tensor_copy(
                                rbsb[0:64, 0:512], pn[0:64, :]),
                                after=[sMM.mark()], inc=sEV)
                            pn_free = sEV.mark()
                            p.do("vector", (lambda qb=qb, qb4=qb4: (lambda v: v.tensor_mul(
                                oth[0:64, qb * TB:(qb + 1) * TB],
                                po[qb4][0:64, :], rbsb[0:64, 0:512])))(),
                                after=[sEV.mark(), oth_free], inc=sEV)
                            po_free[qb4] = sEV.mark()
                    oth_done = sEV.mark()
                    p.dma("sync", io["OT"][hc, hr:hr + 64, :], oth[0:64, :],
                          after=[oth_done], inc=sOT)
                    oth_free = sOT.mark()
                    h_in_free = sMM.mark()
                    q_free = sMM.mark()
                    vph_free = sMM.mark()
                ot_all = sOT.mark()
                p.flush("l1attn_heads")

            with ExitStack() as ctxC:
                woB = ctxC.enter_context(nc.sbuf_tensor(_nm("t_woB"), [128, 8192], BF16))
                otb = ctxC.enter_context(nc.sbuf_tensor(_nm("t_otb"), [128, 2 * 4096], BF16))
                ybuf = ctxC.enter_context(nc.sbuf_tensor(_nm("t_ybuf"), [128, 4096], BF16))
                p.dma("gpsimd", woB.rearrange("p (cc j) -> p cc j", cc=NCH),
                      io["L1_attn_Wo"].rearrange("(cc pp) j -> pp cc j", pp=128),
                      inc=sWT)
                wo_rdy = (sWT, sWT.count)
                otb_free = [NONE_MARK, NONE_MARK]
                ybuf_free = NONE_MARK
                ev_free = {"pS0": NONE_MARK, "pS1": NONE_MARK}
                for tb in range(NTB):
                    sl = tb % 2
                    for cc in range(NCH):
                        p.dma("sync", otb[:, sl * 4096 + cc * TB: sl * 4096 + (cc + 1) * TB],
                              io["OT"][cc, :, tb * TB:(tb + 1) * TB],
                              after=[otb_free[sl], ot_all], inc=sXL)
                    ot_tb = sXL.mark()
                    for jc in range(NCH):
                        bank = [pS0, pS1][jc % 2]
                        bkey = "pS0" if jc % 2 == 0 else "pS1"
                        for cc in range(NCH):
                            p.do("tensor", (lambda jc=jc, cc=cc, sl=sl, bank=bank: (lambda t: t.matmul(
                                bank[:, :],
                                woB[:, cc * 1024 + jc * 128: cc * 1024 + (jc + 1) * 128],
                                otb[:, sl * 4096 + cc * TB: sl * 4096 + (cc + 1) * TB],
                                start=(cc == 0), stop=(cc == NCH - 1))))(),
                                after=[wo_rdy, ot_tb, ev_free[bkey], ybuf_free],
                                inc=(sMM if cc == NCH - 1 else None))
                        p.do("vector", (lambda jc=jc, bank=bank: (lambda v: v.tensor_copy(
                            ybuf[:, jc * TB:(jc + 1) * TB], bank[:, :])))(),
                            after=[sMM.mark()], inc=sEV)
                        ev_free[bkey] = sEV.mark()
                    y_done = sEV.mark()
                    otb_free[sl] = sMM.mark()

                    if not final:
                        def add_src(jc, dst, xr):
                            def f(v):
                                return v.tensor_add(
                                    dst, ybuf[:, jc * TB:(jc + 1) * TB], xr)
                            return f
                        m = emit_epilogue_tb(p, io, c, A, po[0], po[1], tb,
                                             add_src, src_after=[y_done])
                    else:
                        def add_src4(jc, t4, dst, xr):
                            def f(v):
                                return v.tensor_add(
                                    dst,
                                    ybuf[:, jc * TB + t4 * 128: jc * TB + (t4 + 1) * 128],
                                    xr)
                            return f
                        m = emit_final_tb(p, io, c, A, po[0], po[1], tb,
                                          add_src4, src_after=[y_done])
                    ybuf_free = m
                p.fence()
                p.flush("l1attn")



def emit_l1ffn(p, io, c, final=False):
    nc = p.nc
    sXL = p.sem("sXL")
    sWT = p.sem("sWT")
    sYP = p.sem("sYP")
    sGT = p.sem("sGT")
    sGE = p.sem("sGE")
    sHM = p.sem("sHM")
    sLD = p.sem("sLD")

    from contextlib import ExitStack
    with ExitStack() as ctx:
        xln = ctx.enter_context(nc.sbuf_tensor(_nm("n_xln"), [128, NCH * TB], BF16))
        wa = ctx.enter_context(nc.sbuf_tensor(_nm("n_wa"), [128, 8192], BF16))
        wb = ctx.enter_context(nc.sbuf_tensor(_nm("n_wb"), [128, 8192], BF16))
        xta = ctx.enter_context(nc.sbuf_tensor(_nm("n_xta"), [128, NCH * TB], BF16))
        xtb = ctx.enter_context(nc.sbuf_tensor(_nm("n_xtb"), [128, NCH * TB], BF16))
        mbc8 = ctx.enter_context(nc.sbuf_tensor(_nm("n_mbc8"), [128, NE * TB], BF16))
        rwt = ctx.enter_context(nc.sbuf_tensor(_nm("n_rwt"), [128, NCH * NE], BF16))
        rwin = ctx.enter_context(nc.sbuf_tensor(_nm("n_rwin"), [NE, DM], F32))
        rbsb = ctx.enter_context(nc.sbuf_tensor(_nm("n_rb"), [NE, 1], F32))
        ebsb = ctx.enter_context(nc.sbuf_tensor(_nm("n_eb"), [NE, DM], BF16))
        glog = ctx.enter_context(nc.sbuf_tensor(_nm("n_glog"), [NE, TB], F32))
        gt = ctx.enter_context(nc.sbuf_tensor(_nm("n_gt"), [128, 4 * NE], F32))
        gms = ctx.enter_context(nc.sbuf_tensor(_nm("n_gms"), [128, NE], F32))
        grows = ctx.enter_context(nc.sbuf_tensor(_nm("n_grows"), [128, 8], F32))
        mt = ctx.enter_context(nc.sbuf_tensor(_nm("n_mt"), [128, 4 * NE], F32))
        mT = ctx.enter_context(nc.sbuf_tensor(_nm("n_mT"), [NE, TB], BF16))
        pb = [ctx.enter_context(nc.psum_tensor(_nm(f"n_pb{i}"), [128, 512], F32))
              for i in range(NE)]
        A = alloc_epi(nc, ctx, final=final)

        bank_add = [NONE_MARK] * NE
        p.dma("sync", rwin[:, :], io["L1_ffn_router_w"][:, :], inc=sLD)
        p.dma("gpsimd", rbsb[:, :],
              io["L1_ffn_router_b"].rearrange("(e one) -> e one", one=1), inc=sLD)
        p.dma("gpsimd", ebsb[:, :], io["L1_ffn_expert_b"][:, :], inc=sLD)
        ld1 = sLD.mark()
        for cc in range(NCH):
            p.do("tensor", (lambda cc=cc: (lambda t: t.transpose(
                pb[0][:, cc * NE:(cc + 1) * NE],
                rwin[:, cc * 128:(cc + 1) * 128],
                c.ident_f32[0:NE, 0:NE])))(),
                after=[ld1, c.ready], inc=(sGT if cc == NCH - 1 else None))
        p.do("vector", lambda v: v.tensor_copy(rwt[:, :], pb[0][:, 0:NCH * NE]),
             after=[sGT.mark()], inc=sGE)
        rwt_done = sGE.mark()
        bank_add[0] = sGE.mark()

        wt_ready = {}
        xln_free = NONE_MARK
        glog_free = NONE_MARK
        mbc_free = NONE_MARK
        mT_free = NONE_MARK
        xt_free = [NONE_MARK, NONE_MARK]

        def load_w(kk, after):
            e = kk % NE
            buf = wa if kk % 2 == 0 else wb
            src = io["W1T_l1"][e].rearrange("cc p j -> p cc j")
            p.dma("gpsimd", buf.rearrange("p (cc j) -> p cc j", cc=NCH),
                  src, after=after, inc=sWT)
            return sWT.count

        wt_ready[0] = load_w(0, [])
        wt_ready[1] = load_w(1, [])

        for tb in range(NTB):
            ts, te = tb * TB, (tb + 1) * TB
            for cc in range(NCH):
                p.dma("sync", xln[:, cc * TB:(cc + 1) * TB],
                      io["XLN"][cc, :, ts:te], after=[xln_free], inc=sXL)
            xh_done = sXL.mark()

            for cc in range(NCH):
                p.do("tensor", (lambda cc=cc: (lambda t: t.matmul(
                    pb[7][0:NE, :], rwt[:, cc * NE:(cc + 1) * NE],
                    xln[:, cc * TB:(cc + 1) * TB],
                    start=(cc == 0), stop=(cc == NCH - 1))))(),
                    after=[xh_done, rwt_done, bank_add[7]],
                    inc=(sGT if cc == NCH - 1 else None))
            p.do("vector", lambda v: v.tensor_scalar_add(
                glog[:, :], pb[7][0:NE, :], rbsb[:, 0:1]),
                after=[sGT.mark(), glog_free], inc=sGE)
            gl_done = sGE.mark()
            tp_free = sGE.mark()
            for tl in range(4):
                p.do("tensor", (lambda tl=tl: (lambda t: t.transpose(
                    pb[7][:, tl * NE:(tl + 1) * NE],
                    glog[:, tl * 128:(tl + 1) * 128],
                    c.ident_f32[0:NE, 0:NE])))(),
                    after=[gl_done, tp_free], inc=sGT)
                p.do("vector", (lambda tl=tl: (lambda v: v.tensor_copy(
                    gt[:, tl * NE:(tl + 1) * NE],
                    pb[7][:, tl * NE:(tl + 1) * NE])))(),
                    after=[sGT.mark()], inc=sGE)
                tp_free = sGE.mark()
            gt_all = sGE.mark()
            for tl in range(4):
                gtc = gt[:, tl * NE:(tl + 1) * NE]
                mx = grows[:, 0:1]; nmx = grows[:, 1:2]
                ss = grows[:, 2:3]; rs = grows[:, 3:4]
                p.do("vector", (lambda gtc=gtc, mx=mx: (lambda v: v.reduce_max(
                    mx, gtc, AXX)))(), after=[sGE.mark()], inc=sGE)
                p.do("vector", (lambda mx=mx, nmx=nmx: (lambda v: v.tensor_scalar_mul(
                    nmx, mx, -1.0)))(), after=[sGE.mark()], inc=sGE)
                p.do("scalar", (lambda gtc=gtc, nmx=nmx: (lambda s: s.activation(
                    gms[:, :], gtc, AF.Exp, bias=nmx)))(),
                    after=[sGE.mark()], inc=sGE)
                p.do("vector", (lambda ss=ss: (lambda v: v.reduce_sum(
                    ss, gms[:, :], AXX)))(), after=[sGE.mark()], inc=sGE)
                p.do("vector", (lambda ss=ss, rs=rs: (lambda v: v.reciprocal(
                    rs, ss)))(), after=[sGE.mark()], inc=sGE)
                p.do("vector", (lambda gtc=gtc, mx=mx: (lambda v: v.tensor_single_scalar(
                    gms[:, :], gtc, mx, ALU.is_ge)))(),
                    after=[sGE.mark()], inc=sGE)
                p.do("vector", (lambda tl=tl, rs=rs: (lambda v: v.tensor_scalar(
                    mt[:, tl * NE:(tl + 1) * NE], gms[:, :], rs, 0.5,
                    ALU.mult, ALU.mult)))(),
                    after=[sGE.mark()], inc=sGE)
                p.do("tensor", (lambda tl=tl: (lambda t: t.transpose(
                    pb[7][0:NE, tl * 128:(tl + 1) * 128],
                    mt[:, tl * NE:(tl + 1) * NE], c.ident_f32[:, :])))(),
                    after=[sGE.mark(), gt_all, mT_free], inc=sGT)
            glog_free = sGT.mark()
            p.do("vector", lambda v: v.tensor_copy(mT[0:NE, :], pb[7][0:NE, 0:TB]),
                 after=[sGT.mark()], inc=sGE)
            bank_add[7] = sGE.mark()
            mT_done = sGE.mark()

            mbc_bank_free = [bank_add[0], bank_add[1]]
            for e in range(NE):
                bk = pb[e % 2]
                p.do("tensor", (lambda e=e, bk=bk: (lambda t: t.matmul(
                    bk[:, :], c.sel_bf[:, e * 128:(e + 1) * 128],
                    mT[0:NE, :], start=True, stop=True)))(),
                    after=[mT_done, mbc_bank_free[e % 2], mbc_free],
                    inc=sGT)
                p.do("vector", (lambda e=e, bk=bk: (lambda v: v.tensor_copy(
                    mbc8[:, e * TB:(e + 1) * TB], bk[:, :])))(),
                    after=[sGT.mark()], inc=sGE)
                mbc_bank_free[e % 2] = sGE.mark()
            bank_add[0] = mbc_bank_free[0]
            bank_add[1] = mbc_bank_free[1]
            mbc_done = sGE.mark()

            for e in range(NE):
                kk = tb * NE + e
                buf = wa if kk % 2 == 0 else wb
                wready = (sWT, wt_ready[kk])
                xt = xta if e % 2 == 0 else xtb
                for cc in range(NCH):
                    p.do("vector", (lambda cc=cc, e=e, xt=xt: (lambda v: v.tensor_mul(
                        xt[:, cc * TB:(cc + 1) * TB],
                        xln[:, cc * TB:(cc + 1) * TB],
                        mbc8[:, e * TB:(e + 1) * TB])))(),
                        after=[mbc_done, xh_done, xt_free[e % 2]],
                        inc=sHM)
                xt_done = sHM.mark()
                for jc in range(NCH):
                    for dc in range(NCH):
                        p.do("tensor", (lambda jc=jc, dc=dc, e=e, buf=buf, xt=xt: (lambda t: t.matmul(
                            pb[jc][:, :],
                            buf[:, dc * 1024 + jc * 128: dc * 1024 + (jc + 1) * 128],
                            xt[:, dc * TB:(dc + 1) * TB],
                            start=(e == 0 and dc == 0),
                            stop=(e == NE - 1 and dc == NCH - 1))))(),
                            after=[xt_done, wready,
                                   (bank_add[jc] if e == 0 else None)],
                            inc=(sYP if (jc == NCH - 1 and dc == NCH - 1) else None))
                    if e == 0:
                        p.do("tensor", (lambda jc=jc: (lambda t: t.matmul(
                            pb[jc][:, :], ebsb[:, jc * 128:(jc + 1) * 128],
                            mT[0:NE, :], start=False, stop=False)))(),
                            after=[mT_done, ld1], inc=None)
                xt_free[e % 2] = sYP.mark()
                if kk + 2 < NTB * NE:
                    wt_ready[kk + 2] = load_w(kk + 2, [sYP.mark()])
            y_done = sYP.mark()
            xt_free = [y_done, y_done]
            xln_free = y_done
            mT_free = y_done
            mbc_free = y_done

            if not final:
                def add_src(jc, dst, xr):
                    def f(v):
                        return v.tensor_add(dst, pb[jc][:, :], xr)
                    return f
                m = emit_epilogue_tb(p, io, c, A, pb[0], pb[1], tb, add_src,
                                     src_after=[y_done])
            else:
                def add_src4(jc, t4, dst, xr):
                    def f(v):
                        return v.tensor_add(
                            dst, pb[jc][:, t4 * 128:(t4 + 1) * 128], xr)
                    return f
                m = emit_final_tb(p, io, c, A, pb[0], pb[1], tb, add_src4,
                                  src_after=[y_done])
            for jc in range(NE):
                bank_add[jc] = m
    p.fence()
    p.flush("l1ffn")



def emit_final(p, io, c):
    nc = p.nc
    sXL = p.sem("sXL")
    sMM = p.sem("sMM")
    sEV = p.sem("sEV")
    sST = p.sem("sST")
    with (
        nc.sbuf_tensor(_nm("z_in"), [128, 2 * 4096], F32) as zin,
        nc.sbuf_tensor(_nm("z_out"), [128, 1024], F32) as zout,
        nc.psum_tensor(_nm("z_p0"), [128, 512], F32) as zp0,
        nc.psum_tensor(_nm("z_p1"), [128, 512], F32) as zp1,
    ):
        in_free = [NONE_MARK, NONE_MARK]
        bank_free = [NONE_MARK, NONE_MARK]
        out_free = NONE_MARK
        for tb in range(NTB):
            sl = tb % 2
            for cc in range(NCH):
                p.dma("sync", zin[:, sl * 4096 + cc * TB: sl * 4096 + (cc + 1) * TB],
                      io["XRES"][cc, :, tb * TB:(tb + 1) * TB],
                      after=[in_free[sl]], inc=sXL)
            ld = sXL.mark()
            for t4 in range(4):
                for cc in range(NCH):
                    bk = [zp0, zp1][cc % 2]
                    p.do("tensor", (lambda cc=cc, t4=t4, sl=sl, bk=bk: (lambda t: t.transpose(
                        bk[:, 0:128],
                        zin[:, sl * 4096 + cc * TB + t4 * 128: sl * 4096 + cc * TB + (t4 + 1) * 128],
                        c.ident_f32[:, :])))(),
                        after=[ld, bank_free[cc % 2], c.ready], inc=sMM)
                    p.do("vector", (lambda cc=cc, bk=bk: (lambda v: v.tensor_copy(
                        zout[:, cc * 128:(cc + 1) * 128], bk[:, 0:128])))(),
                        after=[sMM.mark(), out_free], inc=sEV)
                    bank_free[cc % 2] = sEV.mark()
                p.dma("sync", io["out"][tb * TB + t4 * 128: tb * TB + (t4 + 1) * 128, :],
                      zout[:, :], after=[sEV.mark()], inc=sST)
                out_free = sST.mark()
            in_free[sl] = sMM.mark()
    p.fence()
    p.flush("final")



def build_program(parts, xio="internal"):
    nc = bass.Bass("TRN2", target_bir_lowering=False)
    io = declare_io(nc, parts, xio=xio)
    p = Prog(nc)
    from contextlib import ExitStack
    with ExitStack() as ctx:
        c = emit_consts(p, io, ctx, need_proj=any(
            pt.startswith("l0attn") for pt in parts))
        p.flush("consts")
        for part in parts:
            if part == "prologue":
                emit_prologue(p, io, c)
            elif part == "l0ffn":
                emit_l0ffn(p, io, c, final=False)
            elif part == "l0ffn_final":
                emit_l0ffn(p, io, c, final=True)
            elif part == "l1prep":
                emit_l1prep(p, io, c)
            elif part == "l0aprep":
                emit_l0aprep(p, io, c)
            elif part == "l0attn":
                emit_l0attn(p, io, c, final=False)
            elif part == "l1attn":
                emit_l1attn(p, io, c, final=False)
            elif part == "l1ffn":
                emit_l1ffn(p, io, c, final=False)
            elif part == "l1ffn_final":
                emit_l1ffn(p, io, c, final=True)
            elif part == "final":
                emit_final(p, io, c)
        p.fence()
        p.flush("tail")
    return nc, io


def _in_maps(parts, inputs, extra=None):
    need = set()
    for part in parts:
        need |= PART_INPUTS[part.replace("_final", "")]
    xs = np.ascontiguousarray(inputs["L0_attn_x"]) if "L0_attn_x" in need else None
    maps = []
    for core in range(8):
        m = {}
        for k in sorted(need):
            if k == "L0_attn_x":
                m[k] = xs[core]
            else:
                m[k] = np.ascontiguousarray(inputs[k])
        if extra:
            for k, v in extra.items():
                m[k] = v[core]
        maps.append(m)
    return maps


DEFAULT_PARTS = ["prologue", "l0ffn", "l0aprep", "l0attn", "l0ffn",
                 "l1prep", "l1ffn", "l1attn", "l1ffn", "final"]


def kernel(**inputs):
    parts = DEFAULT_PARTS
    nc, io = build_program(parts, xio="internal")
    maps = _in_maps(parts, inputs)
    res = run_bass_kernel_spmd(nc, maps, core_ids=list(range(8)))
    return np.stack([r["out"] for r in res.results], axis=0)


# version 06
# speedup vs baseline: 1.1995x; 1.1995x over previous
import math
import numpy as np
import ml_dtypes
import concourse.bass as bass
import concourse.mybir as mybir
from concourse.bass_utils import run_bass_kernel_spmd

F32 = mybir.dt.float32
BF16 = mybir.dt.bfloat16
AF = mybir.ActivationFunctionType
ALU = mybir.AluOpType
AXX = mybir.AxisListType.X

NT = 4096
DM = 1024
NCH = 8
TB = 512
NTB = NT // TB
HALF = 2048
DH = 4096
NE = 8
DKM = 128
DHEAD = 64
NHEADS = 16
DLAT = 256
CDEC = 1e-4
EPS = 1e-5
L1SCALE = 1.0 / (math.sqrt(1024.0) + 1e-8)

ENGINES = ("tensor", "vector", "scalar", "gpsimd", "sync")

_NMC = [0]


def _nm(name):
    _NMC[0] += 1
    return f"{name}_{_NMC[0]}"


NONE_MARK = (None, 0)
_DBG_OT = [False]


class Sem:
    def __init__(self, name, handle):
        self.name = name
        self.h = handle
        self.count = 0

    def mark(self):
        return (self, self.count)


class Prog:

    def __init__(self, nc):
        self.nc = nc
        self.recs = {e: [] for e in ENGINES}
        self.sems = {}
        self.known = {}

    def sem(self, name):
        if name not in self.sems:
            self.sems[name] = Sem(name, self.nc.alloc_semaphore(name=name))
        return self.sems[name]

    def do(self, eng, fn, after=(), inc=None, inc_n=1, dma=False):
        waits = []
        for a in after:
            if a is None:
                continue
            s, v = (a if isinstance(a, tuple) else (a, a.count))
            if s is None or v <= 0:
                continue
            k = (eng, s.name)
            if v > self.known.get(k, 0):
                waits.append((s, v))
                self.known[k] = v
        tag = None
        if inc is not None:
            inc.count += inc_n
            tag = (inc, inc_n)
        self.recs[eng].append((waits, fn, tag))
        return (inc, inc.count) if inc is not None else None

    def dma(self, eng, dst, src, after=(), inc=None, **kw):
        return self.do(eng, lambda e: e.dma_start(dst, src, **kw),
                       after=after, inc=inc, inc_n=16, dma=True)

    def fence(self):
        vals = [(s, s.count) for s in self.sems.values() if s.count > 0]
        for eng in ENGINES:
            waits = []
            for s, v in vals:
                k = (eng, s.name)
                if v > self.known.get(k, 0):
                    waits.append((s, v))
                    self.known[k] = v
            if waits:
                self.recs[eng].append((waits, None, None))

    def flush(self, name="blk"):
        recs = self.recs
        self.recs = {e: [] for e in ENGINES}
        if not any(recs.values()):
            return
        with self.nc.Block(_nm(name)) as blk:
            def make(rlist):
                def run(e):
                    for waits, fn, tag in rlist:
                        for (s, v) in waits:
                            e.wait_ge(s.h, v)
                        if fn is not None:
                            ins = fn(e)
                            if tag is not None:
                                ins.then_inc(tag[0].h, tag[1])
                return run
            for nm, meth in (("tensor", blk.tensor), ("vector", blk.vector),
                             ("scalar", blk.scalar), ("gpsimd", blk.gpsimd),
                             ("sync", blk.sync)):
                if recs[nm]:
                    meth(make(recs[nm]))



INPUT_SHAPES = {
    "L0_attn_x": (NT, DM),
    "L0_attn_proj_mat": (DHEAD, DKM),
    "L0_attn_Wq": (DM, DM), "L0_attn_bq": (DM,),
    "L0_attn_Wk": (DM, DM), "L0_attn_bk": (DM,),
    "L0_attn_Wv": (DM, DM), "L0_attn_bv": (DM,),
    "L0_attn_Wo": (DM, DM), "L0_attn_bo": (DM,),
    "L0_ffn_Wg": (DM, NE), "L0_ffn_bg": (NE,),
    "L0_ffn_W1": (NE, DM, DH), "L0_ffn_b1": (NE, DH),
    "L0_ffn_W2": (NE, DH, DM), "L0_ffn_b2": (NE, DM),
    "L1_attn_Wq": (DM, DM), "L1_attn_Wk": (DM, DM),
    "L1_attn_Wv": (DM, DM), "L1_attn_Wo": (DM, DM),
    "L1_attn_W_down_kv": (DM, DLAT),
    "L1_attn_W_up_k": (DLAT, DM), "L1_attn_W_up_v": (DLAT, DM),
    "L1_ffn_router_w": (NE, DM), "L1_ffn_router_b": (NE,),
    "L1_ffn_expert_w": (NE, DM, DM), "L1_ffn_expert_b": (NE, DM),
}

PART_INPUTS = {
    "prologue": {"L0_attn_x"},
    "l0ffn": {"L0_ffn_Wg", "L0_ffn_bg", "L0_ffn_W1", "L0_ffn_b1",
              "L0_ffn_W2", "L0_ffn_b2"},
    "l0attn": {"L0_attn_proj_mat", "L0_attn_Wq", "L0_attn_bq", "L0_attn_Wk",
               "L0_attn_bk", "L0_attn_Wv", "L0_attn_bv", "L0_attn_Wo",
               "L0_attn_bo"},
    "l1attn": {"L1_attn_Wq", "L1_attn_Wk", "L1_attn_Wv", "L1_attn_Wo",
               "L1_attn_W_down_kv", "L1_attn_W_up_k", "L1_attn_W_up_v"},
    "l1ffn": {"L1_ffn_router_w", "L1_ffn_router_b", "L1_ffn_expert_w",
              "L1_ffn_expert_b"},
    "l1prep": {"L1_ffn_expert_w"},
    "l0aprep": {"L0_attn_Wq", "L0_attn_Wk", "L0_attn_Wv", "L0_attn_Wo"},
    "final": set(),
}


def declare_io(nc, parts, xio="internal"):
    d = {}
    need = set()
    for part in parts:
        need |= PART_INPUTS[part.replace("_final", "")]
    for name in sorted(need):
        d[name] = nc.dram_tensor(name, list(INPUT_SHAPES[name]), F32,
                                 kind="ExternalInput").ap()
    if any(pt.endswith("_final") or pt == "final" for pt in parts):
        d["out"] = nc.dram_tensor("out", [NT, DM], F32, kind="ExternalOutput").ap()
    if any(pt.startswith("l1prep") or pt.startswith("l1ffn") for pt in parts):
        d["W1T_l1"] = nc.dram_tensor("W1T_l1", [NE, NCH, 128, DM], BF16).ap()
    if any(pt.startswith("l0a") for pt in parts):
        d["WT_l0a"] = nc.dram_tensor("WT_l0a", [4, NCH, 128, DM], BF16).ap()
    if any(pt.startswith("l0a") or pt.startswith("l1attn") for pt in parts):
        d["OT"] = nc.dram_tensor("OT", [NCH, 128, NT], BF16,
                                 kind=("ExternalOutput" if _DBG_OT[0] else "Internal")).ap()
    if any(pt.startswith("l1attn") for pt in parts):
        d["QTd"] = nc.dram_tensor("QTd", [NCH, 128, NT], BF16).ap()
        if _DBG_OT[0]:
            d["DBGV"] = nc.dram_tensor("DBGV", [128, NHEADS * 32 * 65], BF16,
                                       kind="ExternalOutput").ap()
        d["KVSd"] = nc.dram_tensor("KVSd", [NCH, 128, NT], BF16).ap()
        d["KRd"] = nc.dram_tensor("KRd", [NCH, 128, NT], BF16).ap()
        d["VRd"] = nc.dram_tensor("VRd", [NCH, 128, NT], BF16).ap()
    if xio == "internal":
        d["XLN"] = nc.dram_tensor("XLN", [NCH, 128, NT], BF16).ap()
        d["XRES"] = nc.dram_tensor("XRES", [NCH, 128, NT], F32).ap()
        d["XLN_o"], d["XRES_o"] = d["XLN"], d["XRES"]
    else:
        if xio in ("in", "inout"):
            d["XLN"] = nc.dram_tensor("XLN", [NCH, 128, NT], BF16,
                                      kind="ExternalInput").ap()
            d["XRES"] = nc.dram_tensor("XRES", [NCH, 128, NT], F32,
                                       kind="ExternalInput").ap()
        if xio in ("out", "inout"):
            d["XLN_o"] = nc.dram_tensor("XLN_o", [NCH, 128, NT], BF16,
                                        kind="ExternalOutput").ap()
            d["XRES_o"] = nc.dram_tensor("XRES_o", [NCH, 128, NT], F32,
                                         kind="ExternalOutput").ap()
        if xio == "in":
            d["XLN_o"], d["XRES_o"] = d["XLN"], d["XRES"]
        if xio == "out":
            d["XLN"], d["XRES"] = d["XLN_o"], d["XRES_o"]
    return d



class Consts:
    pass


def emit_consts(p, io, ctx, need_proj=False):
    nc = p.nc
    c = Consts()
    c.ident_bf = ctx.enter_context(nc.sbuf_tensor(_nm("c_ident_bf"), [128, 128], BF16))
    c.ident_f32 = ctx.enter_context(nc.sbuf_tensor(_nm("c_ident_f32"), [128, 128], F32))
    c.ones_col_bf = ctx.enter_context(nc.sbuf_tensor(_nm("c_ones_col_bf"), [128, 1], BF16))
    c.ones_row_f32 = ctx.enter_context(nc.sbuf_tensor(_nm("c_ones_row_f32"), [1, 128], F32))
    c.sel_bf = ctx.enter_context(nc.sbuf_tensor(_nm("c_sel_bf"), [NE, NE * 128], BF16))
    c.eps_col = ctx.enter_context(nc.sbuf_tensor(_nm("c_eps_col"), [128, 1], F32))
    c.ones_blk = ctx.enter_context(nc.sbuf_tensor(_nm("c_ones_blk"), [128, 128], F32))
    sGP = p.sem("sGP")
    sLD = p.sem("sLD")

    def ident_fill(t):
        def f(g):
            return g.affine_select(out=t[:, :], in_=t[:, :],
                                   compare_op=ALU.not_equal, fill=1.0,
                                   base=0, pattern=[[-1, 128]],
                                   channel_multiplier=1)
        return f

    p.do("gpsimd", lambda g: g.memset(c.ident_bf[:, :], 0.0))
    p.do("gpsimd", ident_fill(c.ident_bf))
    p.do("gpsimd", lambda g: g.memset(c.ident_f32[:, :], 0.0))
    p.do("gpsimd", ident_fill(c.ident_f32))
    p.do("gpsimd", lambda g: g.memset(c.ones_col_bf[:, :], 1.0))
    p.do("gpsimd", lambda g: g.memset(c.ones_row_f32[:, :], 1.0))
    p.do("gpsimd", lambda g: g.memset(c.eps_col[:, :], float(EPS)))
    p.do("gpsimd", lambda g: g.memset(c.ones_blk[:, :], 1.0))
    p.do("gpsimd", lambda g: g.memset(c.sel_bf[:, :], 0.0))
    for e in range(NE):
        def selfill(e=e):
            def f(g):
                return g.affine_select(
                    out=c.sel_bf[:, e * 128:(e + 1) * 128],
                    in_=c.sel_bf[:, e * 128:(e + 1) * 128],
                    compare_op=ALU.not_equal, fill=1.0,
                    base=-e, pattern=[[0, 128]], channel_multiplier=1)
            return f
        p.do("gpsimd", selfill(), inc=(sGP if e == NE - 1 else None))
    c.ready = sGP.mark()
    if need_proj:
        c.projpair = ctx.enter_context(nc.sbuf_tensor(_nm("c_projpair"), [128, 256], BF16))
        p.do("gpsimd", lambda g: g.memset(c.projpair[:, :], 0.0), inc=sGP)
        mz = sGP.mark()
        p.dma("gpsimd", c.projpair[0:64, 0:128], io["L0_attn_proj_mat"][:, :],
              after=[mz], inc=sLD)
        p.dma("gpsimd", c.projpair[64:128, 128:256], io["L0_attn_proj_mat"][:, :],
              after=[mz], inc=sLD)
        c.proj_ready = sLD.mark()
    return c



class EpiBufs:
    pass


def alloc_epi(nc, ctx, final=False):
    A = EpiBufs()
    A.stage = ctx.enter_context(nc.sbuf_tensor(_nm("ep_stage"), [128, 2 * TB], F32))
    A.xbf = ctx.enter_context(nc.sbuf_tensor(_nm("ep_xbf"), [128, 4096], BF16))
    A.sq = ctx.enter_context(nc.sbuf_tensor(_nm("ep_sq"), [128, 2 * TB], BF16))
    A.xr = ctx.enter_context(nc.sbuf_tensor(_nm("ep_xr"), [128, 2 * TB], F32))
    A.rows = ctx.enter_context(nc.sbuf_tensor(_nm("ep_rows"), [1, 2 * TB], F32))
    A.tmp = ctx.enter_context(nc.sbuf_tensor(_nm("ep_tmp"), [128, TB], F32))
    if final:
        A.outst = ctx.enter_context(nc.sbuf_tensor(_nm("ep_outst"), [128, 1024], F32))
    A.prev_done = NONE_MARK
    A.xr_free = [NONE_MARK, NONE_MARK]
    A.st_free = [NONE_MARK, NONE_MARK]
    return A


def emit_epilogue_tb(p, io, c, A, pe0, pe1, tb, add_src, src_after=()):
    sXR = p.sem("sXR")
    sEPs = p.sem("sEPs")
    sEPm = p.sem("sEPm")
    sEPr = p.sem("sEPr")
    sEPd = p.sem("sEPd")
    sST = p.sem("sST")

    ts, te = tb * TB, (tb + 1) * TB
    prev = A.prev_done

    for jc in range(NCH):
        sl = jc % 2
        xrs = A.xr[:, sl * TB:(sl + 1) * TB]
        sts = A.stage[:, sl * TB:(sl + 1) * TB]
        p.dma("sync", xrs, io["XRES"][jc, :, ts:te],
              after=[A.xr_free[sl], prev], inc=sXR)
        xr_done = sXR.mark()
        p.do("vector", add_src(jc, sts, xrs),
             after=list(src_after) + [xr_done, A.st_free[sl], prev], inc=sEPs)
        A.xr_free[sl] = sEPs.mark()
        p.do("vector", (lambda jc=jc, sts=sts: (lambda v: v.tensor_copy(
            A.xbf[:, jc * TB:(jc + 1) * TB], sts)))(),
            after=[sEPs.mark()], inc=sEPs)
        p.dma("sync", io["XRES_o"][jc, :, ts:te], sts,
              after=[sEPs.mark()], inc=sST)
        A.st_free[sl] = sST.mark()
    xbf_done = sEPs.mark()

    for jc in range(NCH):
        p.do("tensor", (lambda jc=jc: (lambda t: t.matmul(
            pe0[0:1, :], c.ones_col_bf[:, :],
            A.xbf[:, jc * TB:(jc + 1) * TB],
            start=(jc == 0), stop=(jc == NCH - 1))))(),
            after=[xbf_done, c.ready, prev], inc=(sEPm if jc == NCH - 1 else None))
    sq_mm = NONE_MARK
    for jc in range(NCH):
        slot = jc % 2
        p.do("scalar", (lambda jc=jc, slot=slot: (lambda s: s.activation(
            A.sq[:, slot * TB:(slot + 1) * TB],
            A.xbf[:, jc * TB:(jc + 1) * TB], AF.Square)))(),
            after=[xbf_done, sq_mm], inc=sEPr)
        sq_done = sEPr.mark()
        p.do("tensor", (lambda jc=jc, slot=slot: (lambda t: t.matmul(
            pe1[0:1, :], c.ones_col_bf[:, :],
            A.sq[:, slot * TB:(slot + 1) * TB],
            start=(jc == 0), stop=(jc == NCH - 1))))(),
            after=[sq_done, prev], inc=sEPm)
        if jc >= 1:
            sq_mm = (p.sems["sEPm"], p.sems["sEPm"].count - 1)
    mm_done = sEPm.mark()

    mean = A.rows[0:1, 0:TB]
    var = A.tmp[0:1, 0:TB]
    rr = A.rows[0:1, TB:2 * TB]
    p.do("vector", lambda v: v.tensor_scalar_mul(mean, pe0[0:1, :], 1.0 / DM),
         after=[mm_done], inc=sEPr)
    p.do("scalar", lambda s: s.activation(var, mean, AF.Square),
         after=[sEPr.mark()], inc=sEPr)
    p.do("vector", lambda v: v.scalar_tensor_tensor(
        var, pe1[0:1, :], 1.0 / DM, var, ALU.mult, ALU.subtract),
        after=[sEPr.mark()], inc=sEPr)
    p.do("scalar", lambda s: s.activation(var, var, AF.Ln,
                                          bias=c.eps_col[0:1, 0:1]),
         after=[sEPr.mark()], inc=sEPr)
    p.do("scalar", lambda s: s.activation(rr, var, AF.Exp, scale=-0.5),
         after=[sEPr.mark()], inc=sEPr)
    rows_done = sEPr.mark()

    p.do("tensor", lambda t: t.matmul(pe0[:, :], c.ones_row_f32[:, :], mean,
                                      start=True, stop=True),
         after=[rows_done], inc=sEPm)
    p.do("tensor", lambda t: t.matmul(pe1[:, :], c.ones_row_f32[:, :], rr,
                                      start=True, stop=True),
         after=[rows_done], inc=sEPm)
    bc_done = sEPm.mark()

    for jc in range(NCH):
        p.do("vector", (lambda jc=jc: (lambda v: v.tensor_sub(
            A.tmp[:, :], A.xbf[:, jc * TB:(jc + 1) * TB], pe0[:, :])))(),
            after=[bc_done], inc=sEPd)
        p.do("vector", (lambda jc=jc: (lambda v: v.tensor_mul(
            A.xbf[:, jc * TB:(jc + 1) * TB], A.tmp[:, :], pe1[:, :])))(),
            after=[sEPd.mark()], inc=sEPd)
        p.dma("sync", io["XLN_o"][jc, :, ts:te], A.xbf[:, jc * TB:(jc + 1) * TB],
              after=[sEPd.mark()], inc=sST)
    A.prev_done = sEPd.mark()
    return A.prev_done


def emit_final_tb(p, io, c, A, pe0, pe1, tb, add_src4, src_after=()):
    sXR = p.sem("sXR")
    sEPs = p.sem("sEPs")
    sEPd = p.sem("sEPd")
    sEPt = p.sem("sEPt")
    sST = p.sem("sST")

    ts = tb * TB
    prev = A.prev_done
    dma_done = NONE_MARK
    for t4 in range(4):
        for jc in range(NCH):
            sl = jc % 2
            xrs = A.xr[:, sl * TB:sl * TB + 128]
            sts = A.stage[:, sl * TB:sl * TB + 128]
            p.dma("sync", xrs,
                  io["XRES"][jc, :, ts + t4 * 128:ts + (t4 + 1) * 128],
                  after=[A.xr_free[sl], prev], inc=sXR)
            xr_done = sXR.mark()
            p.do("vector", add_src4(jc, t4, sts, xrs),
                 after=list(src_after) + [xr_done, A.st_free[sl], prev],
                 inc=sEPs)
            A.xr_free[sl] = sEPs.mark()
            st_done = sEPs.mark()
            bk = pe0 if jc % 2 == 0 else pe1
            p.do("tensor", (lambda bk=bk, sts=sts: (lambda t: t.transpose(
                bk[:, 0:128], sts, c.ident_f32[:, :])))(),
                after=[st_done, c.ready,
                       (p.sems["sEPd"], p.sems["sEPd"].count - 1)], inc=sEPt)
            A.st_free[sl] = sEPt.mark()
            p.do("vector", (lambda jc=jc, bk=bk: (lambda v: v.tensor_copy(
                A.outst[:, jc * 128:(jc + 1) * 128], bk[:, 0:128])))(),
                after=[sEPt.mark(), dma_done], inc=sEPd)
        p.dma("sync", io["out"][ts + t4 * 128: ts + (t4 + 1) * 128, :],
              A.outst[:, :], after=[sEPd.mark()], inc=sST)
        dma_done = sST.mark()
    A.prev_done = sEPd.mark()
    return A.prev_done



def emit_prologue(p, io, c):
    nc = p.nc
    sLD = p.sem("sLD")
    sPR = p.sem("sPR")
    sPT = p.sem("sPT")
    sPV = p.sem("sPV")
    sST = p.sem("sST")

    with (
        nc.sbuf_tensor(_nm("pr_x"), [128, 2 * DM], F32) as xt,
        nc.sbuf_tensor(_nm("pr_sq"), [128, DM], F32) as xsq,
        nc.sbuf_tensor(_nm("pr_ln"), [128, DM], BF16) as xln,
        nc.sbuf_tensor(_nm("pr_rows"), [128, 12], F32) as rows,
        nc.sbuf_tensor(_nm("pr_stb"), [128, 4 * DM], BF16) as stb,
        nc.sbuf_tensor(_nm("pr_stf"), [128, 2 * DM], F32) as stf,
        nc.psum_tensor(_nm(f"pr_pb"), [128, 1024], BF16) as pb,
        nc.psum_tensor(_nm(f"pr_pf0"), [128, 512], F32) as pf0,
        nc.psum_tensor(_nm(f"pr_pf1"), [128, 512], F32) as pf1,
    ):
        slot_free = [NONE_MARK, NONE_MARK]
        xln_free = NONE_MARK
        xsq_free = NONE_MARK
        pb_free = NONE_MARK
        pf_free = NONE_MARK
        stb_free = NONE_MARK
        stf_free = NONE_MARK
        row_free = [NONE_MARK, NONE_MARK]

        for tt in range(NT // 128):
            slot = tt % 2
            grp, sub = tt // 4, tt % 4
            xs = xt[:, slot * DM:(slot + 1) * DM]
            p.dma("sync", xs, io["L0_attn_x"][tt * 128:(tt + 1) * 128, :],
                  after=[slot_free[slot]], inc=sLD)
            ld = sLD.mark()

            rbase = (tt % 2) * 6
            s_ = rows[:, rbase + 0:rbase + 1]
            m_ = rows[:, rbase + 1:rbase + 2]
            s2 = rows[:, rbase + 2:rbase + 3]
            mq = rows[:, rbase + 3:rbase + 4]
            vr = rows[:, rbase + 4:rbase + 5]
            rc = rows[:, rbase + 5:rbase + 6]
            p.do("vector", (lambda xs=xs, s_=s_: (lambda v: v.reduce_sum(
                s_, xs, AXX)))(), after=[ld, row_free[tt % 2]], inc=sPR)
            p.do("vector", (lambda s_=s_, m_=m_: (lambda v: v.tensor_scalar_mul(
                m_, s_, 1.0 / DM)))(), after=[sPR.mark()], inc=sPR)
            p.do("scalar", (lambda xs=xs: (lambda s: s.activation(
                xsq[:, :], xs, AF.Square)))(), after=[ld, xsq_free], inc=sPR)
            p.do("vector", (lambda s2=s2: (lambda v: v.reduce_sum(
                s2, xsq[:, :], AXX)))(), after=[sPR.mark()], inc=sPR)
            xsq_free = sPR.mark()
            p.do("scalar", (lambda m_=m_, mq=mq: (lambda s: s.activation(
                mq, m_, AF.Square)))(), after=[sPR.mark()], inc=sPR)
            p.do("vector", (lambda s2=s2, mq=mq, vr=vr: (lambda v: v.scalar_tensor_tensor(
                vr, s2, 1.0 / DM, mq, ALU.mult, ALU.subtract)))(),
                after=[sPR.mark()], inc=sPR)
            p.do("scalar", (lambda vr=vr: (lambda s: s.activation(
                vr, vr, AF.Ln, bias=c.eps_col[:, 0:1])))(),
                after=[sPR.mark()], inc=sPR)
            p.do("scalar", (lambda vr=vr, rc=rc: (lambda s: s.activation(
                rc, vr, AF.Exp, scale=-0.5)))(),
                after=[sPR.mark()], inc=sPR)
            p.do("vector", (lambda xs=xs, m_=m_, rc=rc: (lambda v: v.tensor_scalar(
                xln[:, :], xs, m_, rc, ALU.subtract, ALU.mult)))(),
                after=[sPR.mark(), xln_free], inc=sPR)
            ln_done = sPR.mark()
            row_free[tt % 2] = ln_done

            for cc in range(NCH):
                p.do("tensor", (lambda cc=cc: (lambda t: t.transpose(
                    pb[:, cc * 128:(cc + 1) * 128],
                    xln[:, cc * 128:(cc + 1) * 128], c.ident_bf[:, :])))(),
                    after=[ln_done, pb_free, c.ready],
                    inc=(sPT if cc == NCH - 1 else None))
            xln_free = sPT.mark()
            p.do("vector", (lambda sub=sub: (lambda v: v.tensor_copy(
                stb[:, sub * DM:(sub + 1) * DM], pb[:, :])))(),
                after=[sPT.mark(), stb_free], inc=sPV)
            pb_free = sPV.mark()

            for cc in range(NCH):
                pf = pf0 if cc % 2 == 0 else pf1
                off = (cc // 2) * 128
                p.do("tensor", (lambda pf=pf, off=off, cc=cc, xs=xs: (lambda t: t.transpose(
                    pf[:, off:off + 128], xs[:, cc * 128:(cc + 1) * 128],
                    c.ident_f32[:, :])))(),
                    after=[ld, pf_free, c.ready],
                    inc=(sPT if cc >= NCH - 2 else None))
            slot_free[slot] = sPT.mark()
            p.do("vector", lambda v: v.tensor_copy(stf[:, 0:512], pf0[:, :]),
                 after=[sPT.mark(), stf_free], inc=sPV)
            p.do("vector", lambda v: v.tensor_copy(stf[:, 512:1024], pf1[:, :]),
                 after=[sPT.mark(), stf_free], inc=sPV)
            pf_free = sPV.mark()
            for cc in range(NCH):
                off = 512 * (cc % 2) + (cc // 2) * 128
                p.dma("sync", io["XRES_o"][cc, :, tt * 128:(tt + 1) * 128],
                      stf[:, off:off + 128], after=[sPV.mark()], inc=sST)
            stf_free = sST.mark()

            if sub == 3:
                for cc in range(NCH):
                    src = stb.rearrange("p (s c x) -> p s c x", s=4, c=NCH)[:, :, cc, :]
                    dst = io["XLN_o"][cc, :, grp * 512:(grp + 1) * 512]
                    dst = dst.rearrange("p (s x) -> p s x", s=4)
                    p.dma("sync", dst, src, after=[sPV.mark()], inc=sST)
                stb_free = sST.mark()
        p.fence()
        p.flush("prologue")



def emit_l0ffn(p, io, c, final=False):
    nc = p.nc
    sXL = p.sem("sXL")
    sWT = p.sem("sWT")
    sHP = p.sem("sHP")
    sHS = p.sem("sHS")
    sHM = p.sem("sHM")
    sYP = p.sem("sYP")
    sYE = p.sem("sYE")
    sGT = p.sem("sGT")
    sGE = p.sem("sGE")
    sGP = p.sem("sGP")
    sLD = p.sem("sLD")

    from contextlib import ExitStack
    with ExitStack() as ctx:
        xh = ctx.enter_context(nc.sbuf_tensor(_nm("f_xh"), [128, NCH * HALF], BF16))
        hb = ctx.enter_context(nc.sbuf_tensor(_nm("f_hb"), [128, NCH * HALF], BF16))
        wa = ctx.enter_context(nc.sbuf_tensor(_nm("f_wa"), [128, 16384], BF16))
        wb = ctx.enter_context(nc.sbuf_tensor(_nm("f_wb"), [128, 16384], BF16))
        yb = ctx.enter_context(nc.sbuf_tensor(_nm("f_yb"), [128, NCH * HALF], BF16))
        wg = ctx.enter_context(nc.sbuf_tensor(_nm("f_wg"), [128, NCH * NE], BF16))
        bgsb = ctx.enter_context(nc.sbuf_tensor(_nm("f_bg"), [NE, 1], F32))
        b1all = ctx.enter_context(nc.sbuf_tensor(_nm("f_b1"), [32, NE * 128], F32))
        b1T = ctx.enter_context(nc.sbuf_tensor(_nm("f_b1T"), [128, 32 * NE], F32))
        b2sb = ctx.enter_context(nc.sbuf_tensor(_nm("f_b2"), [NE, DM], BF16))
        glog = ctx.enter_context(nc.sbuf_tensor(_nm("f_glog"), [NE, TB], F32))
        gt = ctx.enter_context(nc.sbuf_tensor(_nm("f_gt"), [128, 2 * NE], F32))
        gms = ctx.enter_context(nc.sbuf_tensor(_nm("f_gms"), [128, NE], F32))
        grows = ctx.enter_context(nc.sbuf_tensor(_nm("f_grows"), [128, 8], F32))
        mt = ctx.enter_context(nc.sbuf_tensor(_nm("f_mt"), [128, 16 * NE], BF16))
        mT = ctx.enter_context(nc.sbuf_tensor(_nm("f_mT"), [NE, HALF], BF16))
        mbc = ctx.enter_context(nc.sbuf_tensor(_nm("f_mbc"), [128, HALF], BF16))
        ph0 = ctx.enter_context(nc.psum_tensor(_nm(f"f_ph0"), [128, 512], F32))
        ph1 = ctx.enter_context(nc.psum_tensor(_nm(f"f_ph1"), [128, 512], F32))
        py0 = ctx.enter_context(nc.psum_tensor(_nm(f"f_py0"), [128, 512], F32))
        py1 = ctx.enter_context(nc.psum_tensor(_nm(f"f_py1"), [128, 512], F32))
        pg = ctx.enter_context(nc.psum_tensor(_nm(f"f_pg"), [128, 512], F32))
        ptb = ctx.enter_context(nc.psum_tensor(_nm(f"f_ptb"), [128, 1024], BF16))
        pe0 = ctx.enter_context(nc.psum_tensor(_nm(f"f_pe0"), [128, 512], F32))
        pe1 = ctx.enter_context(nc.psum_tensor(_nm(f"f_pe1"), [128, 512], F32))
        A = alloc_epi(nc, ctx, final=final)

        p.dma("gpsimd", wg.rearrange("p (cc e) -> p cc e", cc=NCH),
              io["L0_ffn_Wg"].rearrange("(cc p) e -> p cc e", p=128),
              inc=sLD, allow_slow_non_contiguous=True)
        p.dma("gpsimd", bgsb[:, :],
              io["L0_ffn_bg"].rearrange("(e one) -> e one", one=1), inc=sLD)
        for e in range(NE):
            p.dma("sync", b1all[0:32, e * 128:(e + 1) * 128],
                  io["L0_ffn_b1"][e].rearrange("(cc pp) -> cc pp", pp=128),
                  inc=sLD)
        p.dma("gpsimd", b2sb[:, :], io["L0_ffn_b2"][:, :], inc=sLD)
        ld1 = sLD.mark()
        for e in range(NE):
            p.do("tensor", (lambda e=e: (lambda t: t.transpose(
                pg[:, e * 32:(e + 1) * 32], b1all[0:32, e * 128:(e + 1) * 128],
                c.ident_f32[0:32, 0:32])))(),
                after=[ld1, c.ready], inc=(sGT if e == NE - 1 else None))
        b1Tv = b1T.rearrange("p (hc e) -> p hc e", e=NE)
        for e in range(NE):
            p.do("vector", (lambda e=e: (lambda v: v.tensor_copy(
                b1Tv[:, :, e], pg[:, e * 32:(e + 1) * 32])))(),
                after=[sGT.mark()], inc=sGE)
        pg_free = sGE.mark()
        b1T_ready = sGE.mark()

        phases = [(e, q) for e in range(NE) for q in range(4)]

        def load_phase(kk, after):
            e, q = phases[kk]
            buf = wa if kk % 2 == 0 else wb
            w1src = io["L0_ffn_W1"][e].rearrange("(cc pp) h -> pp cc h", pp=128)
            w1src = w1src[:, :, q * 1024:(q + 1) * 1024]
            p.dma("gpsimd", buf[:, 0:8192].rearrange("p (cc h) -> p cc h", cc=NCH),
                  w1src, after=after, inc=sWT)
            w2src = io["L0_ffn_W2"][e][q * 1024:(q + 1) * 1024, :]
            w2src = w2src.rearrange("(hl pp) j -> pp hl j", pp=128)
            p.dma("gpsimd", buf[:, 8192:16384].rearrange("p (hl j) -> p hl j", hl=NCH),
                  w2src, after=after, inc=sWT)
            return sWT.count

        wt_ready = {}
        xh_free = NONE_MARK
        hb_free = NONE_MARK
        mT_free = NONE_MARK
        mbc_free = NONE_MARK
        yb_free = NONE_MARK
        ptb_free = NONE_MARK

        for half in range(2):
            for cc in range(NCH):
                p.dma("sync", xh[:, cc * HALF:(cc + 1) * HALF],
                      io["XLN"][cc, :, half * HALF:(half + 1) * HALF],
                      after=[xh_free], inc=sXL)
            xh_done = sXL.mark()

            if half == 0:
                wt_ready[0] = load_phase(0, [])
                wt_ready[1] = load_phase(1, [])

            glog_free = NONE_MARK
            for tq in range(4):
                for cc in range(NCH):
                    p.do("tensor", (lambda cc=cc, tq=tq: (lambda t: t.matmul(
                        pg[0:NE, :], wg[:, cc * NE:(cc + 1) * NE],
                        xh[:, cc * HALF + tq * TB: cc * HALF + (tq + 1) * TB],
                        start=(cc == 0), stop=(cc == NCH - 1))))(),
                        after=[xh_done, pg_free, ld1],
                        inc=(sGT if cc == NCH - 1 else None))
                p.do("vector", (lambda tq=tq: (lambda v: v.tensor_scalar_add(
                    glog[:, :], pg[0:NE, :], bgsb[:, 0:1])))(),
                    after=[sGT.mark(), glog_free], inc=sGE)
                pg_free = sGE.mark()
                for tl in range(4):
                    tc = tq * 4 + tl
                    p.do("tensor", (lambda tl=tl: (lambda t: t.transpose(
                        pg[:, 0:NE], glog[:, tl * 128:(tl + 1) * 128],
                        c.ident_f32[0:NE, 0:NE])))(),
                        after=[sGE.mark(), pg_free], inc=sGT)
                    p.do("vector", (lambda tc=tc: (lambda v: v.tensor_copy(
                        gt[:, (tc % 2) * NE:(tc % 2 + 1) * NE], pg[:, 0:NE])))(),
                        after=[sGT.mark()], inc=sGE)
                    pg_free = sGE.mark()
                    gtc = gt[:, (tc % 2) * NE:(tc % 2 + 1) * NE]
                    mx = grows[:, 0:1]; nmx = grows[:, 1:2]
                    ss = grows[:, 2:3]; rs = grows[:, 3:4]
                    p.do("vector", (lambda gtc=gtc, mx=mx: (lambda v: v.reduce_max(
                        mx, gtc, AXX)))(), after=[sGE.mark()], inc=sGE)
                    p.do("vector", (lambda mx=mx, nmx=nmx: (lambda v: v.tensor_scalar_mul(
                        nmx, mx, -1.0)))(), after=[sGE.mark()], inc=sGE)
                    p.do("scalar", (lambda gtc=gtc, nmx=nmx: (lambda s: s.activation(
                        gms[:, :], gtc, AF.Exp, bias=nmx)))(),
                        after=[sGE.mark()], inc=sGE)
                    p.do("vector", (lambda ss=ss: (lambda v: v.reduce_sum(
                        ss, gms[:, :], AXX)))(), after=[sGE.mark()], inc=sGE)
                    p.do("vector", (lambda ss=ss, rs=rs: (lambda v: v.reciprocal(
                        rs, ss)))(), after=[sGE.mark()], inc=sGE)
                    p.do("vector", (lambda gtc=gtc, mx=mx: (lambda v: v.tensor_single_scalar(
                        gms[:, :], gtc, mx, ALU.is_ge)))(),
                        after=[sGE.mark()], inc=sGE)
                    p.do("vector", (lambda tc=tc, rs=rs: (lambda v: v.tensor_scalar(
                        mt[:, tc * NE:(tc + 1) * NE], gms[:, :], rs, 0.5,
                        ALU.mult, ALU.mult)))(),
                        after=[sGE.mark()], inc=sGE)
                    p.do("tensor", (lambda tc=tc: (lambda t: t.transpose(
                        ptb[0:NE, (tc % 8) * 128:(tc % 8 + 1) * 128],
                        mt[:, tc * NE:(tc + 1) * NE], c.ident_bf[:, :])))(),
                        after=[sGE.mark(), ptb_free], inc=sGT)
                    if tc % 8 == 7:
                        p.do("vector", (lambda tc=tc: (lambda v: v.tensor_copy(
                            mT[0:NE, (tc - 7) * 128:(tc + 1) * 128], ptb[0:NE, :])))(),
                            after=[sGT.mark()], inc=sGE)
                        ptb_free = sGE.mark()
                glog_free = sGT.mark()
            mT_done = sGE.mark()

            yg = [A.prev_done] if half > 0 else []
            p.do("gpsimd", lambda g: g.memset(yb[:, :], 0.0),
                 after=yg + [yb_free], inc=sGP)
            y_clear = sGP.mark()

            for e in range(NE):
                for tq in range(4):
                    p.do("tensor", (lambda e=e, tq=tq: (lambda t: t.matmul(
                        pg[:, :], c.sel_bf[:, e * 128:(e + 1) * 128],
                        mT[0:NE, tq * TB:(tq + 1) * TB], start=True, stop=True)))(),
                        after=[mT_done, pg_free, mbc_free], inc=sGT)
                    p.do("vector", (lambda tq=tq: (lambda v: v.tensor_copy(
                        mbc[:, tq * TB:(tq + 1) * TB], pg[:, :])))(),
                        after=[sGT.mark()], inc=sGE)
                    pg_free = sGE.mark()
                mbc_done = sGE.mark()

                for q in range(4):
                    kk = e * 4 + q
                    buf = wa if kk % 2 == 0 else wb
                    wready = (sWT, wt_ready[kk])
                    hfree = {0: hb_free, 1: hb_free}

                    for hc in range(NCH):
                        for tq in range(4):
                            g = hc * 4 + tq
                            bank = [ph0, ph1][g % 2]
                            for dc in range(NCH):
                                p.do("tensor", (lambda dc=dc, hc=hc, tq=tq, bank=bank, buf=buf: (lambda t: t.matmul(
                                    bank[:, :],
                                    buf[:, dc * 1024 + hc * 128: dc * 1024 + (hc + 1) * 128],
                                    xh[:, dc * HALF + tq * TB: dc * HALF + (tq + 1) * TB],
                                    start=(dc == 0), stop=(dc == NCH - 1))))(),
                                    after=[wready, xh_done, hfree[g % 2]],
                                    inc=(sHP if dc == NCH - 1 else None))
                            mm_done = sHP.mark()
                            bias_ap = b1T[:, (q * 8 + hc) * NE + e:(q * 8 + hc) * NE + e + 1]
                            p.do("scalar", (lambda hc=hc, tq=tq, bank=bank, bias_ap=bias_ap: (lambda s: s.activation(
                                hb[:, hc * HALF + tq * TB: hc * HALF + (tq + 1) * TB],
                                bank[:, :], AF.Silu, bias=bias_ap)))(),
                                after=[mm_done, b1T_ready], inc=sHS)
                            hfree[g % 2] = sHS.mark()
                            p.do("vector", (lambda hc=hc, tq=tq: (lambda v: v.tensor_mul(
                                hb[:, hc * HALF + tq * TB: hc * HALF + (tq + 1) * TB],
                                hb[:, hc * HALF + tq * TB: hc * HALF + (tq + 1) * TB],
                                mbc[:, tq * TB:(tq + 1) * TB])))(),
                                after=[sHS.mark(), mbc_done], inc=sHM)
                    hm_all = sHM.mark()

                    ye_base = sYE.count
                    for oc in range(NCH):
                        for tq in range(4):
                            g = oc * 4 + tq
                            bank = [py0, py1][g % 2]
                            bank_guard = (sYE, ye_base + g - 1)
                            is_first_phase = (e == 0 and q == 0)
                            for hl in range(NCH):
                                p.do("tensor", (lambda hl=hl, oc=oc, tq=tq, bank=bank, buf=buf, ifp=is_first_phase: (lambda t: t.matmul(
                                    bank[:, :],
                                    buf[:, 8192 + hl * 1024 + oc * 128: 8192 + hl * 1024 + (oc + 1) * 128],
                                    hb[:, hl * HALF + tq * TB: hl * HALF + (tq + 1) * TB],
                                    start=(hl == 0),
                                    stop=(hl == NCH - 1 and not ifp))))(),
                                    after=[hm_all, wready, bank_guard],
                                    inc=(sYP if (hl == NCH - 1 and not is_first_phase) else None))
                            if is_first_phase:
                                p.do("tensor", (lambda oc=oc, tq=tq, bank=bank: (lambda t: t.matmul(
                                    bank[:, :], b2sb[:, oc * 128:(oc + 1) * 128],
                                    mT[0:NE, tq * TB:(tq + 1) * TB],
                                    start=False, stop=True)))(),
                                    after=[mT_done, ld1], inc=sYP)
                            p.do("vector", (lambda oc=oc, tq=tq, bank=bank: (lambda v: v.tensor_add(
                                yb[:, oc * HALF + tq * TB: oc * HALF + (tq + 1) * TB],
                                bank[:, :],
                                yb[:, oc * HALF + tq * TB: oc * HALF + (tq + 1) * TB])))(),
                                after=[sYP.mark(), y_clear], inc=sYE)
                    hb_free = sYP.mark()
                    if kk + 2 < 32:
                        wt_ready[kk + 2] = load_phase(kk + 2, [sYP.mark()])
                    elif half == 0:
                        wt_ready[kk + 2 - 32] = load_phase(kk + 2 - 32, [sYP.mark()])
            xh_free = sHP.mark()
            mT_free = sYP.mark()
            mbc_free = sHM.mark()

            y_done = sYE.mark()
            for tbl in range(4):
                tb = half * 4 + tbl
                if not final:
                    def add_src(jc, dst, xr, tbl=tbl):
                        def f(v):
                            return v.tensor_add(
                                dst,
                                yb[:, jc * HALF + tbl * TB: jc * HALF + (tbl + 1) * TB],
                                xr)
                        return f
                    emit_epilogue_tb(p, io, c, A, pe0, pe1, tb, add_src,
                                     src_after=[y_done])
                else:
                    def add_src4(jc, t4, dst, xr, tbl=tbl):
                        def f(v):
                            return v.tensor_add(
                                dst,
                                yb[:, jc * HALF + tbl * TB + t4 * 128:
                                   jc * HALF + tbl * TB + (t4 + 1) * 128],
                                xr)
                        return f
                    emit_final_tb(p, io, c, A, pe0, pe1, tb, add_src4,
                                  src_after=[y_done])
            yb_free = A.prev_done
        p.fence()
        p.flush("l0ffn")



def emit_wprep(p, io, c, jobs, name):
    nc = p.nc
    sLD = p.sem("sLD")
    sPT = p.sem("sPT")
    sPV = p.sem("sPV")
    sST = p.sem("sST")
    with (
        nc.sbuf_tensor(_nm("wp_in"), [128, 2 * DM], BF16) as win,
        nc.sbuf_tensor(_nm("wp_out"), [128, DM], BF16) as wout,
        nc.psum_tensor(_nm(f"wp_ptb"), [128, 1024], BF16) as ptb,
    ):
        slot_free = [NONE_MARK, NONE_MARK]
        ptb_free = NONE_MARK
        wout_free = NONE_MARK
        k = 0
        for (srcm, dstm) in jobs:
            for rcq in range(NCH):
                sl = k % 2
                p.dma("gpsimd", win[:, sl * DM:(sl + 1) * DM],
                      srcm[rcq * 128:(rcq + 1) * 128, :],
                      after=[slot_free[sl]], inc=sLD)
                ld = sLD.mark()
                for cc in range(NCH):
                    p.do("tensor", (lambda cc=cc, sl=sl: (lambda t: t.transpose(
                        ptb[:, cc * 128:(cc + 1) * 128],
                        win[:, sl * DM + cc * 128: sl * DM + (cc + 1) * 128],
                        c.ident_bf[:, :])))(),
                        after=[ld, ptb_free, c.ready],
                        inc=(sPT if cc == NCH - 1 else None))
                slot_free[sl] = sPT.mark()
                p.do("vector", lambda v: v.tensor_copy(wout[:, :], ptb[:, :]),
                     after=[sPT.mark(), wout_free], inc=sPV)
                ptb_free = sPV.mark()
                dst = dstm[:, :, rcq * 128:(rcq + 1) * 128]
                dst = dst.rearrange("cc p x -> p cc x")
                src = wout.rearrange("p (cc x) -> p cc x", cc=NCH)
                p.dma("sync", dst, src, after=[sPV.mark()], inc=sST)
                wout_free = sST.mark()
                k += 1
    p.fence()
    p.flush(name)


def emit_l1prep(p, io, c):
    jobs = [(io["L1_ffn_expert_w"][e], io["W1T_l1"][e]) for e in range(NE)]
    emit_wprep(p, io, c, jobs, "l1prep")


def emit_l0aprep(p, io, c):
    jobs = [(io["L0_attn_" + w], io["WT_l0a"][m])
            for m, w in enumerate(("Wq", "Wk", "Wv", "Wo"))]
    emit_wprep(p, io, c, jobs, "l0aprep")



SQI128 = 1.0 / math.sqrt(128.0)


def emit_l0attn(p, io, c, final=False):
    nc = p.nc
    sXL = p.sem("sXL")
    sWT = p.sem("sWT")
    sMM = p.sem("sMM")
    sEV = p.sem("sEV")
    sAC = p.sem("sAC")
    sOT = p.sem("sOT")
    sLD = p.sem("sLD")

    from contextlib import ExitStack
    with ExitStack() as ctx:
        xh = ctx.enter_context(nc.sbuf_tensor(_nm("a_xh"), [128, NCH * NT], BF16))
        kvs = ctx.enter_context(nc.sbuf_tensor(_nm("a_kvs"), [128, 16 * 128], BF16))
        bqkv = ctx.enter_context(nc.sbuf_tensor(_nm("a_bqkv"), [128, 4 * NCH], F32))
        pA = ctx.enter_context(nc.psum_tensor(_nm(f"a_pA"), [128, 512], F32))
        pB = ctx.enter_context(nc.psum_tensor(_nm(f"a_pB"), [128, 512], F32))
        pk0 = ctx.enter_context(nc.psum_tensor(_nm(f"a_pk0"), [128, 512], F32))
        pk1 = ctx.enter_context(nc.psum_tensor(_nm(f"a_pk1"), [128, 512], F32))
        ptb = ctx.enter_context(nc.psum_tensor(_nm(f"a_ptb"), [128, 1024], BF16))
        pe0 = ctx.enter_context(nc.psum_tensor(_nm(f"a_pe0"), [128, 512], F32))
        pe1 = ctx.enter_context(nc.psum_tensor(_nm(f"a_pe1"), [128, 512], F32))
        A = alloc_epi(nc, ctx, final=final)
        ctx12 = ctx.enter_context(ExitStack())
        wpA = ctx12.enter_context(nc.sbuf_tensor(_nm("a_wpA"), [128, 3072], BF16))
        wpB = ctx12.enter_context(nc.sbuf_tensor(_nm("a_wpB"), [128, 3072], BF16))
        kT2 = ctx12.enter_context(nc.sbuf_tensor(_nm("a_kT2"), [128, NT], BF16))
        vT2 = ctx12.enter_context(nc.sbuf_tensor(_nm("a_vT2"), [128, NT], BF16))
        qT2 = kT2
        ktm = ctx12.enter_context(nc.sbuf_tensor(_nm("a_ktm"), [128, 32 * 512], BF16))
        vtm = ctx12.enter_context(nc.sbuf_tensor(_nm("a_vtm"), [128, 32 * 128], BF16))
        kvt = ctx12.enter_context(nc.sbuf_tensor(_nm("a_kvt"), [128, 512], BF16))
        qpT = ctx12.enter_context(nc.sbuf_tensor(_nm("a_qpT"), [128, 4 * NT], BF16))
        csA = ctx12.enter_context(nc.sbuf_tensor(_nm("a_csA"), [128, 512], F32))
        csB = ctx12.enter_context(nc.sbuf_tensor(_nm("a_csB"), [128, 512], F32))
        otst = ctx12.enter_context(nc.sbuf_tensor(_nm("a_otst"), [128, 2 * 512], BF16))


        for cc in range(NCH):
            p.dma("sync", xh[:, cc * NT:(cc + 1) * NT], io["XLN"][cc, :, :],
                  inc=sXL)
        xh_done = sXL.mark()
        for m, w in enumerate(("bq", "bk", "bv", "bo")):
            for cc in range(NCH):
                p.dma("gpsimd", bqkv[:, m * NCH + cc:m * NCH + cc + 1],
                      io["L0_attn_" + w].rearrange("(cc pp) -> cc pp", pp=128)
                      [cc].rearrange("(pp one) -> pp one", one=1), inc=sLD)
        bias_done = sLD.mark()

        def load_pair(hp, buf, after, mats=(0, 1, 2)):
            for i, m in enumerate(mats):
                src = io["WT_l0a"][m][:, :, hp * 128:(hp + 1) * 128]
                src = src.rearrange("cc p x -> p cc x")
                dst = buf[:, i * 1024:(i + 1) * 1024]
                dst = dst.rearrange("p (cc x) -> p cc x", cc=NCH)
                p.dma("gpsimd", dst, src, after=after, inc=sWT)
            return sWT.count

        wt_ready = {}
        wt_ready[0] = load_pair(0, wpA, [])
        wt_ready[1] = load_pair(1, wpB, [])

        pair_mm_end = {}
        ev_free = {"pA": NONE_MARK, "pB": NONE_MARK, "pk0": NONE_MARK,
                   "pk1": NONE_MARK, "ptb": NONE_MARK}
        ktm_free = NONE_MARK
        vtm_free = NONE_MARK
        kT2_free = NONE_MARK
        vT2_free = NONE_MARK
        kvt_free = NONE_MARK

        for hp in range(NHEADS // 2):
            buf = wpA if hp % 2 == 0 else wpB
            wready = (sWT, wt_ready[hp])

            for mi, (dstb, bcol, dfree) in enumerate(
                    ((kT2, 1, kT2_free), (vT2, 2, vT2_free))):
                for tbq in range(NTB):
                    bank = [pA, pB][tbq % 2]
                    bkey = "pA" if tbq % 2 == 0 else "pB"
                    for cc in range(NCH):
                        p.do("tensor", (lambda cc=cc, tbq=tbq, bank=bank, buf=buf, mi=mi: (lambda t: t.matmul(
                            bank[:, :],
                            buf[:, (mi + 1) * 1024 + cc * 128:(mi + 1) * 1024 + (cc + 1) * 128],
                            xh[:, cc * NT + tbq * TB: cc * NT + (tbq + 1) * TB],
                            start=(cc == 0), stop=(cc == NCH - 1))))(),
                            after=[wready, xh_done, ev_free[bkey], dfree],
                            inc=(sMM if cc == NCH - 1 else None))
                        if cc == 0:
                            dfree = None
                    p.do("vector", (lambda tbq=tbq, bank=bank, dstb=dstb, bcol=bcol, hp=hp: (lambda v: v.tensor_scalar_add(
                        dstb[:, tbq * TB:(tbq + 1) * TB], bank[:, :],
                        bqkv[:, bcol * NCH + hp:bcol * NCH + hp + 1])))(),
                        after=[sMM.mark(), bias_done], inc=sEV)
                    ev_free[bkey] = sEV.mark()
            kv_proj_done = sEV.mark()

            for tt in range(32):
                for h in range(2):
                    g = tt * 2 + h
                    bank = [pk0, pk1][g % 2]
                    bkey = "pk0" if g % 2 == 0 else "pk1"
                    base = h * 64
                    p.do("tensor", (lambda tt=tt, h=h, bank=bank, base=base: (lambda t: t.matmul(
                        bank[0:128, 0:128],
                        kT2[base:base + 64, tt * 128:(tt + 1) * 128],
                        c.projpair[base:base + 64, h * 128:(h + 1) * 128],
                        start=True, stop=True)))(),
                        after=[kv_proj_done, ev_free[bkey], c.proj_ready,
                               ktm_free], inc=sMM)
                    mmm = sMM.mark()
                    p.do("scalar", (lambda bank=bank: (lambda s: s.activation(
                        csA[:, 0:128], bank[0:128, 0:128], AF.Sin,
                        scale=0.125)))(), after=[mmm, sEV.mark()], inc=sAC)
                    p.do("scalar", (lambda bank=bank: (lambda s: s.activation(
                        csB[:, 0:128], bank[0:128, 0:128], AF.Sin,
                        scale=0.0625)))(), after=[sAC.mark()], inc=sAC)
                    ev_free[bkey] = sAC.mark()
                    p.do("scalar", lambda s: s.activation(
                        csB[:, 0:128], csB[:, 0:128], AF.Square),
                        after=[sAC.mark()], inc=sAC)
                    trig = sAC.mark()
                    koff = tt * 512 + h * 256
                    p.do("vector", (lambda koff=koff: (lambda v: v.tensor_scalar(
                        ktm[:, koff:koff + 128], csB[:, 0:128],
                        -2.0 * SQI128, SQI128, ALU.mult, ALU.add)))(),
                        after=[trig], inc=sEV)
                    p.do("vector", (lambda koff=koff: (lambda v: v.tensor_scalar_mul(
                        ktm[:, koff + 128:koff + 256], csA[:, 0:128],
                        SQI128)))(), after=[trig], inc=sEV)
            ktrig_done = sEV.mark()
            kT2_free = sMM.mark()

            for ttg in range(4):
                for tl in range(8):
                    tt = ttg * 8 + tl
                    p.do("tensor", (lambda tt=tt, tl=tl: (lambda t: t.transpose(
                        ptb[:, tl * 128:(tl + 1) * 128],
                        vT2[:, tt * 128:(tt + 1) * 128], c.ident_bf[:, :])))(),
                        after=[kv_proj_done, ev_free["ptb"], c.ready, vtm_free],
                        inc=(sMM if tl == 7 else None))
                p.do("vector", (lambda ttg=ttg: (lambda v: v.tensor_copy(
                    vtm[:, ttg * 1024:(ttg + 1) * 1024], ptb[:, :])))(),
                    after=[sMM.mark()], inc=sEV)
                ev_free["ptb"] = sEV.mark()
            vtr_done = sEV.mark()
            vT2_free = sMM.mark()

            for h in range(2):
                bank = [pk0, pk1][h]
                for tt in range(32):
                    p.do("tensor", (lambda tt=tt, h=h, bank=bank: (lambda t: t.matmul(
                        bank[0:64, 0:256],
                        vtm[:, tt * 128 + h * 64: tt * 128 + (h + 1) * 64],
                        ktm[:, tt * 512 + h * 256: tt * 512 + (h + 1) * 256],
                        start=(tt == 0), stop=(tt == 31))))(),
                        after=[ktrig_done, vtr_done,
                               (ev_free["pk0"] if h == 0 else ev_free["pk1"])],
                        inc=(sMM if tt == 31 else None))
            kv_done = sMM.mark()
            ktm_free = kv_done
            vtm_free = kv_done
            for h in range(2):
                bank = [pk0, pk1][h]
                p.do("vector", (lambda h=h, bank=bank: (lambda v: v.tensor_copy(
                    kvt[0:64, h * 256:(h + 1) * 256], bank[0:64, 0:256])))(),
                    after=[kv_done, kvt_free], inc=sEV)
                ev_free["pk0" if h == 0 else "pk1"] = sEV.mark()
            kvt_done = sEV.mark()
            for h in range(2):
                for mc in range(2):
                    p.do("tensor", (lambda h=h, mc=mc: (lambda t: t.transpose(
                        ptb[:, 512 + (h * 2 + mc) * 64: 512 + (h * 2 + mc + 1) * 64],
                        kvt[0:64, h * 256 + mc * 128: h * 256 + (mc + 1) * 128],
                        c.ident_bf[0:64, 0:64])))(),
                        after=[kvt_done, ev_free["ptb"], c.ready], inc=sMM)
            p.do("vector", (lambda hp=hp: (lambda v: v.tensor_copy(
                kvs[:, hp * 256:(hp + 1) * 256], ptb[:, 512:512 + 256])))(),
                after=[sMM.mark()], inc=sEV)
            ev_free["ptb"] = sEV.mark()
            kvt_free = sEV.mark()
            pair_mm_end[hp] = sMM.mark()
            if hp + 2 < 8:
                wt_ready[hp + 2] = load_pair(
                    hp + 2, wpA if hp % 2 == 0 else wpB, [pair_mm_end[hp]])
        kvs_done = sEV.mark()

        wt_ready2 = {}
        wt_ready2[0] = load_pair(0, wpA, [pair_mm_end[6]], mats=(0,))
        wt_ready2[1] = load_pair(1, wpB, [pair_mm_end[7]], mats=(0,))
        qT2_free = pair_mm_end[7]
        qpT_free = NONE_MARK
        otst_free = [NONE_MARK, NONE_MARK]
        pass2_end = {}
        for hp in range(8):
            buf = wpA if hp % 2 == 0 else wpB
            wready = (sWT, wt_ready2[hp])
            qfree = qT2_free
            for tbq in range(NTB):
                bank = [pA, pB][tbq % 2]
                bkey = "pA" if tbq % 2 == 0 else "pB"
                for cc in range(NCH):
                    p.do("tensor", (lambda cc=cc, tbq=tbq, bank=bank, buf=buf: (lambda t: t.matmul(
                        bank[:, :],
                        buf[:, cc * 128:(cc + 1) * 128],
                        xh[:, cc * NT + tbq * TB: cc * NT + (tbq + 1) * TB],
                        start=(cc == 0), stop=(cc == NCH - 1))))(),
                        after=[wready, xh_done, ev_free[bkey], qfree],
                        inc=(sMM if cc == NCH - 1 else None))
                    if cc == 0:
                        qfree = None
                p.do("vector", (lambda tbq=tbq, bank=bank, hp=hp: (lambda v: v.tensor_scalar_add(
                    qT2[:, tbq * TB:(tbq + 1) * TB], bank[:, :],
                    bqkv[:, 0 * NCH + hp:0 * NCH + hp + 1])))(),
                    after=[sMM.mark(), bias_done], inc=sEV)
                ev_free[bkey] = sEV.mark()
            q_proj_done = sEV.mark()

            for h in range(2):
                for tbq in range(NTB):
                    g = h * NTB + tbq
                    bank = [pk0, pk1][g % 2]
                    bkey = "pk0" if g % 2 == 0 else "pk1"
                    p.do("tensor", (lambda h=h, tbq=tbq, bank=bank: (lambda t: t.matmul(
                        bank[:, :],
                        c.projpair[:, h * 128:(h + 1) * 128],
                        qT2[:, tbq * TB:(tbq + 1) * TB],
                        start=True, stop=True)))(),
                        after=[q_proj_done, ev_free[bkey], c.proj_ready,
                               qpT_free], inc=sMM)
                    mmm = sMM.mark()
                    p.do("scalar", (lambda bank=bank: (lambda s: s.activation(
                        csA[:, :], bank[:, :], AF.Sin, scale=0.125)))(),
                        after=[mmm, sEV.mark()], inc=sAC)
                    p.do("scalar", (lambda bank=bank: (lambda s: s.activation(
                        csB[:, :], bank[:, :], AF.Sin, scale=0.0625)))(),
                        after=[sAC.mark()], inc=sAC)
                    ev_free[bkey] = sAC.mark()
                    p.do("scalar", lambda s: s.activation(
                        csB[:, :], csB[:, :], AF.Square),
                        after=[sAC.mark()], inc=sAC)
                    trig = sAC.mark()
                    co = (h * 2 + 0) * NT + tbq * TB
                    so = (h * 2 + 1) * NT + tbq * TB
                    p.do("vector", (lambda co=co: (lambda v: v.tensor_scalar(
                        qpT[:, co:co + TB], csB[:, :],
                        -2.0 * SQI128, SQI128, ALU.mult, ALU.add)))(),
                        after=[trig], inc=sEV)
                    p.do("vector", (lambda so=so: (lambda v: v.tensor_scalar_mul(
                        qpT[:, so:so + TB], csA[:, :], SQI128)))(),
                        after=[trig], inc=sEV)
            qtrig_done = sEV.mark()
            qT2_free = sMM.mark()

            for h in range(2):
                for tbq in range(NTB):
                    g = h * NTB + tbq
                    bank = [pk0, pk1][g % 2]
                    bkey = "pk0" if g % 2 == 0 else "pk1"
                    for mc in range(2):
                        p.do("tensor", (lambda h=h, tbq=tbq, mc=mc, hp=hp, bank=bank: (lambda t: t.matmul(
                            bank[0:64, 0:512],
                            kvs[:, hp * 256 + (h * 2 + mc) * 64:
                                hp * 256 + (h * 2 + mc + 1) * 64],
                            qpT[:, (h * 2 + mc) * NT + tbq * TB:
                                (h * 2 + mc) * NT + (tbq + 1) * TB],
                            start=(mc == 0), stop=(mc == 1))))(),
                            after=[qtrig_done, kvs_done, ev_free[bkey]],
                            inc=(sMM if mc == 1 else None))
                    mmm = sMM.mark()
                    sl = g % 2
                    p.do("vector", (lambda bank=bank, sl=sl: (lambda v: v.tensor_copy(
                        otst[0:64, sl * 512:(sl + 1) * 512],
                        bank[0:64, 0:512])))(),
                        after=[mmm, otst_free[sl]], inc=sEV)
                    ev_free[bkey] = sEV.mark()
                    p.dma("sync", io["OT"][hp, h * 64:(h + 1) * 64,
                                           tbq * TB:(tbq + 1) * TB],
                          otst[0:64, sl * 512:(sl + 1) * 512],
                          after=[sEV.mark()], inc=sOT)
                    otst_free[sl] = sOT.mark()
            qpT_free = sMM.mark()
            pass2_end[hp] = sMM.mark()
            if hp + 2 < 8:
                wt_ready2[hp + 2] = load_pair(
                    hp + 2, wpA if hp % 2 == 0 else wpB, [pass2_end[hp]],
                    mats=(0,))
        ot_all = sOT.mark()

        p.flush("l0attn_p12")
        ctx12.close()
        woT = ctx.enter_context(nc.sbuf_tensor(_nm("a_woT"), [128, NCH * 1024], BF16))
        otb = ctx.enter_context(nc.sbuf_tensor(_nm("a_otb"), [128, 2 * NCH * 512], BF16))
        ybuf = ctx.enter_context(nc.sbuf_tensor(_nm("a_ybuf"), [128, NCH * 512], BF16))
        src = io["WT_l0a"][3].rearrange("cc p x -> p cc x")
        p.dma("gpsimd", woT.rearrange("p (cc x) -> p cc x", cc=NCH), src,
              after=[pass2_end[7]], inc=sWT)
        wo_ready = (sWT, sWT.count)
        otb_free = [NONE_MARK, NONE_MARK]
        ybuf_free = NONE_MARK
        for tb in range(NTB):
            sl = tb % 2
            for cc in range(NCH):
                p.dma("sync", otb[:, sl * 4096 + cc * TB: sl * 4096 + (cc + 1) * TB],
                      io["OT"][cc, :, tb * TB:(tb + 1) * TB],
                      after=[otb_free[sl], ot_all], inc=sXL)
            ot_tb = sXL.mark()
            for jc in range(NCH):
                bank = [pA, pB][jc % 2]
                bkey = "pA" if jc % 2 == 0 else "pB"
                for cc in range(NCH):
                    p.do("tensor", (lambda jc=jc, cc=cc, sl=sl, bank=bank: (lambda t: t.matmul(
                        bank[:, :],
                        woT[:, cc * 1024 + jc * 128: cc * 1024 + (jc + 1) * 128],
                        otb[:, sl * 4096 + cc * TB: sl * 4096 + (cc + 1) * TB],
                        start=(cc == 0), stop=(cc == NCH - 1))))(),
                        after=[wo_ready, ot_tb, ev_free[bkey], ybuf_free],
                        inc=(sMM if cc == NCH - 1 else None))
                p.do("vector", (lambda jc=jc, bank=bank: (lambda v: v.tensor_scalar_add(
                    ybuf[:, jc * TB:(jc + 1) * TB], bank[:, :],
                    bqkv[:, 3 * NCH + jc:3 * NCH + jc + 1])))(),
                    after=[sMM.mark(), bias_done], inc=sEV)
                ev_free[bkey] = sEV.mark()
            ybuf_free = None
            y_done = sEV.mark()
            otb_free[sl] = sMM.mark()

            if not final:
                def add_src(jc, dst, xr):
                    def f(v):
                        return v.tensor_add(dst, ybuf[:, jc * TB:(jc + 1) * TB], xr)
                    return f
                m = emit_epilogue_tb(p, io, c, A, pe0, pe1, tb, add_src,
                                     src_after=[y_done])
            else:
                def add_src4(jc, t4, dst, xr):
                    def f(v):
                        return v.tensor_add(
                            dst,
                            ybuf[:, jc * TB + t4 * 128: jc * TB + (t4 + 1) * 128],
                            xr)
                    return f
                m = emit_final_tb(p, io, c, A, pe0, pe1, tb, add_src4,
                                  src_after=[y_done])
            ybuf_free = m
    p.fence()
    p.flush("l0attn")



def emit_l1attn(p, io, c, final=False):
    nc = p.nc
    sXL = p.sem("sXL")
    sWT = p.sem("sWT")
    sMM = p.sem("sMM")
    sEV = p.sem("sEV")
    sAC = p.sem("sAC")
    sOT = p.sem("sOT")
    sLD = p.sem("sLD")
    sGP = p.sem("sGP")

    from contextlib import ExitStack
    with ExitStack() as ctx:
        epk = ctx.enter_context(nc.sbuf_tensor(_nm("t_epk"), [128, 32], F32))
        enk = ctx.enter_context(nc.sbuf_tensor(_nm("t_enk"), [128, 32], F32))
        Eneg = ctx.enter_context(nc.sbuf_tensor(_nm("t_Eneg"), [128, NT], BF16))
        Epos = ctx.enter_context(nc.sbuf_tensor(_nm("t_Epos"), [128, NT], BF16))
        Ff = ctx.enter_context(nc.sbuf_tensor(_nm("t_Ff"), [128, 4 * 512], F32))
        pSw = [ctx.enter_context(nc.psum_tensor(_nm(f"t_pSw{i}"), [128, 1024], F32))
               for i in range(2)]
        po = [ctx.enter_context(nc.psum_tensor(_nm(f"t_po{i}"), [128, 512], F32))
              for i in range(4)]
        pS0, pS1 = pSw[0][:, 0:512], pSw[0][:, 512:1024]
        A = alloc_epi(nc, ctx, final=final)

        with nc.sbuf_tensor(_nm("t_iot"), [128, NT], F32) as iot:
            p.do("gpsimd", lambda g: g.iota(
                epk[:, :], [[128, 32]], base=0, channel_multiplier=1,
                allow_small_or_imprecise_dtypes=True), inc=sGP)
            m1 = sGP.mark()
            p.do("scalar", lambda s: s.activation(enk[:, :], epk[:, :], AF.Exp,
                                                  scale=-CDEC),
                 after=[m1], inc=sAC)
            p.do("scalar", lambda s: s.activation(epk[:, :], epk[:, :], AF.Exp,
                                                  scale=CDEC),
                 after=[m1, sAC.mark()], inc=sAC)
            p.do("gpsimd", lambda g: g.iota(
                iot[:, :], [[1, NT]], base=0, channel_multiplier=0,
                allow_small_or_imprecise_dtypes=True), inc=sGP)
            m2 = sGP.mark()
            p.do("scalar", lambda s: s.activation(Eneg[:, :], iot[:, :], AF.Exp,
                                                  scale=-CDEC),
                 after=[m2, sAC.mark()], inc=sAC)
            p.do("vector", lambda v: v.tensor_scalar_mul(Eneg[:, :], Eneg[:, :],
                                                         float(L1SCALE)),
                 after=[sAC.mark()], inc=sEV)
            p.do("scalar", lambda s: s.activation(Epos[:, :], iot[:, :], AF.Exp,
                                                  scale=CDEC),
                 after=[m2, sAC.mark()], inc=sAC)
            p.do("vector", lambda v: v.tensor_scalar_mul(Epos[:, :], Epos[:, :],
                                                         float(L1SCALE)),
                 after=[sAC.mark()], inc=sEV)
            for o in range(4):
                p.do("gpsimd", (lambda o=o: (lambda g: g.iota(
                    Ff[:, o * 512:(o + 1) * 512], [[-1, 512]], base=o * 128,
                    channel_multiplier=1,
                    allow_small_or_imprecise_dtypes=True)))(), inc=sGP)
            m3 = sGP.mark()
            p.do("scalar", lambda s: s.activation(Ff[:, :], Ff[:, :], AF.Relu),
                 after=[m3, sAC.mark()], inc=sAC)
            p.do("scalar", lambda s: s.activation(Ff[:, :], Ff[:, :], AF.Exp,
                                                  scale=-2.0 * CDEC),
                 after=[sAC.mark()], inc=sAC)
            tables_done_ac = sAC.mark()
            tables_done_ev = sEV.mark()

            with ExitStack() as ctxA:
                xh = ctxA.enter_context(nc.sbuf_tensor(_nm("t_xh"), [128, NCH * NT], BF16))
                stg = ctxA.enter_context(nc.sbuf_tensor(_nm("t_stg"), [128, NT], BF16))
                for cc in range(NCH):
                    p.dma("sync", xh[:, cc * NT:(cc + 1) * NT], io["XLN"][cc, :, :],
                          inc=sXL)
                xh_done = sXL.mark()

                with nc.sbuf_tensor(_nm("t_wq"), [128, 8192], BF16) as wq:
                    p.dma("gpsimd", wq.rearrange("p (cc j) -> p cc j", cc=NCH),
                          io["L1_attn_Wq"].rearrange("(cc pp) j -> pp cc j", pp=128),
                          inc=sWT)
                    wq_rdy = (sWT, sWT.count)
                    stg_free = NONE_MARK
                    for jc in range(NCH):
                        for tbq in range(NTB):
                            bank = [pS0, pS1][tbq % 2]
                            for cc in range(NCH):
                                p.do("tensor", (lambda jc=jc, tbq=tbq, bank=bank, wq=wq, cc=cc: (lambda t: t.matmul(
                                    bank[:, :],
                                    wq[:, cc * 1024 + jc * 128: cc * 1024 + (jc + 1) * 128],
                                    xh[:, cc * NT + tbq * TB: cc * NT + (tbq + 1) * TB],
                                    start=(cc == 0), stop=(cc == NCH - 1))))(),
                                    after=[wq_rdy, xh_done, (sEV, sEV.count - 1)],
                                    inc=(sMM if cc == NCH - 1 else None))
                            p.do("vector", (lambda tbq=tbq, bank=bank: (lambda v: v.tensor_copy(
                                stg[:, tbq * TB:(tbq + 1) * TB], bank[:, :])))(),
                                after=[sMM.mark(), stg_free], inc=sEV)
                        p.dma("sync", io["QTd"][jc, :, :], stg[:, :],
                              after=[sEV.mark()], inc=sOT)
                        stg_free = sOT.mark()
                    qt_mm_end = sMM.mark()

                with (
                    nc.sbuf_tensor(_nm("t_wk"), [128, 8192], BF16) as wk,
                    nc.sbuf_tensor(_nm("t_wv"), [128, 8192], BF16) as wv,
                ):
                    p.dma("gpsimd", wk.rearrange("p (cc j) -> p cc j", cc=NCH),
                          io["L1_attn_Wk"].rearrange("(cc pp) j -> pp cc j", pp=128),
                          after=[qt_mm_end], inc=sWT)
                    p.dma("gpsimd", wv.rearrange("p (cc j) -> p cc j", cc=NCH),
                          io["L1_attn_Wv"].rearrange("(cc pp) j -> pp cc j", pp=128),
                          inc=sWT)
                    wkv_rdy = (sWT, sWT.count)
                    for jc in range(NCH):
                        for tbq in range(NTB):
                            bank = [pS0, pS1][tbq % 2]
                            for i in range(16):
                                wbuf = wk if i < 8 else wv
                                cc = i % 8
                                p.do("tensor", (lambda jc=jc, tbq=tbq, bank=bank, wbuf=wbuf, cc=cc, i=i: (lambda t: t.matmul(
                                    bank[:, :],
                                    wbuf[:, cc * 1024 + jc * 128: cc * 1024 + (jc + 1) * 128],
                                    xh[:, cc * NT + tbq * TB: cc * NT + (tbq + 1) * TB],
                                    start=(i == 0), stop=(i == 15))))(),
                                    after=[wkv_rdy, xh_done, (sEV, sEV.count - 1)],
                                    inc=(sMM if i == 15 else None))
                            p.do("vector", (lambda tbq=tbq, bank=bank: (lambda v: v.tensor_copy(
                                stg[:, tbq * TB:(tbq + 1) * TB], bank[:, :])))(),
                                after=[sMM.mark(), stg_free], inc=sEV)
                        p.dma("sync", io["KVSd"][jc, :, :], stg[:, :],
                              after=[sEV.mark()], inc=sOT)
                        stg_free = sOT.mark()
                    kvs_mm_end = sMM.mark()
                    kvs_st = sOT.mark()

                latT = ctxA.enter_context(nc.sbuf_tensor(_nm("t_latT"), [128, 2 * NT], BF16))
                kvstb = ctxA.enter_context(nc.sbuf_tensor(_nm("t_kvstb"), [128, 2 * 4096], BF16))
                kvstb_free = [NONE_MARK, NONE_MARK]
                with nc.sbuf_tensor(_nm("t_wd"), [128, 2048], BF16) as wd:
                    p.dma("gpsimd", wd.rearrange("p (cc j) -> p cc j", cc=NCH),
                          io["L1_attn_W_down_kv"].rearrange("(cc pp) j -> pp cc j", pp=128),
                          after=[kvs_mm_end], inc=sWT)
                    wd_rdy = (sWT, sWT.count)
                    for tbq in range(NTB):
                        sl = tbq % 2
                        for cc in range(NCH):
                            p.dma("sync", kvstb[:, sl * 4096 + cc * TB: sl * 4096 + (cc + 1) * TB],
                                  io["KVSd"][cc, :, tbq * TB:(tbq + 1) * TB],
                                  after=[kvstb_free[sl], kvs_st], inc=sXL)
                        kv_tb = sXL.mark()
                        for jc in range(2):
                            bank = [pS0, pS1][jc]
                            for cc in range(NCH):
                                p.do("tensor", (lambda jc=jc, tbq=tbq, sl=sl, bank=bank, wd=wd, cc=cc: (lambda t: t.matmul(
                                    bank[:, :],
                                    wd[:, cc * 256 + jc * 128: cc * 256 + (jc + 1) * 128],
                                    kvstb[:, sl * 4096 + cc * TB: sl * 4096 + (cc + 1) * TB],
                                    start=(cc == 0), stop=(cc == NCH - 1))))(),
                                    after=[wd_rdy, kv_tb, (sEV, sEV.count - 2)],
                                    inc=(sMM if cc == NCH - 1 else None))
                            p.do("vector", (lambda jc=jc, tbq=tbq, bank=bank: (lambda v: v.tensor_copy(
                                latT[:, jc * NT + tbq * TB: jc * NT + (tbq + 1) * TB],
                                bank[:, :])))(),
                                after=[sMM.mark()], inc=sEV)
                        kvstb_free[sl] = sMM.mark()
                    lat_done = sEV.mark()
                    lat_mm_end = sMM.mark()

                with (
                    nc.sbuf_tensor(_nm("t_wuk"), [128, 2 * 1024], BF16) as wuk,
                    nc.sbuf_tensor(_nm("t_wuv"), [128, 2 * 1024], BF16) as wuv,
                ):
                    p.dma("gpsimd", wuk.rearrange("p (cc j) -> p cc j", cc=2),
                          io["L1_attn_W_up_k"].rearrange("(cc pp) j -> pp cc j", pp=128),
                          after=[lat_mm_end], inc=sWT)
                    p.dma("gpsimd", wuv.rearrange("p (cc j) -> p cc j", cc=2),
                          io["L1_attn_W_up_v"].rearrange("(cc pp) j -> pp cc j", pp=128),
                          inc=sWT)
                    wu_rdy = (sWT, sWT.count)
                    for wub, dstd in ((wuk, io["KRd"]), (wuv, io["VRd"])):
                        for jc in range(NCH):
                            for tbq in range(NTB):
                                bank = [pS0, pS1][tbq % 2]
                                for cc in range(2):
                                    p.do("tensor", (lambda jc=jc, tbq=tbq, bank=bank, wub=wub, cc=cc: (lambda t: t.matmul(
                                        bank[:, :],
                                        wub[:, cc * 1024 + jc * 128: cc * 1024 + (jc + 1) * 128],
                                        latT[:, cc * NT + tbq * TB: cc * NT + (tbq + 1) * TB],
                                        start=(cc == 0), stop=(cc == 1))))(),
                                        after=[wu_rdy, lat_done, (sEV, sEV.count - 1)],
                                        inc=(sMM if cc == 1 else None))
                                p.do("vector", (lambda tbq=tbq, bank=bank: (lambda v: v.tensor_copy(
                                    stg[:, tbq * TB:(tbq + 1) * TB], bank[:, :])))(),
                                    after=[sMM.mark(), stg_free], inc=sEV)
                            p.dma("sync", dstd[jc, :, :], stg[:, :],
                                  after=[sEV.mark()], inc=sOT)
                            stg_free = sOT.mark()
                    proj_all_mm = sMM.mark()
                    proj_all_st = sOT.mark()
                p.flush("l1attn_proj")

            with ExitStack() as ctxB:
                vtmall = ctxB.enter_context(nc.sbuf_tensor(_nm("t_vtmall"), [128, NHEADS * 32 * 65], BF16))

                ctxV = ctxB.enter_context(ExitStack())
                vrh2 = [ctxV.enter_context(nc.sbuf_tensor(_nm("t_vrh"), [64, NT], F32)) for _ in range(2)]
                vt_free = [NONE_MARK, NONE_MARK]
                pS_free2 = [NONE_MARK, NONE_MARK]
                for h in range(NHEADS):
                    par = h % 2
                    hc, hr = h // 2, (h % 2) * 64
                    p.dma("gpsimd", vrh2[par][0:64, :], io["VRd"][hc, hr:hr + 64, :],
                          after=[vt_free[par], proj_all_st], inc=sXL)
                    vld = sXL.mark()
                    for g2 in range(2):
                        for tl in range(16):
                            kc = g2 * 16 + tl
                            p.do("tensor", (lambda kc=kc, tl=tl, par=par, g2=g2: (lambda t: t.transpose(
                                pSw[g2][:, tl * 64:(tl + 1) * 64],
                                vrh2[par][0:64, kc * 128:(kc + 1) * 128],
                                c.ident_f32[0:64, 0:64])))(),
                                after=[vld, c.ready, pS_free2[g2 % 2]],
                                inc=(sMM if tl == 15 else None))
                        vv = vtmall.rearrange("p (hh kc x) -> p hh kc x", hh=NHEADS, x=65)
                        p.do("vector", (lambda h=h, g2=g2, vv=vv: (lambda v: v.tensor_copy(
                            vv[:, h, g2 * 16:(g2 + 1) * 16, 0:64],
                            pSw[g2 % 2].rearrange("p (tl x) -> p tl x", x=64)[:, :, :])))(),
                            after=[sMM.mark()], inc=sEV)
                        pS_free2[g2 % 2] = sEV.mark()
                    vt_free[par] = sMM.mark()
                vvall = vtmall.rearrange("p (hh kc x) -> p hh kc x", hh=NHEADS, x=65)
                p.do("gpsimd", lambda g: g.memset(vvall[:, :, :, 64:65], 1.0),
                     after=[sEV.mark()], inc=sGP)
                vph_done = sGP.mark()
                if "DBGV" in io:
                    p.dma("sync", io["DBGV"][:, :], vtmall[:, :],
                          after=[vph_done], inc=sOT)
                p.flush("l1attn_vpre")
                ctxV.close()
                krh2 = [ctxB.enter_context(nc.sbuf_tensor(_nm("t_krh"), [64, NT], BF16)) for _ in range(2)]
                qh1 = ctxB.enter_context(nc.sbuf_tensor(_nm("t_qh"), [64, NT], BF16))
                qh2 = [qh1, qh1]
                qmin2 = [ctxB.enter_context(nc.sbuf_tensor(_nm("t_qmin"), [64, NT], BF16)) for _ in range(2)]
                qpl2 = [ctxB.enter_context(nc.sbuf_tensor(_nm("t_qpl"), [64, NT], BF16)) for _ in range(2)]
                ptile = ctxB.enter_context(nc.sbuf_tensor(_nm("t_ptile"), [128, 4 * 512], BF16))
                oth = ctxB.enter_context(nc.sbuf_tensor(_nm("t_oth"), [64, NT], BF16))
                rrow = ctxB.enter_context(nc.sbuf_tensor(_nm("t_rrow"), [128, 512], F32))
                otmp = ctxB.enter_context(nc.sbuf_tensor(_nm("t_otmp"), [64, 512], F32))

                h_in_free = [NONE_MARK, NONE_MARK]
                q_free = [NONE_MARK, NONE_MARK]
                oth_free = NONE_MARK
                pt_free = [NONE_MARK] * 4
                pSw_free = [pS_free2[0], pS_free2[1]]
                po_free = [NONE_MARK] * 4
                h_dma = {}
                hq_done = {}
                qh_free = [NONE_MARK]
                otmp_free = NONE_MARK

                def head_loads(h):
                    par = h % 2
                    hc, hr = h // 2, (h % 2) * 64
                    p.dma("sync", krh2[par][0:64, :], io["KRd"][hc, hr:hr + 64, :],
                          after=[h_in_free[par], proj_all_st], inc=sXL)
                    p.dma("sync", qh2[par][0:64, :], io["QTd"][hc, hr:hr + 64, :],
                          after=[h_in_free[par], qh_free[0]], inc=sXL)
                    h_dma[h] = sXL.mark()
                    p.do("vector", (lambda par=par: (lambda v: v.tensor_mul(
                        qmin2[par][0:64, :], qh2[par][0:64, :], Eneg[0:64, :])))(),
                        after=[h_dma[h], q_free[par], tables_done_ev], inc=sEV)
                    p.do("vector", (lambda par=par: (lambda v: v.tensor_mul(
                        qpl2[par][0:64, :], qh2[par][0:64, :], Epos[0:64, :])))(),
                        after=[h_dma[h], q_free[par]], inc=sEV)
                    hq_done[h] = sEV.mark()
                    qh_free[0] = sEV.mark()

                head_loads(0)
                head_loads(1)

                def tile_kind(kc, qb):
                    qlo, qhi = qb * 512, qb * 512 + 511
                    klo, khi = kc * 128, kc * 128 + 127
                    if qlo > khi:
                        return "up"
                    if qhi < klo:
                        return "lo"
                    return "st"

                for h in range(NHEADS):
                    par = h % 2
                    hc, hr = h // 2, (h % 2) * 64
                    krh = krh2[par]
                    qmin, qpl = qmin2[par], qpl2[par]
                    h_in = h_dma[h]
                    q_done = hq_done[h]
                    vb = h * 32 * 65

                    exp_mark = {}
                    for qbg in range(4):
                        qbs = (qbg * 2, qbg * 2 + 1)
                        pop = [po[(qbg % 2) * 2], po[(qbg % 2) * 2 + 1]]

                        def emit_S(kc, qbs=qbs, krh=krh, qmin=qmin, qpl=qpl,
                                   h_in=h_in, q_done=q_done):
                            w = pSw[kc % 2]
                            marks = []
                            for i, qb in enumerate(qbs):
                                kind = tile_kind(kc, qb)
                                rhs = qmin if kind in ("up", "st") else qpl
                                p.do("tensor", (lambda kc=kc, qb=qb, i=i, rhs=rhs, krh=krh, w=w: (lambda t: t.matmul(
                                    w[:, i * 512:(i + 1) * 512],
                                    krh[0:64, kc * 128:(kc + 1) * 128],
                                    rhs[0:64, qb * TB:(qb + 1) * TB],
                                    start=True, stop=True)))(),
                                    after=[h_in, q_done, pSw_free[kc % 2]],
                                    inc=sMM)
                                marks.append(sMM.mark())
                            return marks

                        def emit_exp(kc, marks, qbs=qbs):
                            w = pSw[kc % 2]
                            kinds = [tile_kind(kc, qb) for qb in qbs]
                            for i in range(2):
                                if kinds[i] == "st":
                                    p.do("vector", (lambda i=i, kc=kc, w=w: (lambda v: v.tensor_mul(
                                        w[:, i * 512:(i + 1) * 512],
                                        w[:, i * 512:(i + 1) * 512],
                                        Ff[:, (kc % 4) * 512:(kc % 4 + 1) * 512])))(),
                                        after=[marks[i], tables_done_ac], inc=sEV)
                                    marks[i] = sEV.mark()
                            slot = (kc % 2) * 2
                            same = (kinds[0] == "lo") == (kinds[1] == "lo")
                            if same:
                                scol = epk if kinds[0] in ("up", "st") else enk
                                p.do("scalar", (lambda kc=kc, slot=slot, scol=scol, w=w: (lambda s: s.activation(
                                    ptile[:, slot * 512:(slot + 2) * 512],
                                    w[:, :], AF.Exp, scale=scol[:, kc:kc + 1])))(),
                                    after=marks + [pt_free[slot], pt_free[slot + 1],
                                                   tables_done_ac], inc=sAC)
                            else:
                                for i in range(2):
                                    scol = epk if kinds[i] in ("up", "st") else enk
                                    p.do("scalar", (lambda kc=kc, i=i, slot=slot, scol=scol, w=w: (lambda s: s.activation(
                                        ptile[:, (slot + i) * 512:(slot + i + 1) * 512],
                                        w[:, i * 512:(i + 1) * 512], AF.Exp,
                                        scale=scol[:, kc:kc + 1])))(),
                                        after=[marks[i], pt_free[slot + i],
                                               tables_done_ac], inc=sAC)
                            pSw_free[kc % 2] = sAC.mark()
                            exp_mark[kc] = sAC.mark()

                        def emit_PV(kc, pop=pop, qbg=qbg, vb=vb):
                            slot = (kc % 2) * 2
                            for i in range(2):
                                p.do("tensor", (lambda kc=kc, i=i, slot=slot, pop=pop, vb=vb: (lambda t: t.matmul(
                                    pop[i][0:65, :],
                                    vtmall[:, vb + kc * 65: vb + (kc + 1) * 65],
                                    ptile[:, (slot + i) * 512:(slot + i + 1) * 512],
                                    start=(kc == 0), stop=(kc == 31))))(),
                                    after=[exp_mark[kc], vph_done,
                                           (po_free[(qbg % 2) * 2 + i] if kc == 0 else None)],
                                    inc=sMM)
                                pt_free[slot + i] = sMM.mark()

                        mk = emit_S(0)
                        emit_exp(0, mk)
                        for kc in range(32):
                            if kc + 1 < 32:
                                mk = emit_S(kc + 1)
                                emit_exp(kc + 1, mk)
                            emit_PV(kc)
                        pv_done = sMM.mark()
                        for i, qb in enumerate(qbs):
                            bi = (qbg % 2) * 2 + i
                            p.do("scalar", (lambda bi=bi: (lambda s: s.activation(
                                rrow[64:65, 0:512], po[bi][64:65, :], AF.Ln)))(),
                                after=[pv_done, (p.sems["sEV"], p.sems["sEV"].count)],
                                inc=sAC)
                            p.do("scalar", (lambda: (lambda s: s.activation(
                                rrow[64:65, 0:512], rrow[64:65, 0:512], AF.Exp,
                                scale=-1.0)))(),
                                after=[sAC.mark()], inc=sAC)
                            rr_done = sAC.mark()
                            p.do("vector", (lambda bi=bi: (lambda v: v.tensor_copy(
                                otmp[0:64, :], po[bi][0:64, :])))(),
                                after=[pv_done, otmp_free], inc=sEV)
                            ev_cp = sEV.mark()
                            p.do("tensor", (lambda bi=bi: (lambda t: t.matmul(
                                po[bi][0:64, :], c.ones_blk[64:65, 0:64],
                                rrow[64:65, 0:512], start=True, stop=True)))(),
                                after=[rr_done, ev_cp], inc=sMM)
                            p.do("vector", (lambda qb=qb, bi=bi: (lambda v: v.tensor_mul(
                                oth[0:64, qb * TB:(qb + 1) * TB],
                                otmp[0:64, :],
                                po[bi][0:64, :])))(),
                                after=[sMM.mark(), oth_free], inc=sEV)
                            po_free[bi] = sEV.mark()
                            otmp_free = sEV.mark()
                    oth_done = sEV.mark()
                    p.dma("sync", io["OT"][hc, hr:hr + 64, :], oth[0:64, :],
                          after=[oth_done], inc=sOT)
                    oth_free = sOT.mark()
                    h_in_free[par] = sMM.mark()
                    q_free[par] = sMM.mark()
                    if h + 2 < NHEADS:
                        head_loads(h + 2)
                ot_all = sOT.mark()
                p.flush("l1attn_heads")

            with ExitStack() as ctxC:
                woB = ctxC.enter_context(nc.sbuf_tensor(_nm("t_woB"), [128, 8192], BF16))
                otb = ctxC.enter_context(nc.sbuf_tensor(_nm("t_otb"), [128, 2 * 4096], BF16))
                ybuf = ctxC.enter_context(nc.sbuf_tensor(_nm("t_ybuf"), [128, 4096], BF16))
                p.dma("gpsimd", woB.rearrange("p (cc j) -> p cc j", cc=NCH),
                      io["L1_attn_Wo"].rearrange("(cc pp) j -> pp cc j", pp=128),
                      inc=sWT)
                wo_rdy = (sWT, sWT.count)
                otb_free = [NONE_MARK, NONE_MARK]
                ybuf_free = NONE_MARK
                ev_free = {"pS0": NONE_MARK, "pS1": NONE_MARK}
                for tb in range(NTB):
                    sl = tb % 2
                    for cc in range(NCH):
                        p.dma("sync", otb[:, sl * 4096 + cc * TB: sl * 4096 + (cc + 1) * TB],
                              io["OT"][cc, :, tb * TB:(tb + 1) * TB],
                              after=[otb_free[sl], ot_all], inc=sXL)
                    ot_tb = sXL.mark()
                    for jc in range(NCH):
                        bank = [pS0, pS1][jc % 2]
                        bkey = "pS0" if jc % 2 == 0 else "pS1"
                        for cc in range(NCH):
                            p.do("tensor", (lambda jc=jc, cc=cc, sl=sl, bank=bank: (lambda t: t.matmul(
                                bank[:, :],
                                woB[:, cc * 1024 + jc * 128: cc * 1024 + (jc + 1) * 128],
                                otb[:, sl * 4096 + cc * TB: sl * 4096 + (cc + 1) * TB],
                                start=(cc == 0), stop=(cc == NCH - 1))))(),
                                after=[wo_rdy, ot_tb, ev_free[bkey], ybuf_free],
                                inc=(sMM if cc == NCH - 1 else None))
                        p.do("vector", (lambda jc=jc, bank=bank: (lambda v: v.tensor_copy(
                            ybuf[:, jc * TB:(jc + 1) * TB], bank[:, :])))(),
                            after=[sMM.mark()], inc=sEV)
                        ev_free[bkey] = sEV.mark()
                    y_done = sEV.mark()
                    otb_free[sl] = sMM.mark()

                    if not final:
                        def add_src(jc, dst, xr):
                            def f(v):
                                return v.tensor_add(
                                    dst, ybuf[:, jc * TB:(jc + 1) * TB], xr)
                            return f
                        m = emit_epilogue_tb(p, io, c, A, po[0], po[1], tb,
                                             add_src, src_after=[y_done])
                    else:
                        def add_src4(jc, t4, dst, xr):
                            def f(v):
                                return v.tensor_add(
                                    dst,
                                    ybuf[:, jc * TB + t4 * 128: jc * TB + (t4 + 1) * 128],
                                    xr)
                            return f
                        m = emit_final_tb(p, io, c, A, po[0], po[1], tb,
                                          add_src4, src_after=[y_done])
                    ybuf_free = m
                p.fence()
                p.flush("l1attn")



def emit_l1ffn(p, io, c, final=False):
    nc = p.nc
    sXL = p.sem("sXL")
    sWT = p.sem("sWT")
    sYP = p.sem("sYP")
    sGT = p.sem("sGT")
    sGE = p.sem("sGE")
    sHM = p.sem("sHM")
    sLD = p.sem("sLD")

    from contextlib import ExitStack
    with ExitStack() as ctx:
        xln = ctx.enter_context(nc.sbuf_tensor(_nm("n_xln"), [128, NCH * TB], BF16))
        wa = ctx.enter_context(nc.sbuf_tensor(_nm("n_wa"), [128, 8192], BF16))
        wb = ctx.enter_context(nc.sbuf_tensor(_nm("n_wb"), [128, 8192], BF16))
        xta = ctx.enter_context(nc.sbuf_tensor(_nm("n_xta"), [128, NCH * TB], BF16))
        xtb = ctx.enter_context(nc.sbuf_tensor(_nm("n_xtb"), [128, NCH * TB], BF16))
        mbc8 = ctx.enter_context(nc.sbuf_tensor(_nm("n_mbc8"), [128, NE * TB], BF16))
        rwt = ctx.enter_context(nc.sbuf_tensor(_nm("n_rwt"), [128, NCH * NE], BF16))
        rwin = ctx.enter_context(nc.sbuf_tensor(_nm("n_rwin"), [NE, DM], F32))
        rbsb = ctx.enter_context(nc.sbuf_tensor(_nm("n_rb"), [NE, 1], F32))
        ebsb = ctx.enter_context(nc.sbuf_tensor(_nm("n_eb"), [NE, DM], BF16))
        glog = ctx.enter_context(nc.sbuf_tensor(_nm("n_glog"), [NE, TB], F32))
        gt = ctx.enter_context(nc.sbuf_tensor(_nm("n_gt"), [128, 4 * NE], F32))
        gms = ctx.enter_context(nc.sbuf_tensor(_nm("n_gms"), [128, NE], F32))
        grows = ctx.enter_context(nc.sbuf_tensor(_nm("n_grows"), [128, 8], F32))
        mt = ctx.enter_context(nc.sbuf_tensor(_nm("n_mt"), [128, 4 * NE], F32))
        mT = ctx.enter_context(nc.sbuf_tensor(_nm("n_mT"), [NE, TB], BF16))
        pb = [ctx.enter_context(nc.psum_tensor(_nm(f"n_pb{i}"), [128, 512], F32))
              for i in range(NE)]
        A = alloc_epi(nc, ctx, final=final)

        bank_add = [NONE_MARK] * NE
        p.dma("sync", rwin[:, :], io["L1_ffn_router_w"][:, :], inc=sLD)
        p.dma("gpsimd", rbsb[:, :],
              io["L1_ffn_router_b"].rearrange("(e one) -> e one", one=1), inc=sLD)
        p.dma("gpsimd", ebsb[:, :], io["L1_ffn_expert_b"][:, :], inc=sLD)
        ld1 = sLD.mark()
        for cc in range(NCH):
            p.do("tensor", (lambda cc=cc: (lambda t: t.transpose(
                pb[0][:, cc * NE:(cc + 1) * NE],
                rwin[:, cc * 128:(cc + 1) * 128],
                c.ident_f32[0:NE, 0:NE])))(),
                after=[ld1, c.ready], inc=(sGT if cc == NCH - 1 else None))
        p.do("vector", lambda v: v.tensor_copy(rwt[:, :], pb[0][:, 0:NCH * NE]),
             after=[sGT.mark()], inc=sGE)
        rwt_done = sGE.mark()
        bank_add[0] = sGE.mark()

        wt_ready = {}
        xln_free = NONE_MARK
        glog_free = NONE_MARK
        mbc_free = NONE_MARK
        mT_free = NONE_MARK
        xt_free = [NONE_MARK, NONE_MARK]

        def load_w(kk, after):
            e = kk % NE
            buf = wa if kk % 2 == 0 else wb
            src = io["W1T_l1"][e].rearrange("cc p j -> p cc j")
            p.dma("gpsimd", buf.rearrange("p (cc j) -> p cc j", cc=NCH),
                  src, after=after, inc=sWT)
            return sWT.count

        wt_ready[0] = load_w(0, [])
        wt_ready[1] = load_w(1, [])

        for tb in range(NTB):
            ts, te = tb * TB, (tb + 1) * TB
            for cc in range(NCH):
                p.dma("sync", xln[:, cc * TB:(cc + 1) * TB],
                      io["XLN"][cc, :, ts:te], after=[xln_free], inc=sXL)
            xh_done = sXL.mark()

            for cc in range(NCH):
                p.do("tensor", (lambda cc=cc: (lambda t: t.matmul(
                    pb[7][0:NE, :], rwt[:, cc * NE:(cc + 1) * NE],
                    xln[:, cc * TB:(cc + 1) * TB],
                    start=(cc == 0), stop=(cc == NCH - 1))))(),
                    after=[xh_done, rwt_done, bank_add[7]],
                    inc=(sGT if cc == NCH - 1 else None))
            p.do("vector", lambda v: v.tensor_scalar_add(
                glog[:, :], pb[7][0:NE, :], rbsb[:, 0:1]),
                after=[sGT.mark(), glog_free], inc=sGE)
            gl_done = sGE.mark()
            tp_free = sGE.mark()
            for tl in range(4):
                p.do("tensor", (lambda tl=tl: (lambda t: t.transpose(
                    pb[7][:, tl * NE:(tl + 1) * NE],
                    glog[:, tl * 128:(tl + 1) * 128],
                    c.ident_f32[0:NE, 0:NE])))(),
                    after=[gl_done, tp_free], inc=sGT)
                p.do("vector", (lambda tl=tl: (lambda v: v.tensor_copy(
                    gt[:, tl * NE:(tl + 1) * NE],
                    pb[7][:, tl * NE:(tl + 1) * NE])))(),
                    after=[sGT.mark()], inc=sGE)
                tp_free = sGE.mark()
            gt_all = sGE.mark()
            for tl in range(4):
                gtc = gt[:, tl * NE:(tl + 1) * NE]
                mx = grows[:, 0:1]; nmx = grows[:, 1:2]
                ss = grows[:, 2:3]; rs = grows[:, 3:4]
                p.do("vector", (lambda gtc=gtc, mx=mx: (lambda v: v.reduce_max(
                    mx, gtc, AXX)))(), after=[sGE.mark()], inc=sGE)
                p.do("vector", (lambda mx=mx, nmx=nmx: (lambda v: v.tensor_scalar_mul(
                    nmx, mx, -1.0)))(), after=[sGE.mark()], inc=sGE)
                p.do("scalar", (lambda gtc=gtc, nmx=nmx: (lambda s: s.activation(
                    gms[:, :], gtc, AF.Exp, bias=nmx)))(),
                    after=[sGE.mark()], inc=sGE)
                p.do("vector", (lambda ss=ss: (lambda v: v.reduce_sum(
                    ss, gms[:, :], AXX)))(), after=[sGE.mark()], inc=sGE)
                p.do("vector", (lambda ss=ss, rs=rs: (lambda v: v.reciprocal(
                    rs, ss)))(), after=[sGE.mark()], inc=sGE)
                p.do("vector", (lambda gtc=gtc, mx=mx: (lambda v: v.tensor_single_scalar(
                    gms[:, :], gtc, mx, ALU.is_ge)))(),
                    after=[sGE.mark()], inc=sGE)
                p.do("vector", (lambda tl=tl, rs=rs: (lambda v: v.tensor_scalar(
                    mt[:, tl * NE:(tl + 1) * NE], gms[:, :], rs, 0.5,
                    ALU.mult, ALU.mult)))(),
                    after=[sGE.mark()], inc=sGE)
                p.do("tensor", (lambda tl=tl: (lambda t: t.transpose(
                    pb[7][0:NE, tl * 128:(tl + 1) * 128],
                    mt[:, tl * NE:(tl + 1) * NE], c.ident_f32[:, :])))(),
                    after=[sGE.mark(), gt_all, mT_free], inc=sGT)
            glog_free = sGT.mark()
            p.do("vector", lambda v: v.tensor_copy(mT[0:NE, :], pb[7][0:NE, 0:TB]),
                 after=[sGT.mark()], inc=sGE)
            bank_add[7] = sGE.mark()
            mT_done = sGE.mark()

            mbc_bank_free = [bank_add[0], bank_add[1]]
            for e in range(NE):
                bk = pb[e % 2]
                p.do("tensor", (lambda e=e, bk=bk: (lambda t: t.matmul(
                    bk[:, :], c.sel_bf[:, e * 128:(e + 1) * 128],
                    mT[0:NE, :], start=True, stop=True)))(),
                    after=[mT_done, mbc_bank_free[e % 2], mbc_free],
                    inc=sGT)
                p.do("vector", (lambda e=e, bk=bk: (lambda v: v.tensor_copy(
                    mbc8[:, e * TB:(e + 1) * TB], bk[:, :])))(),
                    after=[sGT.mark()], inc=sGE)
                mbc_bank_free[e % 2] = sGE.mark()
            bank_add[0] = mbc_bank_free[0]
            bank_add[1] = mbc_bank_free[1]
            mbc_done = sGE.mark()

            for e in range(NE):
                kk = tb * NE + e
                buf = wa if kk % 2 == 0 else wb
                wready = (sWT, wt_ready[kk])
                xt = xta if e % 2 == 0 else xtb
                for cc in range(NCH):
                    p.do("vector", (lambda cc=cc, e=e, xt=xt: (lambda v: v.tensor_mul(
                        xt[:, cc * TB:(cc + 1) * TB],
                        xln[:, cc * TB:(cc + 1) * TB],
                        mbc8[:, e * TB:(e + 1) * TB])))(),
                        after=[mbc_done, xh_done, xt_free[e % 2]],
                        inc=sHM)
                xt_done = sHM.mark()
                for jc in range(NCH):
                    for dc in range(NCH):
                        p.do("tensor", (lambda jc=jc, dc=dc, e=e, buf=buf, xt=xt: (lambda t: t.matmul(
                            pb[jc][:, :],
                            buf[:, dc * 1024 + jc * 128: dc * 1024 + (jc + 1) * 128],
                            xt[:, dc * TB:(dc + 1) * TB],
                            start=(e == 0 and dc == 0),
                            stop=(e == NE - 1 and dc == NCH - 1))))(),
                            after=[xt_done, wready,
                                   (bank_add[jc] if e == 0 else None)],
                            inc=(sYP if (jc == NCH - 1 and dc == NCH - 1) else None))
                    if e == 0:
                        p.do("tensor", (lambda jc=jc: (lambda t: t.matmul(
                            pb[jc][:, :], ebsb[:, jc * 128:(jc + 1) * 128],
                            mT[0:NE, :], start=False, stop=False)))(),
                            after=[mT_done, ld1], inc=None)
                xt_free[e % 2] = sYP.mark()
                if kk + 2 < NTB * NE:
                    wt_ready[kk + 2] = load_w(kk + 2, [sYP.mark()])
            y_done = sYP.mark()
            xt_free = [y_done, y_done]
            xln_free = y_done
            mT_free = y_done
            mbc_free = y_done

            if not final:
                def add_src(jc, dst, xr):
                    def f(v):
                        return v.tensor_add(dst, pb[jc][:, :], xr)
                    return f
                m = emit_epilogue_tb(p, io, c, A, pb[0], pb[1], tb, add_src,
                                     src_after=[y_done])
            else:
                def add_src4(jc, t4, dst, xr):
                    def f(v):
                        return v.tensor_add(
                            dst, pb[jc][:, t4 * 128:(t4 + 1) * 128], xr)
                    return f
                m = emit_final_tb(p, io, c, A, pb[0], pb[1], tb, add_src4,
                                  src_after=[y_done])
            for jc in range(NE):
                bank_add[jc] = m
    p.fence()
    p.flush("l1ffn")



def emit_final(p, io, c):
    nc = p.nc
    sXL = p.sem("sXL")
    sMM = p.sem("sMM")
    sEV = p.sem("sEV")
    sST = p.sem("sST")
    with (
        nc.sbuf_tensor(_nm("z_in"), [128, 2 * 4096], F32) as zin,
        nc.sbuf_tensor(_nm("z_out"), [128, 1024], F32) as zout,
        nc.psum_tensor(_nm("z_p0"), [128, 512], F32) as zp0,
        nc.psum_tensor(_nm("z_p1"), [128, 512], F32) as zp1,
    ):
        in_free = [NONE_MARK, NONE_MARK]
        bank_free = [NONE_MARK, NONE_MARK]
        out_free = NONE_MARK
        for tb in range(NTB):
            sl = tb % 2
            for cc in range(NCH):
                p.dma("sync", zin[:, sl * 4096 + cc * TB: sl * 4096 + (cc + 1) * TB],
                      io["XRES"][cc, :, tb * TB:(tb + 1) * TB],
                      after=[in_free[sl]], inc=sXL)
            ld = sXL.mark()
            for t4 in range(4):
                for cc in range(NCH):
                    bk = [zp0, zp1][cc % 2]
                    p.do("tensor", (lambda cc=cc, t4=t4, sl=sl, bk=bk: (lambda t: t.transpose(
                        bk[:, 0:128],
                        zin[:, sl * 4096 + cc * TB + t4 * 128: sl * 4096 + cc * TB + (t4 + 1) * 128],
                        c.ident_f32[:, :])))(),
                        after=[ld, bank_free[cc % 2], c.ready], inc=sMM)
                    p.do("vector", (lambda cc=cc, bk=bk: (lambda v: v.tensor_copy(
                        zout[:, cc * 128:(cc + 1) * 128], bk[:, 0:128])))(),
                        after=[sMM.mark(), out_free], inc=sEV)
                    bank_free[cc % 2] = sEV.mark()
                p.dma("sync", io["out"][tb * TB + t4 * 128: tb * TB + (t4 + 1) * 128, :],
                      zout[:, :], after=[sEV.mark()], inc=sST)
                out_free = sST.mark()
            in_free[sl] = sMM.mark()
    p.fence()
    p.flush("final")



def build_program(parts, xio="internal"):
    nc = bass.Bass("TRN2", target_bir_lowering=False)
    io = declare_io(nc, parts, xio=xio)
    p = Prog(nc)
    from contextlib import ExitStack
    with ExitStack() as ctx:
        c = emit_consts(p, io, ctx, need_proj=any(
            pt.startswith("l0attn") for pt in parts))
        p.flush("consts")
        for part in parts:
            if part == "prologue":
                emit_prologue(p, io, c)
            elif part == "l0ffn":
                emit_l0ffn(p, io, c, final=False)
            elif part == "l0ffn_final":
                emit_l0ffn(p, io, c, final=True)
            elif part == "l1prep":
                emit_l1prep(p, io, c)
            elif part == "l0aprep":
                emit_l0aprep(p, io, c)
            elif part == "l0attn":
                emit_l0attn(p, io, c, final=False)
            elif part == "l1attn":
                emit_l1attn(p, io, c, final=False)
            elif part == "l1ffn":
                emit_l1ffn(p, io, c, final=False)
            elif part == "l1ffn_final":
                emit_l1ffn(p, io, c, final=True)
            elif part == "final":
                emit_final(p, io, c)
        p.fence()
        p.flush("tail")
    return nc, io


def _in_maps(parts, inputs, extra=None):
    need = set()
    for part in parts:
        need |= PART_INPUTS[part.replace("_final", "")]
    xs = np.ascontiguousarray(inputs["L0_attn_x"]) if "L0_attn_x" in need else None
    maps = []
    for core in range(8):
        m = {}
        for k in sorted(need):
            if k == "L0_attn_x":
                m[k] = xs[core]
            else:
                m[k] = np.ascontiguousarray(inputs[k])
        if extra:
            for k, v in extra.items():
                m[k] = v[core]
        maps.append(m)
    return maps


DEFAULT_PARTS = ["prologue", "l0ffn", "l0aprep", "l0attn", "l0ffn",
                 "l1prep", "l1ffn", "l1attn", "l1ffn", "final"]


def kernel(**inputs):
    parts = DEFAULT_PARTS
    nc, io = build_program(parts, xio="internal")
    maps = _in_maps(parts, inputs)
    res = run_bass_kernel_spmd(nc, maps, core_ids=list(range(8)))
    return np.stack([r["out"] for r in res.results], axis=0)


# version 07
# speedup vs baseline: 1.2858x; 1.0720x over previous
import math
import numpy as np
import ml_dtypes
import concourse.bass as bass
import concourse.mybir as mybir
from concourse.bass_utils import run_bass_kernel_spmd

F32 = mybir.dt.float32
BF16 = mybir.dt.bfloat16
AF = mybir.ActivationFunctionType
ALU = mybir.AluOpType
AXX = mybir.AxisListType.X

NT = 4096
DM = 1024
NCH = 8
TB = 512
NTB = NT // TB
HALF = 2048
DH = 4096
NE = 8
DKM = 128
DHEAD = 64
NHEADS = 16
DLAT = 256
CDEC = 1e-4
EPS = 1e-5
L1SCALE = 1.0 / (math.sqrt(1024.0) + 1e-8)

ENGINES = ("tensor", "vector", "scalar", "gpsimd", "sync")

_NMC = [0]


def _nm(name):
    _NMC[0] += 1
    return f"{name}_{_NMC[0]}"


NONE_MARK = (None, 0)
_DBG_OT = [False]


class Sem:
    def __init__(self, name, handle):
        self.name = name
        self.h = handle
        self.count = 0

    def mark(self):
        return (self, self.count)


class Prog:

    def __init__(self, nc):
        self.nc = nc
        self.recs = {e: [] for e in ENGINES}
        self.sems = {}
        self.known = {}

    def sem(self, name):
        if name not in self.sems:
            self.sems[name] = Sem(name, self.nc.alloc_semaphore(name=name))
        return self.sems[name]

    def do(self, eng, fn, after=(), inc=None, inc_n=1, dma=False):
        waits = []
        for a in after:
            if a is None:
                continue
            s, v = (a if isinstance(a, tuple) else (a, a.count))
            if s is None or v <= 0:
                continue
            k = (eng, s.name)
            if v > self.known.get(k, 0):
                waits.append((s, v))
                self.known[k] = v
        tag = None
        if inc is not None:
            inc.count += inc_n
            tag = (inc, inc_n)
        self.recs[eng].append((waits, fn, tag))
        return (inc, inc.count) if inc is not None else None

    def dma(self, eng, dst, src, after=(), inc=None, **kw):
        return self.do(eng, lambda e: e.dma_start(dst, src, **kw),
                       after=after, inc=inc, inc_n=16, dma=True)

    def fence(self):
        vals = [(s, s.count) for s in self.sems.values() if s.count > 0]
        for eng in ENGINES:
            waits = []
            for s, v in vals:
                k = (eng, s.name)
                if v > self.known.get(k, 0):
                    waits.append((s, v))
                    self.known[k] = v
            if waits:
                self.recs[eng].append((waits, None, None))

    def flush(self, name="blk"):
        recs = self.recs
        self.recs = {e: [] for e in ENGINES}
        if not any(recs.values()):
            return
        with self.nc.Block(_nm(name)) as blk:
            def make(rlist):
                def run(e):
                    for waits, fn, tag in rlist:
                        for (s, v) in waits:
                            e.wait_ge(s.h, v)
                        if fn is not None:
                            ins = fn(e)
                            if tag is not None:
                                ins.then_inc(tag[0].h, tag[1])
                return run
            for nm, meth in (("tensor", blk.tensor), ("vector", blk.vector),
                             ("scalar", blk.scalar), ("gpsimd", blk.gpsimd),
                             ("sync", blk.sync)):
                if recs[nm]:
                    meth(make(recs[nm]))



INPUT_SHAPES = {
    "L0_attn_x": (NT, DM),
    "L0_attn_proj_mat": (DHEAD, DKM),
    "L0_attn_Wq": (DM, DM), "L0_attn_bq": (DM,),
    "L0_attn_Wk": (DM, DM), "L0_attn_bk": (DM,),
    "L0_attn_Wv": (DM, DM), "L0_attn_bv": (DM,),
    "L0_attn_Wo": (DM, DM), "L0_attn_bo": (DM,),
    "L0_ffn_Wg": (DM, NE), "L0_ffn_bg": (NE,),
    "L0_ffn_W1": (NE, DM, DH), "L0_ffn_b1": (NE, DH),
    "L0_ffn_W2": (NE, DH, DM), "L0_ffn_b2": (NE, DM),
    "L1_attn_Wq": (DM, DM), "L1_attn_Wk": (DM, DM),
    "L1_attn_Wv": (DM, DM), "L1_attn_Wo": (DM, DM),
    "L1_attn_W_down_kv": (DM, DLAT),
    "L1_attn_W_up_k": (DLAT, DM), "L1_attn_W_up_v": (DLAT, DM),
    "L1_ffn_router_w": (NE, DM), "L1_ffn_router_b": (NE,),
    "L1_ffn_expert_w": (NE, DM, DM), "L1_ffn_expert_b": (NE, DM),
}

PART_INPUTS = {
    "prologue": {"L0_attn_x"},
    "l0ffn": {"L0_ffn_Wg", "L0_ffn_bg", "L0_ffn_W1", "L0_ffn_b1",
              "L0_ffn_W2", "L0_ffn_b2"},
    "l0attn": {"L0_attn_proj_mat", "L0_attn_Wq", "L0_attn_bq", "L0_attn_Wk",
               "L0_attn_bk", "L0_attn_Wv", "L0_attn_bv", "L0_attn_Wo",
               "L0_attn_bo"},
    "l1attn": {"L1_attn_Wq", "L1_attn_Wk", "L1_attn_Wv", "L1_attn_Wo",
               "L1_attn_W_down_kv", "L1_attn_W_up_k", "L1_attn_W_up_v"},
    "l1ffn": {"L1_ffn_router_w", "L1_ffn_router_b", "L1_ffn_expert_w",
              "L1_ffn_expert_b"},
    "l1prep": {"L1_ffn_expert_w"},
    "l0aprep": {"L0_attn_Wq", "L0_attn_Wk", "L0_attn_Wv", "L0_attn_Wo"},
    "final": set(),
}


def declare_io(nc, parts, xio="internal"):
    d = {}
    need = set()
    for part in parts:
        need |= PART_INPUTS[part.replace("_final", "")]
    for name in sorted(need):
        d[name] = nc.dram_tensor(name, list(INPUT_SHAPES[name]), F32,
                                 kind="ExternalInput").ap()
    if any(pt.endswith("_final") or pt == "final" for pt in parts):
        d["out"] = nc.dram_tensor("out", [NT, DM], F32, kind="ExternalOutput").ap()
    if any(pt.startswith("l1prep") or pt.startswith("l1ffn") for pt in parts):
        d["W1T_l1"] = nc.dram_tensor("W1T_l1", [NE, NCH, 128, DM], BF16).ap()
    if any(pt.startswith("l0a") for pt in parts):
        d["WT_l0a"] = nc.dram_tensor("WT_l0a", [4, NCH, 128, DM], BF16).ap()
    if any(pt.startswith("l0a") or pt.startswith("l1attn") for pt in parts):
        d["OT"] = nc.dram_tensor("OT", [NCH, 128, NT], BF16,
                                 kind=("ExternalOutput" if _DBG_OT[0] else "Internal")).ap()
    if any(pt.startswith("l1attn") for pt in parts):
        d["QTd"] = nc.dram_tensor("QTd", [NCH, 128, NT], BF16).ap()
        if _DBG_OT[0]:
            d["DBGV"] = nc.dram_tensor("DBGV", [128, NHEADS * 32 * 65], BF16,
                                       kind="ExternalOutput").ap()
        d["KVSd"] = nc.dram_tensor("KVSd", [NCH, 128, NT], BF16).ap()
        d["KRd"] = nc.dram_tensor("KRd", [NCH, 128, NT], BF16).ap()
        d["VRd"] = nc.dram_tensor("VRd", [NCH, 128, NT], BF16).ap()
    if xio == "internal":
        d["XLN"] = nc.dram_tensor("XLN", [NCH, 128, NT], BF16).ap()
        d["XRES"] = nc.dram_tensor("XRES", [NCH, 128, NT], F32).ap()
        d["XLN_o"], d["XRES_o"] = d["XLN"], d["XRES"]
    else:
        if xio in ("in", "inout"):
            d["XLN"] = nc.dram_tensor("XLN", [NCH, 128, NT], BF16,
                                      kind="ExternalInput").ap()
            d["XRES"] = nc.dram_tensor("XRES", [NCH, 128, NT], F32,
                                       kind="ExternalInput").ap()
        if xio in ("out", "inout"):
            d["XLN_o"] = nc.dram_tensor("XLN_o", [NCH, 128, NT], BF16,
                                        kind="ExternalOutput").ap()
            d["XRES_o"] = nc.dram_tensor("XRES_o", [NCH, 128, NT], F32,
                                         kind="ExternalOutput").ap()
        if xio == "in":
            d["XLN_o"], d["XRES_o"] = d["XLN"], d["XRES"]
        if xio == "out":
            d["XLN"], d["XRES"] = d["XLN_o"], d["XRES_o"]
    return d



class Consts:
    pass


def emit_consts(p, io, ctx, need_proj=False):
    nc = p.nc
    c = Consts()
    c.ident_bf = ctx.enter_context(nc.sbuf_tensor(_nm("c_ident_bf"), [128, 128], BF16))
    c.ident_f32 = ctx.enter_context(nc.sbuf_tensor(_nm("c_ident_f32"), [128, 128], F32))
    c.ones_col_bf = ctx.enter_context(nc.sbuf_tensor(_nm("c_ones_col_bf"), [128, 1], BF16))
    c.ones_row_f32 = ctx.enter_context(nc.sbuf_tensor(_nm("c_ones_row_f32"), [1, 128], F32))
    c.sel_bf = ctx.enter_context(nc.sbuf_tensor(_nm("c_sel_bf"), [NE, NE * 128], BF16))
    c.eps_col = ctx.enter_context(nc.sbuf_tensor(_nm("c_eps_col"), [128, 1], F32))
    c.ones_blk = ctx.enter_context(nc.sbuf_tensor(_nm("c_ones_blk"), [128, 128], F32))
    sGP = p.sem("sGP")
    sLD = p.sem("sLD")

    def ident_fill(t):
        def f(g):
            return g.affine_select(out=t[:, :], in_=t[:, :],
                                   compare_op=ALU.not_equal, fill=1.0,
                                   base=0, pattern=[[-1, 128]],
                                   channel_multiplier=1)
        return f

    p.do("gpsimd", lambda g: g.memset(c.ident_bf[:, :], 0.0))
    p.do("gpsimd", ident_fill(c.ident_bf))
    p.do("gpsimd", lambda g: g.memset(c.ident_f32[:, :], 0.0))
    p.do("gpsimd", ident_fill(c.ident_f32))
    p.do("gpsimd", lambda g: g.memset(c.ones_col_bf[:, :], 1.0))
    p.do("gpsimd", lambda g: g.memset(c.ones_row_f32[:, :], 1.0))
    p.do("gpsimd", lambda g: g.memset(c.eps_col[:, :], float(EPS)))
    p.do("gpsimd", lambda g: g.memset(c.ones_blk[:, :], 1.0))
    p.do("gpsimd", lambda g: g.memset(c.sel_bf[:, :], 0.0))
    for e in range(NE):
        def selfill(e=e):
            def f(g):
                return g.affine_select(
                    out=c.sel_bf[:, e * 128:(e + 1) * 128],
                    in_=c.sel_bf[:, e * 128:(e + 1) * 128],
                    compare_op=ALU.not_equal, fill=1.0,
                    base=-e, pattern=[[0, 128]], channel_multiplier=1)
            return f
        p.do("gpsimd", selfill(), inc=(sGP if e == NE - 1 else None))
    c.ready = sGP.mark()
    if need_proj:
        c.projpair = ctx.enter_context(nc.sbuf_tensor(_nm("c_projpair"), [128, 256], BF16))
        p.do("gpsimd", lambda g: g.memset(c.projpair[:, :], 0.0), inc=sGP)
        mz = sGP.mark()
        p.dma("gpsimd", c.projpair[0:64, 0:128], io["L0_attn_proj_mat"][:, :],
              after=[mz], inc=sLD)
        p.dma("gpsimd", c.projpair[64:128, 128:256], io["L0_attn_proj_mat"][:, :],
              after=[mz], inc=sLD)
        c.proj_ready = sLD.mark()
    return c



class EpiBufs:
    pass


def alloc_epi(nc, ctx, final=False):
    A = EpiBufs()
    A.stage = ctx.enter_context(nc.sbuf_tensor(_nm("ep_stage"), [128, 2 * TB], F32))
    A.xbf = ctx.enter_context(nc.sbuf_tensor(_nm("ep_xbf"), [128, 4096], BF16))
    A.sq = ctx.enter_context(nc.sbuf_tensor(_nm("ep_sq"), [128, 2 * TB], BF16))
    A.xr = ctx.enter_context(nc.sbuf_tensor(_nm("ep_xr"), [128, 2 * TB], F32))
    A.rows = ctx.enter_context(nc.sbuf_tensor(_nm("ep_rows"), [1, 2 * TB], F32))
    A.tmp = ctx.enter_context(nc.sbuf_tensor(_nm("ep_tmp"), [128, TB], F32))
    if final:
        A.outst = ctx.enter_context(nc.sbuf_tensor(_nm("ep_outst"), [128, 1024], F32))
    A.prev_done = NONE_MARK
    A.xr_free = [NONE_MARK, NONE_MARK]
    A.st_free = [NONE_MARK, NONE_MARK]
    return A


def emit_epilogue_tb(p, io, c, A, pe0, pe1, tb, add_src, src_after=()):
    sXR = p.sem("sXR")
    sEPs = p.sem("sEPs")
    sEPm = p.sem("sEPm")
    sEPr = p.sem("sEPr")
    sEPd = p.sem("sEPd")
    sST = p.sem("sST")

    ts, te = tb * TB, (tb + 1) * TB
    prev = A.prev_done

    for jc in range(NCH):
        sl = jc % 2
        xrs = A.xr[:, sl * TB:(sl + 1) * TB]
        sts = A.stage[:, sl * TB:(sl + 1) * TB]
        p.dma("sync", xrs, io["XRES"][jc, :, ts:te],
              after=[A.xr_free[sl], prev], inc=sXR)
        xr_done = sXR.mark()
        p.do("vector", add_src(jc, sts, xrs),
             after=list(src_after) + [xr_done, A.st_free[sl], prev], inc=sEPs)
        A.xr_free[sl] = sEPs.mark()
        p.do("vector", (lambda jc=jc, sts=sts: (lambda v: v.tensor_copy(
            A.xbf[:, jc * TB:(jc + 1) * TB], sts)))(),
            after=[sEPs.mark()], inc=sEPs)
        p.dma("sync", io["XRES_o"][jc, :, ts:te], sts,
              after=[sEPs.mark()], inc=sST)
        A.st_free[sl] = sST.mark()
    xbf_done = sEPs.mark()

    for jc in range(NCH):
        p.do("tensor", (lambda jc=jc: (lambda t: t.matmul(
            pe0[0:1, :], c.ones_col_bf[:, :],
            A.xbf[:, jc * TB:(jc + 1) * TB],
            start=(jc == 0), stop=(jc == NCH - 1))))(),
            after=[xbf_done, c.ready, prev], inc=(sEPm if jc == NCH - 1 else None))
    sq_mm = NONE_MARK
    for jc in range(NCH):
        slot = jc % 2
        p.do("scalar", (lambda jc=jc, slot=slot: (lambda s: s.activation(
            A.sq[:, slot * TB:(slot + 1) * TB],
            A.xbf[:, jc * TB:(jc + 1) * TB], AF.Square)))(),
            after=[xbf_done, sq_mm], inc=sEPr)
        sq_done = sEPr.mark()
        p.do("tensor", (lambda jc=jc, slot=slot: (lambda t: t.matmul(
            pe1[0:1, :], c.ones_col_bf[:, :],
            A.sq[:, slot * TB:(slot + 1) * TB],
            start=(jc == 0), stop=(jc == NCH - 1))))(),
            after=[sq_done, prev], inc=sEPm)
        if jc >= 1:
            sq_mm = (p.sems["sEPm"], p.sems["sEPm"].count - 1)
    mm_done = sEPm.mark()

    mean = A.rows[0:1, 0:TB]
    var = A.tmp[0:1, 0:TB]
    rr = A.rows[0:1, TB:2 * TB]
    p.do("vector", lambda v: v.tensor_scalar_mul(mean, pe0[0:1, :], 1.0 / DM),
         after=[mm_done], inc=sEPr)
    p.do("scalar", lambda s: s.activation(var, mean, AF.Square),
         after=[sEPr.mark()], inc=sEPr)
    p.do("vector", lambda v: v.scalar_tensor_tensor(
        var, pe1[0:1, :], 1.0 / DM, var, ALU.mult, ALU.subtract),
        after=[sEPr.mark()], inc=sEPr)
    p.do("scalar", lambda s: s.activation(var, var, AF.Ln,
                                          bias=c.eps_col[0:1, 0:1]),
         after=[sEPr.mark()], inc=sEPr)
    p.do("scalar", lambda s: s.activation(rr, var, AF.Exp, scale=-0.5),
         after=[sEPr.mark()], inc=sEPr)
    rows_done = sEPr.mark()

    p.do("tensor", lambda t: t.matmul(pe0[:, :], c.ones_row_f32[:, :], mean,
                                      start=True, stop=True),
         after=[rows_done], inc=sEPm)
    p.do("tensor", lambda t: t.matmul(pe1[:, :], c.ones_row_f32[:, :], rr,
                                      start=True, stop=True),
         after=[rows_done], inc=sEPm)
    bc_done = sEPm.mark()

    for jc in range(NCH):
        p.do("vector", (lambda jc=jc: (lambda v: v.tensor_sub(
            A.tmp[:, :], A.xbf[:, jc * TB:(jc + 1) * TB], pe0[:, :])))(),
            after=[bc_done], inc=sEPd)
        p.do("vector", (lambda jc=jc: (lambda v: v.tensor_mul(
            A.xbf[:, jc * TB:(jc + 1) * TB], A.tmp[:, :], pe1[:, :])))(),
            after=[sEPd.mark()], inc=sEPd)
        p.dma("sync", io["XLN_o"][jc, :, ts:te], A.xbf[:, jc * TB:(jc + 1) * TB],
              after=[sEPd.mark()], inc=sST)
    A.prev_done = sEPd.mark()
    return A.prev_done


def emit_final_tb(p, io, c, A, pe0, pe1, tb, add_src4, src_after=()):
    sXR = p.sem("sXR")
    sEPs = p.sem("sEPs")
    sEPd = p.sem("sEPd")
    sEPt = p.sem("sEPt")
    sST = p.sem("sST")

    ts = tb * TB
    prev = A.prev_done
    dma_done = NONE_MARK
    for t4 in range(4):
        for jc in range(NCH):
            sl = jc % 2
            xrs = A.xr[:, sl * TB:sl * TB + 128]
            sts = A.stage[:, sl * TB:sl * TB + 128]
            p.dma("sync", xrs,
                  io["XRES"][jc, :, ts + t4 * 128:ts + (t4 + 1) * 128],
                  after=[A.xr_free[sl], prev], inc=sXR)
            xr_done = sXR.mark()
            p.do("vector", add_src4(jc, t4, sts, xrs),
                 after=list(src_after) + [xr_done, A.st_free[sl], prev],
                 inc=sEPs)
            A.xr_free[sl] = sEPs.mark()
            st_done = sEPs.mark()
            bk = pe0 if jc % 2 == 0 else pe1
            p.do("tensor", (lambda bk=bk, sts=sts: (lambda t: t.transpose(
                bk[:, 0:128], sts, c.ident_f32[:, :])))(),
                after=[st_done, c.ready,
                       (p.sems["sEPd"], p.sems["sEPd"].count - 1)], inc=sEPt)
            A.st_free[sl] = sEPt.mark()
            p.do("vector", (lambda jc=jc, bk=bk: (lambda v: v.tensor_copy(
                A.outst[:, jc * 128:(jc + 1) * 128], bk[:, 0:128])))(),
                after=[sEPt.mark(), dma_done], inc=sEPd)
        p.dma("sync", io["out"][ts + t4 * 128: ts + (t4 + 1) * 128, :],
              A.outst[:, :], after=[sEPd.mark()], inc=sST)
        dma_done = sST.mark()
    A.prev_done = sEPd.mark()
    return A.prev_done



def emit_prologue(p, io, c):
    nc = p.nc
    sLD = p.sem("sLD")
    sPR = p.sem("sPR")
    sPT = p.sem("sPT")
    sPV = p.sem("sPV")
    sST = p.sem("sST")

    with (
        nc.sbuf_tensor(_nm("pr_x"), [128, 2 * DM], F32) as xt,
        nc.sbuf_tensor(_nm("pr_sq"), [128, DM], F32) as xsq,
        nc.sbuf_tensor(_nm("pr_ln"), [128, DM], BF16) as xln,
        nc.sbuf_tensor(_nm("pr_rows"), [128, 12], F32) as rows,
        nc.sbuf_tensor(_nm("pr_stb"), [128, 4 * DM], BF16) as stb,
        nc.sbuf_tensor(_nm("pr_stf"), [128, 2 * DM], F32) as stf,
        nc.psum_tensor(_nm(f"pr_pb"), [128, 1024], BF16) as pb,
        nc.psum_tensor(_nm(f"pr_pf0"), [128, 512], F32) as pf0,
        nc.psum_tensor(_nm(f"pr_pf1"), [128, 512], F32) as pf1,
    ):
        slot_free = [NONE_MARK, NONE_MARK]
        xln_free = NONE_MARK
        xsq_free = NONE_MARK
        pb_free = NONE_MARK
        pf_free = NONE_MARK
        stb_free = NONE_MARK
        stf_free = NONE_MARK
        row_free = [NONE_MARK, NONE_MARK]

        for tt in range(NT // 128):
            slot = tt % 2
            grp, sub = tt // 4, tt % 4
            xs = xt[:, slot * DM:(slot + 1) * DM]
            p.dma("sync", xs, io["L0_attn_x"][tt * 128:(tt + 1) * 128, :],
                  after=[slot_free[slot]], inc=sLD)
            ld = sLD.mark()

            rbase = (tt % 2) * 6
            s_ = rows[:, rbase + 0:rbase + 1]
            m_ = rows[:, rbase + 1:rbase + 2]
            s2 = rows[:, rbase + 2:rbase + 3]
            mq = rows[:, rbase + 3:rbase + 4]
            vr = rows[:, rbase + 4:rbase + 5]
            rc = rows[:, rbase + 5:rbase + 6]
            p.do("vector", (lambda xs=xs, s_=s_: (lambda v: v.reduce_sum(
                s_, xs, AXX)))(), after=[ld, row_free[tt % 2]], inc=sPR)
            p.do("vector", (lambda s_=s_, m_=m_: (lambda v: v.tensor_scalar_mul(
                m_, s_, 1.0 / DM)))(), after=[sPR.mark()], inc=sPR)
            p.do("scalar", (lambda xs=xs: (lambda s: s.activation(
                xsq[:, :], xs, AF.Square)))(), after=[ld, xsq_free], inc=sPR)
            p.do("vector", (lambda s2=s2: (lambda v: v.reduce_sum(
                s2, xsq[:, :], AXX)))(), after=[sPR.mark()], inc=sPR)
            xsq_free = sPR.mark()
            p.do("scalar", (lambda m_=m_, mq=mq: (lambda s: s.activation(
                mq, m_, AF.Square)))(), after=[sPR.mark()], inc=sPR)
            p.do("vector", (lambda s2=s2, mq=mq, vr=vr: (lambda v: v.scalar_tensor_tensor(
                vr, s2, 1.0 / DM, mq, ALU.mult, ALU.subtract)))(),
                after=[sPR.mark()], inc=sPR)
            p.do("scalar", (lambda vr=vr: (lambda s: s.activation(
                vr, vr, AF.Ln, bias=c.eps_col[:, 0:1])))(),
                after=[sPR.mark()], inc=sPR)
            p.do("scalar", (lambda vr=vr, rc=rc: (lambda s: s.activation(
                rc, vr, AF.Exp, scale=-0.5)))(),
                after=[sPR.mark()], inc=sPR)
            p.do("vector", (lambda xs=xs, m_=m_, rc=rc: (lambda v: v.tensor_scalar(
                xln[:, :], xs, m_, rc, ALU.subtract, ALU.mult)))(),
                after=[sPR.mark(), xln_free], inc=sPR)
            ln_done = sPR.mark()
            row_free[tt % 2] = ln_done

            for cc in range(NCH):
                p.do("tensor", (lambda cc=cc: (lambda t: t.transpose(
                    pb[:, cc * 128:(cc + 1) * 128],
                    xln[:, cc * 128:(cc + 1) * 128], c.ident_bf[:, :])))(),
                    after=[ln_done, pb_free, c.ready],
                    inc=(sPT if cc == NCH - 1 else None))
            xln_free = sPT.mark()
            p.do("vector", (lambda sub=sub: (lambda v: v.tensor_copy(
                stb[:, sub * DM:(sub + 1) * DM], pb[:, :])))(),
                after=[sPT.mark(), stb_free], inc=sPV)
            pb_free = sPV.mark()

            for cc in range(NCH):
                pf = pf0 if cc % 2 == 0 else pf1
                off = (cc // 2) * 128
                p.do("tensor", (lambda pf=pf, off=off, cc=cc, xs=xs: (lambda t: t.transpose(
                    pf[:, off:off + 128], xs[:, cc * 128:(cc + 1) * 128],
                    c.ident_f32[:, :])))(),
                    after=[ld, pf_free, c.ready],
                    inc=(sPT if cc >= NCH - 2 else None))
            slot_free[slot] = sPT.mark()
            p.do("vector", lambda v: v.tensor_copy(stf[:, 0:512], pf0[:, :]),
                 after=[sPT.mark(), stf_free], inc=sPV)
            p.do("vector", lambda v: v.tensor_copy(stf[:, 512:1024], pf1[:, :]),
                 after=[sPT.mark(), stf_free], inc=sPV)
            pf_free = sPV.mark()
            for cc in range(NCH):
                off = 512 * (cc % 2) + (cc // 2) * 128
                p.dma("sync", io["XRES_o"][cc, :, tt * 128:(tt + 1) * 128],
                      stf[:, off:off + 128], after=[sPV.mark()], inc=sST)
            stf_free = sST.mark()

            if sub == 3:
                for cc in range(NCH):
                    src = stb.rearrange("p (s c x) -> p s c x", s=4, c=NCH)[:, :, cc, :]
                    dst = io["XLN_o"][cc, :, grp * 512:(grp + 1) * 512]
                    dst = dst.rearrange("p (s x) -> p s x", s=4)
                    p.dma("sync", dst, src, after=[sPV.mark()], inc=sST)
                stb_free = sST.mark()
        p.fence()
        p.flush("prologue")



def emit_l0ffn(p, io, c, final=False):
    nc = p.nc
    sXL = p.sem("sXL")
    sWT = p.sem("sWT")
    sHP = p.sem("sHP")
    sHS = p.sem("sHS")
    sHM = p.sem("sHM")
    sYP = p.sem("sYP")
    sYE = p.sem("sYE")
    sGT = p.sem("sGT")
    sGE = p.sem("sGE")
    sGP = p.sem("sGP")
    sLD = p.sem("sLD")

    from contextlib import ExitStack
    with ExitStack() as ctx:
        xh = ctx.enter_context(nc.sbuf_tensor(_nm("f_xh"), [128, NCH * HALF], BF16))
        hb = ctx.enter_context(nc.sbuf_tensor(_nm("f_hb"), [128, NCH * HALF], BF16))
        wa = ctx.enter_context(nc.sbuf_tensor(_nm("f_wa"), [128, 16384], BF16))
        wb = ctx.enter_context(nc.sbuf_tensor(_nm("f_wb"), [128, 16384], BF16))
        yb = ctx.enter_context(nc.sbuf_tensor(_nm("f_yb"), [128, NCH * HALF], BF16))
        wg = ctx.enter_context(nc.sbuf_tensor(_nm("f_wg"), [128, NCH * NE], BF16))
        bgsb = ctx.enter_context(nc.sbuf_tensor(_nm("f_bg"), [NE, 1], F32))
        b1all = ctx.enter_context(nc.sbuf_tensor(_nm("f_b1"), [32, NE * 128], F32))
        b1T = ctx.enter_context(nc.sbuf_tensor(_nm("f_b1T"), [128, 32 * NE], F32))
        b2sb = ctx.enter_context(nc.sbuf_tensor(_nm("f_b2"), [NE, DM], BF16))
        glog = ctx.enter_context(nc.sbuf_tensor(_nm("f_glog"), [NE, TB], F32))
        gt = ctx.enter_context(nc.sbuf_tensor(_nm("f_gt"), [128, 2 * NE], F32))
        gms = ctx.enter_context(nc.sbuf_tensor(_nm("f_gms"), [128, NE], F32))
        grows = ctx.enter_context(nc.sbuf_tensor(_nm("f_grows"), [128, 8], F32))
        mt = ctx.enter_context(nc.sbuf_tensor(_nm("f_mt"), [128, 16 * NE], BF16))
        mT = ctx.enter_context(nc.sbuf_tensor(_nm("f_mT"), [NE, HALF], BF16))
        mbc = ctx.enter_context(nc.sbuf_tensor(_nm("f_mbc"), [128, HALF], BF16))
        ph0 = ctx.enter_context(nc.psum_tensor(_nm(f"f_ph0"), [128, 512], F32))
        ph1 = ctx.enter_context(nc.psum_tensor(_nm(f"f_ph1"), [128, 512], F32))
        py0 = ctx.enter_context(nc.psum_tensor(_nm(f"f_py0"), [128, 512], F32))
        py1 = ctx.enter_context(nc.psum_tensor(_nm(f"f_py1"), [128, 512], F32))
        pg = ctx.enter_context(nc.psum_tensor(_nm(f"f_pg"), [128, 512], F32))
        ptb = ctx.enter_context(nc.psum_tensor(_nm(f"f_ptb"), [128, 1024], BF16))
        pe0 = ctx.enter_context(nc.psum_tensor(_nm(f"f_pe0"), [128, 512], F32))
        pe1 = ctx.enter_context(nc.psum_tensor(_nm(f"f_pe1"), [128, 512], F32))
        A = alloc_epi(nc, ctx, final=final)

        p.dma("gpsimd", wg.rearrange("p (cc e) -> p cc e", cc=NCH),
              io["L0_ffn_Wg"].rearrange("(cc p) e -> p cc e", p=128),
              inc=sLD, allow_slow_non_contiguous=True)
        p.dma("gpsimd", bgsb[:, :],
              io["L0_ffn_bg"].rearrange("(e one) -> e one", one=1), inc=sLD)
        for e in range(NE):
            p.dma("sync", b1all[0:32, e * 128:(e + 1) * 128],
                  io["L0_ffn_b1"][e].rearrange("(cc pp) -> cc pp", pp=128),
                  inc=sLD)
        p.dma("gpsimd", b2sb[:, :], io["L0_ffn_b2"][:, :], inc=sLD)
        ld1 = sLD.mark()
        for e in range(NE):
            p.do("tensor", (lambda e=e: (lambda t: t.transpose(
                pg[:, e * 32:(e + 1) * 32], b1all[0:32, e * 128:(e + 1) * 128],
                c.ident_f32[0:32, 0:32])))(),
                after=[ld1, c.ready], inc=(sGT if e == NE - 1 else None))
        b1Tv = b1T.rearrange("p (hc e) -> p hc e", e=NE)
        for e in range(NE):
            p.do("vector", (lambda e=e: (lambda v: v.tensor_copy(
                b1Tv[:, :, e], pg[:, e * 32:(e + 1) * 32])))(),
                after=[sGT.mark()], inc=sGE)
        pg_free = sGE.mark()
        b1T_ready = sGE.mark()

        phases = [(e, q) for e in range(NE) for q in range(4)]

        def load_phase(kk, after):
            e, q = phases[kk]
            buf = wa if kk % 2 == 0 else wb
            w1src = io["L0_ffn_W1"][e].rearrange("(cc pp) h -> pp cc h", pp=128)
            w1src = w1src[:, :, q * 1024:(q + 1) * 1024]
            p.dma("gpsimd", buf[:, 0:8192].rearrange("p (cc h) -> p cc h", cc=NCH),
                  w1src, after=after, inc=sWT)
            w2src = io["L0_ffn_W2"][e][q * 1024:(q + 1) * 1024, :]
            w2src = w2src.rearrange("(hl pp) j -> pp hl j", pp=128)
            p.dma("gpsimd", buf[:, 8192:16384].rearrange("p (hl j) -> p hl j", hl=NCH),
                  w2src, after=after, inc=sWT)
            return sWT.count

        wt_ready = {}
        xh_free = NONE_MARK
        hb_free = NONE_MARK
        mT_free = NONE_MARK
        mbc_free = NONE_MARK
        yb_free = NONE_MARK
        ptb_free = NONE_MARK

        for half in range(2):
            for cc in range(NCH):
                p.dma("sync", xh[:, cc * HALF:(cc + 1) * HALF],
                      io["XLN"][cc, :, half * HALF:(half + 1) * HALF],
                      after=[xh_free], inc=sXL)
            xh_done = sXL.mark()

            if half == 0:
                wt_ready[0] = load_phase(0, [])
                wt_ready[1] = load_phase(1, [])

            glog_free = NONE_MARK
            for tq in range(4):
                for cc in range(NCH):
                    p.do("tensor", (lambda cc=cc, tq=tq: (lambda t: t.matmul(
                        pg[0:NE, :], wg[:, cc * NE:(cc + 1) * NE],
                        xh[:, cc * HALF + tq * TB: cc * HALF + (tq + 1) * TB],
                        start=(cc == 0), stop=(cc == NCH - 1))))(),
                        after=[xh_done, pg_free, ld1],
                        inc=(sGT if cc == NCH - 1 else None))
                p.do("vector", (lambda tq=tq: (lambda v: v.tensor_scalar_add(
                    glog[:, :], pg[0:NE, :], bgsb[:, 0:1])))(),
                    after=[sGT.mark(), glog_free], inc=sGE)
                pg_free = sGE.mark()
                for tl in range(4):
                    tc = tq * 4 + tl
                    p.do("tensor", (lambda tl=tl: (lambda t: t.transpose(
                        pg[:, 0:NE], glog[:, tl * 128:(tl + 1) * 128],
                        c.ident_f32[0:NE, 0:NE])))(),
                        after=[sGE.mark(), pg_free], inc=sGT)
                    p.do("vector", (lambda tc=tc: (lambda v: v.tensor_copy(
                        gt[:, (tc % 2) * NE:(tc % 2 + 1) * NE], pg[:, 0:NE])))(),
                        after=[sGT.mark()], inc=sGE)
                    pg_free = sGE.mark()
                    gtc = gt[:, (tc % 2) * NE:(tc % 2 + 1) * NE]
                    mx = grows[:, 0:1]; nmx = grows[:, 1:2]
                    ss = grows[:, 2:3]; rs = grows[:, 3:4]
                    p.do("vector", (lambda gtc=gtc, mx=mx: (lambda v: v.reduce_max(
                        mx, gtc, AXX)))(), after=[sGE.mark()], inc=sGE)
                    p.do("vector", (lambda mx=mx, nmx=nmx: (lambda v: v.tensor_scalar_mul(
                        nmx, mx, -1.0)))(), after=[sGE.mark()], inc=sGE)
                    p.do("scalar", (lambda gtc=gtc, nmx=nmx: (lambda s: s.activation(
                        gms[:, :], gtc, AF.Exp, bias=nmx)))(),
                        after=[sGE.mark()], inc=sGE)
                    p.do("vector", (lambda ss=ss: (lambda v: v.reduce_sum(
                        ss, gms[:, :], AXX)))(), after=[sGE.mark()], inc=sGE)
                    p.do("vector", (lambda ss=ss, rs=rs: (lambda v: v.reciprocal(
                        rs, ss)))(), after=[sGE.mark()], inc=sGE)
                    p.do("vector", (lambda gtc=gtc, mx=mx: (lambda v: v.tensor_single_scalar(
                        gms[:, :], gtc, mx, ALU.is_ge)))(),
                        after=[sGE.mark()], inc=sGE)
                    p.do("vector", (lambda tc=tc, rs=rs: (lambda v: v.tensor_scalar(
                        mt[:, tc * NE:(tc + 1) * NE], gms[:, :], rs, 0.5,
                        ALU.mult, ALU.mult)))(),
                        after=[sGE.mark()], inc=sGE)
                    p.do("tensor", (lambda tc=tc: (lambda t: t.transpose(
                        ptb[0:NE, (tc % 8) * 128:(tc % 8 + 1) * 128],
                        mt[:, tc * NE:(tc + 1) * NE], c.ident_bf[:, :])))(),
                        after=[sGE.mark(), ptb_free], inc=sGT)
                    if tc % 8 == 7:
                        p.do("vector", (lambda tc=tc: (lambda v: v.tensor_copy(
                            mT[0:NE, (tc - 7) * 128:(tc + 1) * 128], ptb[0:NE, :])))(),
                            after=[sGT.mark()], inc=sGE)
                        ptb_free = sGE.mark()
                glog_free = sGT.mark()
            mT_done = sGE.mark()

            yg = [A.prev_done] if half > 0 else []
            p.do("gpsimd", lambda g: g.memset(yb[:, :], 0.0),
                 after=yg + [yb_free], inc=sGP)
            y_clear = sGP.mark()

            for e in range(NE):
                for tq in range(4):
                    p.do("tensor", (lambda e=e, tq=tq: (lambda t: t.matmul(
                        pg[:, :], c.sel_bf[:, e * 128:(e + 1) * 128],
                        mT[0:NE, tq * TB:(tq + 1) * TB], start=True, stop=True)))(),
                        after=[mT_done, pg_free, mbc_free], inc=sGT)
                    p.do("vector", (lambda tq=tq: (lambda v: v.tensor_copy(
                        mbc[:, tq * TB:(tq + 1) * TB], pg[:, :])))(),
                        after=[sGT.mark()], inc=sGE)
                    pg_free = sGE.mark()
                mbc_done = sGE.mark()

                for q in range(4):
                    kk = e * 4 + q
                    buf = wa if kk % 2 == 0 else wb
                    wready = (sWT, wt_ready[kk])
                    hfree = {0: hb_free, 1: hb_free}

                    for hc in range(NCH):
                        for tq in range(4):
                            g = hc * 4 + tq
                            bank = [ph0, ph1][g % 2]
                            for dc in range(NCH):
                                p.do("tensor", (lambda dc=dc, hc=hc, tq=tq, bank=bank, buf=buf: (lambda t: t.matmul(
                                    bank[:, :],
                                    buf[:, dc * 1024 + hc * 128: dc * 1024 + (hc + 1) * 128],
                                    xh[:, dc * HALF + tq * TB: dc * HALF + (tq + 1) * TB],
                                    start=(dc == 0), stop=(dc == NCH - 1))))(),
                                    after=[wready, xh_done, hfree[g % 2]],
                                    inc=(sHP if dc == NCH - 1 else None))
                            mm_done = sHP.mark()
                            bias_ap = b1T[:, (q * 8 + hc) * NE + e:(q * 8 + hc) * NE + e + 1]
                            p.do("scalar", (lambda hc=hc, tq=tq, bank=bank, bias_ap=bias_ap: (lambda s: s.activation(
                                hb[:, hc * HALF + tq * TB: hc * HALF + (tq + 1) * TB],
                                bank[:, :], AF.Silu, bias=bias_ap)))(),
                                after=[mm_done, b1T_ready], inc=sHS)
                            hfree[g % 2] = sHS.mark()
                            p.do("vector", (lambda hc=hc, tq=tq: (lambda v: v.tensor_mul(
                                hb[:, hc * HALF + tq * TB: hc * HALF + (tq + 1) * TB],
                                hb[:, hc * HALF + tq * TB: hc * HALF + (tq + 1) * TB],
                                mbc[:, tq * TB:(tq + 1) * TB])))(),
                                after=[sHS.mark(), mbc_done], inc=sHM)
                    hm_all = sHM.mark()

                    ye_base = sYE.count
                    for oc in range(NCH):
                        for tq in range(4):
                            g = oc * 4 + tq
                            bank = [py0, py1][g % 2]
                            bank_guard = (sYE, ye_base + g - 1)
                            is_first_phase = (e == 0 and q == 0)
                            for hl in range(NCH):
                                p.do("tensor", (lambda hl=hl, oc=oc, tq=tq, bank=bank, buf=buf, ifp=is_first_phase: (lambda t: t.matmul(
                                    bank[:, :],
                                    buf[:, 8192 + hl * 1024 + oc * 128: 8192 + hl * 1024 + (oc + 1) * 128],
                                    hb[:, hl * HALF + tq * TB: hl * HALF + (tq + 1) * TB],
                                    start=(hl == 0),
                                    stop=(hl == NCH - 1 and not ifp))))(),
                                    after=[hm_all, wready, bank_guard],
                                    inc=(sYP if (hl == NCH - 1 and not is_first_phase) else None))
                            if is_first_phase:
                                p.do("tensor", (lambda oc=oc, tq=tq, bank=bank: (lambda t: t.matmul(
                                    bank[:, :], b2sb[:, oc * 128:(oc + 1) * 128],
                                    mT[0:NE, tq * TB:(tq + 1) * TB],
                                    start=False, stop=True)))(),
                                    after=[mT_done, ld1], inc=sYP)
                            p.do("vector", (lambda oc=oc, tq=tq, bank=bank: (lambda v: v.tensor_add(
                                yb[:, oc * HALF + tq * TB: oc * HALF + (tq + 1) * TB],
                                bank[:, :],
                                yb[:, oc * HALF + tq * TB: oc * HALF + (tq + 1) * TB])))(),
                                after=[sYP.mark(), y_clear], inc=sYE)
                    hb_free = sYP.mark()
                    if kk + 2 < 32:
                        wt_ready[kk + 2] = load_phase(kk + 2, [sYP.mark()])
                    elif half == 0:
                        wt_ready[kk + 2 - 32] = load_phase(kk + 2 - 32, [sYP.mark()])
            xh_free = sHP.mark()
            mT_free = sYP.mark()
            mbc_free = sHM.mark()

            y_done = sYE.mark()
            for tbl in range(4):
                tb = half * 4 + tbl
                if not final:
                    def add_src(jc, dst, xr, tbl=tbl):
                        def f(v):
                            return v.tensor_add(
                                dst,
                                yb[:, jc * HALF + tbl * TB: jc * HALF + (tbl + 1) * TB],
                                xr)
                        return f
                    emit_epilogue_tb(p, io, c, A, pe0, pe1, tb, add_src,
                                     src_after=[y_done])
                else:
                    def add_src4(jc, t4, dst, xr, tbl=tbl):
                        def f(v):
                            return v.tensor_add(
                                dst,
                                yb[:, jc * HALF + tbl * TB + t4 * 128:
                                   jc * HALF + tbl * TB + (t4 + 1) * 128],
                                xr)
                        return f
                    emit_final_tb(p, io, c, A, pe0, pe1, tb, add_src4,
                                  src_after=[y_done])
            yb_free = A.prev_done
        p.fence()
        p.flush("l0ffn")



def emit_wprep(p, io, c, jobs, name):
    nc = p.nc
    sLD = p.sem("sLD")
    sPT = p.sem("sPT")
    sPV = p.sem("sPV")
    sST = p.sem("sST")
    with (
        nc.sbuf_tensor(_nm("wp_in"), [128, 2 * DM], BF16) as win,
        nc.sbuf_tensor(_nm("wp_out"), [128, DM], BF16) as wout,
        nc.psum_tensor(_nm(f"wp_ptb"), [128, 1024], BF16) as ptb,
    ):
        slot_free = [NONE_MARK, NONE_MARK]
        ptb_free = NONE_MARK
        wout_free = NONE_MARK
        k = 0
        for (srcm, dstm) in jobs:
            for rcq in range(NCH):
                sl = k % 2
                p.dma("gpsimd", win[:, sl * DM:(sl + 1) * DM],
                      srcm[rcq * 128:(rcq + 1) * 128, :],
                      after=[slot_free[sl]], inc=sLD)
                ld = sLD.mark()
                for cc in range(NCH):
                    p.do("tensor", (lambda cc=cc, sl=sl: (lambda t: t.transpose(
                        ptb[:, cc * 128:(cc + 1) * 128],
                        win[:, sl * DM + cc * 128: sl * DM + (cc + 1) * 128],
                        c.ident_bf[:, :])))(),
                        after=[ld, ptb_free, c.ready],
                        inc=(sPT if cc == NCH - 1 else None))
                slot_free[sl] = sPT.mark()
                p.do("vector", lambda v: v.tensor_copy(wout[:, :], ptb[:, :]),
                     after=[sPT.mark(), wout_free], inc=sPV)
                ptb_free = sPV.mark()
                dst = dstm[:, :, rcq * 128:(rcq + 1) * 128]
                dst = dst.rearrange("cc p x -> p cc x")
                src = wout.rearrange("p (cc x) -> p cc x", cc=NCH)
                p.dma("sync", dst, src, after=[sPV.mark()], inc=sST)
                wout_free = sST.mark()
                k += 1
    p.fence()
    p.flush(name)


def emit_l1prep(p, io, c):
    jobs = [(io["L1_ffn_expert_w"][e], io["W1T_l1"][e]) for e in range(NE)]
    emit_wprep(p, io, c, jobs, "l1prep")


def emit_l0aprep(p, io, c):
    jobs = [(io["L0_attn_" + w], io["WT_l0a"][m])
            for m, w in enumerate(("Wq", "Wk", "Wv", "Wo"))]
    emit_wprep(p, io, c, jobs, "l0aprep")



SQI128 = 1.0 / math.sqrt(128.0)


def emit_l0attn(p, io, c, final=False):
    nc = p.nc
    sXL = p.sem("sXL")
    sWT = p.sem("sWT")
    sMM = p.sem("sMM")
    sEV = p.sem("sEV")
    sAC = p.sem("sAC")
    sOT = p.sem("sOT")
    sLD = p.sem("sLD")

    from contextlib import ExitStack
    with ExitStack() as ctx:
        xh = ctx.enter_context(nc.sbuf_tensor(_nm("a_xh"), [128, NCH * NT], BF16))
        kvs = ctx.enter_context(nc.sbuf_tensor(_nm("a_kvs"), [128, 16 * 128], BF16))
        bqkv = ctx.enter_context(nc.sbuf_tensor(_nm("a_bqkv"), [128, 4 * NCH], F32))
        pA = ctx.enter_context(nc.psum_tensor(_nm(f"a_pA"), [128, 512], F32))
        pB = ctx.enter_context(nc.psum_tensor(_nm(f"a_pB"), [128, 512], F32))
        pk0 = ctx.enter_context(nc.psum_tensor(_nm(f"a_pk0"), [128, 512], F32))
        pk1 = ctx.enter_context(nc.psum_tensor(_nm(f"a_pk1"), [128, 512], F32))
        ptb = ctx.enter_context(nc.psum_tensor(_nm(f"a_ptb"), [128, 1024], BF16))
        pe0 = ctx.enter_context(nc.psum_tensor(_nm(f"a_pe0"), [128, 512], F32))
        pe1 = ctx.enter_context(nc.psum_tensor(_nm(f"a_pe1"), [128, 512], F32))
        A = alloc_epi(nc, ctx, final=final)
        ctx12 = ctx.enter_context(ExitStack())
        wpA = ctx12.enter_context(nc.sbuf_tensor(_nm("a_wpA"), [128, 3072], BF16))
        wpB = ctx12.enter_context(nc.sbuf_tensor(_nm("a_wpB"), [128, 3072], BF16))
        kT2 = ctx12.enter_context(nc.sbuf_tensor(_nm("a_kT2"), [128, NT], BF16))
        vT2 = ctx12.enter_context(nc.sbuf_tensor(_nm("a_vT2"), [128, NT], BF16))
        qT2 = kT2
        ktm = ctx12.enter_context(nc.sbuf_tensor(_nm("a_ktm"), [128, 32 * 512], BF16))
        vtm = ctx12.enter_context(nc.sbuf_tensor(_nm("a_vtm"), [128, 32 * 128], BF16))
        kvt = ctx12.enter_context(nc.sbuf_tensor(_nm("a_kvt"), [128, 512], BF16))
        qpT = ctx12.enter_context(nc.sbuf_tensor(_nm("a_qpT"), [128, 4 * NT], BF16))
        csA = ctx12.enter_context(nc.sbuf_tensor(_nm("a_csA"), [128, 512], F32))
        csB = ctx12.enter_context(nc.sbuf_tensor(_nm("a_csB"), [128, 512], F32))
        otst = ctx12.enter_context(nc.sbuf_tensor(_nm("a_otst"), [128, 2 * 512], BF16))


        for cc in range(NCH):
            p.dma("sync", xh[:, cc * NT:(cc + 1) * NT], io["XLN"][cc, :, :],
                  inc=sXL)
        xh_done = sXL.mark()
        for m, w in enumerate(("bq", "bk", "bv", "bo")):
            for cc in range(NCH):
                p.dma("gpsimd", bqkv[:, m * NCH + cc:m * NCH + cc + 1],
                      io["L0_attn_" + w].rearrange("(cc pp) -> cc pp", pp=128)
                      [cc].rearrange("(pp one) -> pp one", one=1), inc=sLD)
        bias_done = sLD.mark()

        def load_pair(hp, buf, after, mats=(0, 1, 2)):
            for i, m in enumerate(mats):
                src = io["WT_l0a"][m][:, :, hp * 128:(hp + 1) * 128]
                src = src.rearrange("cc p x -> p cc x")
                dst = buf[:, i * 1024:(i + 1) * 1024]
                dst = dst.rearrange("p (cc x) -> p cc x", cc=NCH)
                p.dma("gpsimd", dst, src, after=after, inc=sWT)
            return sWT.count

        wt_ready = {}
        wt_ready[0] = load_pair(0, wpA, [])
        wt_ready[1] = load_pair(1, wpB, [])

        pair_mm_end = {}
        ev_free = {"pA": NONE_MARK, "pB": NONE_MARK, "pk0": NONE_MARK,
                   "pk1": NONE_MARK, "ptb": NONE_MARK}
        ktm_free = NONE_MARK
        vtm_free = NONE_MARK
        kT2_free = NONE_MARK
        vT2_free = NONE_MARK
        kvt_free = NONE_MARK

        for hp in range(NHEADS // 2):
            buf = wpA if hp % 2 == 0 else wpB
            wready = (sWT, wt_ready[hp])

            for mi, (dstb, bcol, dfree) in enumerate(
                    ((kT2, 1, kT2_free), (vT2, 2, vT2_free))):
                for tbq in range(NTB):
                    bank = [pA, pB][tbq % 2]
                    bkey = "pA" if tbq % 2 == 0 else "pB"
                    for cc in range(NCH):
                        p.do("tensor", (lambda cc=cc, tbq=tbq, bank=bank, buf=buf, mi=mi: (lambda t: t.matmul(
                            bank[:, :],
                            buf[:, (mi + 1) * 1024 + cc * 128:(mi + 1) * 1024 + (cc + 1) * 128],
                            xh[:, cc * NT + tbq * TB: cc * NT + (tbq + 1) * TB],
                            start=(cc == 0), stop=(cc == NCH - 1))))(),
                            after=[wready, xh_done, ev_free[bkey], dfree],
                            inc=(sMM if cc == NCH - 1 else None))
                        if cc == 0:
                            dfree = None
                    p.do("vector", (lambda tbq=tbq, bank=bank, dstb=dstb, bcol=bcol, hp=hp: (lambda v: v.tensor_scalar_add(
                        dstb[:, tbq * TB:(tbq + 1) * TB], bank[:, :],
                        bqkv[:, bcol * NCH + hp:bcol * NCH + hp + 1])))(),
                        after=[sMM.mark(), bias_done], inc=sEV)
                    ev_free[bkey] = sEV.mark()
            kv_proj_done = sEV.mark()

            for tt in range(32):
                for h in range(2):
                    g = tt * 2 + h
                    bank = [pk0, pk1][g % 2]
                    bkey = "pk0" if g % 2 == 0 else "pk1"
                    base = h * 64
                    p.do("tensor", (lambda tt=tt, h=h, bank=bank, base=base: (lambda t: t.matmul(
                        bank[0:128, 0:128],
                        kT2[base:base + 64, tt * 128:(tt + 1) * 128],
                        c.projpair[base:base + 64, h * 128:(h + 1) * 128],
                        start=True, stop=True)))(),
                        after=[kv_proj_done, ev_free[bkey], c.proj_ready,
                               ktm_free], inc=sMM)
                    mmm = sMM.mark()
                    p.do("scalar", (lambda bank=bank: (lambda s: s.activation(
                        csA[:, 0:128], bank[0:128, 0:128], AF.Sin,
                        scale=0.125)))(), after=[mmm, sEV.mark()], inc=sAC)
                    p.do("scalar", (lambda bank=bank: (lambda s: s.activation(
                        csB[:, 0:128], bank[0:128, 0:128], AF.Sin,
                        scale=0.0625)))(), after=[sAC.mark()], inc=sAC)
                    ev_free[bkey] = sAC.mark()
                    p.do("scalar", lambda s: s.activation(
                        csB[:, 0:128], csB[:, 0:128], AF.Square),
                        after=[sAC.mark()], inc=sAC)
                    trig = sAC.mark()
                    koff = tt * 512 + h * 256
                    p.do("vector", (lambda koff=koff: (lambda v: v.tensor_scalar(
                        ktm[:, koff:koff + 128], csB[:, 0:128],
                        -2.0 * SQI128, SQI128, ALU.mult, ALU.add)))(),
                        after=[trig], inc=sEV)
                    p.do("vector", (lambda koff=koff: (lambda v: v.tensor_scalar_mul(
                        ktm[:, koff + 128:koff + 256], csA[:, 0:128],
                        SQI128)))(), after=[trig], inc=sEV)
            ktrig_done = sEV.mark()
            kT2_free = sMM.mark()

            for ttg in range(4):
                for tl in range(8):
                    tt = ttg * 8 + tl
                    p.do("tensor", (lambda tt=tt, tl=tl: (lambda t: t.transpose(
                        ptb[:, tl * 128:(tl + 1) * 128],
                        vT2[:, tt * 128:(tt + 1) * 128], c.ident_bf[:, :])))(),
                        after=[kv_proj_done, ev_free["ptb"], c.ready, vtm_free],
                        inc=(sMM if tl == 7 else None))
                p.do("vector", (lambda ttg=ttg: (lambda v: v.tensor_copy(
                    vtm[:, ttg * 1024:(ttg + 1) * 1024], ptb[:, :])))(),
                    after=[sMM.mark()], inc=sEV)
                ev_free["ptb"] = sEV.mark()
            vtr_done = sEV.mark()
            vT2_free = sMM.mark()

            for h in range(2):
                bank = [pk0, pk1][h]
                for tt in range(32):
                    p.do("tensor", (lambda tt=tt, h=h, bank=bank: (lambda t: t.matmul(
                        bank[0:64, 0:256],
                        vtm[:, tt * 128 + h * 64: tt * 128 + (h + 1) * 64],
                        ktm[:, tt * 512 + h * 256: tt * 512 + (h + 1) * 256],
                        start=(tt == 0), stop=(tt == 31))))(),
                        after=[ktrig_done, vtr_done,
                               (ev_free["pk0"] if h == 0 else ev_free["pk1"])],
                        inc=(sMM if tt == 31 else None))
            kv_done = sMM.mark()
            ktm_free = kv_done
            vtm_free = kv_done
            for h in range(2):
                bank = [pk0, pk1][h]
                p.do("vector", (lambda h=h, bank=bank: (lambda v: v.tensor_copy(
                    kvt[0:64, h * 256:(h + 1) * 256], bank[0:64, 0:256])))(),
                    after=[kv_done, kvt_free], inc=sEV)
                ev_free["pk0" if h == 0 else "pk1"] = sEV.mark()
            kvt_done = sEV.mark()
            for h in range(2):
                for mc in range(2):
                    p.do("tensor", (lambda h=h, mc=mc: (lambda t: t.transpose(
                        ptb[:, 512 + (h * 2 + mc) * 64: 512 + (h * 2 + mc + 1) * 64],
                        kvt[0:64, h * 256 + mc * 128: h * 256 + (mc + 1) * 128],
                        c.ident_bf[0:64, 0:64])))(),
                        after=[kvt_done, ev_free["ptb"], c.ready], inc=sMM)
            p.do("vector", (lambda hp=hp: (lambda v: v.tensor_copy(
                kvs[:, hp * 256:(hp + 1) * 256], ptb[:, 512:512 + 256])))(),
                after=[sMM.mark()], inc=sEV)
            ev_free["ptb"] = sEV.mark()
            kvt_free = sEV.mark()
            pair_mm_end[hp] = sMM.mark()
            if hp + 2 < 8:
                wt_ready[hp + 2] = load_pair(
                    hp + 2, wpA if hp % 2 == 0 else wpB, [pair_mm_end[hp]])
        kvs_done = sEV.mark()

        wt_ready2 = {}
        wt_ready2[0] = load_pair(0, wpA, [pair_mm_end[6]], mats=(0,))
        wt_ready2[1] = load_pair(1, wpB, [pair_mm_end[7]], mats=(0,))
        qT2_free = pair_mm_end[7]
        qpT_free = NONE_MARK
        otst_free = [NONE_MARK, NONE_MARK]
        pass2_end = {}
        for hp in range(8):
            buf = wpA if hp % 2 == 0 else wpB
            wready = (sWT, wt_ready2[hp])
            qfree = qT2_free
            for tbq in range(NTB):
                bank = [pA, pB][tbq % 2]
                bkey = "pA" if tbq % 2 == 0 else "pB"
                for cc in range(NCH):
                    p.do("tensor", (lambda cc=cc, tbq=tbq, bank=bank, buf=buf: (lambda t: t.matmul(
                        bank[:, :],
                        buf[:, cc * 128:(cc + 1) * 128],
                        xh[:, cc * NT + tbq * TB: cc * NT + (tbq + 1) * TB],
                        start=(cc == 0), stop=(cc == NCH - 1))))(),
                        after=[wready, xh_done, ev_free[bkey], qfree],
                        inc=(sMM if cc == NCH - 1 else None))
                    if cc == 0:
                        qfree = None
                p.do("vector", (lambda tbq=tbq, bank=bank, hp=hp: (lambda v: v.tensor_scalar_add(
                    qT2[:, tbq * TB:(tbq + 1) * TB], bank[:, :],
                    bqkv[:, 0 * NCH + hp:0 * NCH + hp + 1])))(),
                    after=[sMM.mark(), bias_done], inc=sEV)
                ev_free[bkey] = sEV.mark()
            q_proj_done = sEV.mark()

            for h in range(2):
                for tbq in range(NTB):
                    g = h * NTB + tbq
                    bank = [pk0, pk1][g % 2]
                    bkey = "pk0" if g % 2 == 0 else "pk1"
                    p.do("tensor", (lambda h=h, tbq=tbq, bank=bank: (lambda t: t.matmul(
                        bank[:, :],
                        c.projpair[:, h * 128:(h + 1) * 128],
                        qT2[:, tbq * TB:(tbq + 1) * TB],
                        start=True, stop=True)))(),
                        after=[q_proj_done, ev_free[bkey], c.proj_ready,
                               qpT_free], inc=sMM)
                    mmm = sMM.mark()
                    p.do("scalar", (lambda bank=bank: (lambda s: s.activation(
                        csA[:, :], bank[:, :], AF.Sin, scale=0.125)))(),
                        after=[mmm, sEV.mark()], inc=sAC)
                    p.do("scalar", (lambda bank=bank: (lambda s: s.activation(
                        csB[:, :], bank[:, :], AF.Sin, scale=0.0625)))(),
                        after=[sAC.mark()], inc=sAC)
                    ev_free[bkey] = sAC.mark()
                    p.do("scalar", lambda s: s.activation(
                        csB[:, :], csB[:, :], AF.Square),
                        after=[sAC.mark()], inc=sAC)
                    trig = sAC.mark()
                    co = (h * 2 + 0) * NT + tbq * TB
                    so = (h * 2 + 1) * NT + tbq * TB
                    p.do("vector", (lambda co=co: (lambda v: v.tensor_scalar(
                        qpT[:, co:co + TB], csB[:, :],
                        -2.0 * SQI128, SQI128, ALU.mult, ALU.add)))(),
                        after=[trig], inc=sEV)
                    p.do("vector", (lambda so=so: (lambda v: v.tensor_scalar_mul(
                        qpT[:, so:so + TB], csA[:, :], SQI128)))(),
                        after=[trig], inc=sEV)
            qtrig_done = sEV.mark()
            qT2_free = sMM.mark()

            for h in range(2):
                for tbq in range(NTB):
                    g = h * NTB + tbq
                    bank = [pk0, pk1][g % 2]
                    bkey = "pk0" if g % 2 == 0 else "pk1"
                    for mc in range(2):
                        p.do("tensor", (lambda h=h, tbq=tbq, mc=mc, hp=hp, bank=bank: (lambda t: t.matmul(
                            bank[0:64, 0:512],
                            kvs[:, hp * 256 + (h * 2 + mc) * 64:
                                hp * 256 + (h * 2 + mc + 1) * 64],
                            qpT[:, (h * 2 + mc) * NT + tbq * TB:
                                (h * 2 + mc) * NT + (tbq + 1) * TB],
                            start=(mc == 0), stop=(mc == 1))))(),
                            after=[qtrig_done, kvs_done, ev_free[bkey]],
                            inc=(sMM if mc == 1 else None))
                    mmm = sMM.mark()
                    sl = g % 2
                    p.do("vector", (lambda bank=bank, sl=sl: (lambda v: v.tensor_copy(
                        otst[0:64, sl * 512:(sl + 1) * 512],
                        bank[0:64, 0:512])))(),
                        after=[mmm, otst_free[sl]], inc=sEV)
                    ev_free[bkey] = sEV.mark()
                    p.dma("sync", io["OT"][hp, h * 64:(h + 1) * 64,
                                           tbq * TB:(tbq + 1) * TB],
                          otst[0:64, sl * 512:(sl + 1) * 512],
                          after=[sEV.mark()], inc=sOT)
                    otst_free[sl] = sOT.mark()
            qpT_free = sMM.mark()
            pass2_end[hp] = sMM.mark()
            if hp + 2 < 8:
                wt_ready2[hp + 2] = load_pair(
                    hp + 2, wpA if hp % 2 == 0 else wpB, [pass2_end[hp]],
                    mats=(0,))
        ot_all = sOT.mark()

        p.flush("l0attn_p12")
        ctx12.close()
        woT = ctx.enter_context(nc.sbuf_tensor(_nm("a_woT"), [128, NCH * 1024], BF16))
        otb = ctx.enter_context(nc.sbuf_tensor(_nm("a_otb"), [128, 2 * NCH * 512], BF16))
        ybuf = ctx.enter_context(nc.sbuf_tensor(_nm("a_ybuf"), [128, NCH * 512], BF16))
        src = io["WT_l0a"][3].rearrange("cc p x -> p cc x")
        p.dma("gpsimd", woT.rearrange("p (cc x) -> p cc x", cc=NCH), src,
              after=[pass2_end[7]], inc=sWT)
        wo_ready = (sWT, sWT.count)
        otb_free = [NONE_MARK, NONE_MARK]
        ybuf_free = NONE_MARK
        for tb in range(NTB):
            sl = tb % 2
            for cc in range(NCH):
                p.dma("sync", otb[:, sl * 4096 + cc * TB: sl * 4096 + (cc + 1) * TB],
                      io["OT"][cc, :, tb * TB:(tb + 1) * TB],
                      after=[otb_free[sl], ot_all], inc=sXL)
            ot_tb = sXL.mark()
            for jc in range(NCH):
                bank = [pA, pB][jc % 2]
                bkey = "pA" if jc % 2 == 0 else "pB"
                for cc in range(NCH):
                    p.do("tensor", (lambda jc=jc, cc=cc, sl=sl, bank=bank: (lambda t: t.matmul(
                        bank[:, :],
                        woT[:, cc * 1024 + jc * 128: cc * 1024 + (jc + 1) * 128],
                        otb[:, sl * 4096 + cc * TB: sl * 4096 + (cc + 1) * TB],
                        start=(cc == 0), stop=(cc == NCH - 1))))(),
                        after=[wo_ready, ot_tb, ev_free[bkey], ybuf_free],
                        inc=(sMM if cc == NCH - 1 else None))
                p.do("vector", (lambda jc=jc, bank=bank: (lambda v: v.tensor_scalar_add(
                    ybuf[:, jc * TB:(jc + 1) * TB], bank[:, :],
                    bqkv[:, 3 * NCH + jc:3 * NCH + jc + 1])))(),
                    after=[sMM.mark(), bias_done], inc=sEV)
                ev_free[bkey] = sEV.mark()
            ybuf_free = None
            y_done = sEV.mark()
            otb_free[sl] = sMM.mark()

            if not final:
                def add_src(jc, dst, xr):
                    def f(v):
                        return v.tensor_add(dst, ybuf[:, jc * TB:(jc + 1) * TB], xr)
                    return f
                m = emit_epilogue_tb(p, io, c, A, pe0, pe1, tb, add_src,
                                     src_after=[y_done])
            else:
                def add_src4(jc, t4, dst, xr):
                    def f(v):
                        return v.tensor_add(
                            dst,
                            ybuf[:, jc * TB + t4 * 128: jc * TB + (t4 + 1) * 128],
                            xr)
                    return f
                m = emit_final_tb(p, io, c, A, pe0, pe1, tb, add_src4,
                                  src_after=[y_done])
            ybuf_free = m
    p.fence()
    p.flush("l0attn")



def emit_l1attn(p, io, c, final=False):
    nc = p.nc
    sXL = p.sem("sXL")
    sWT = p.sem("sWT")
    sMM = p.sem("sMM")
    sEV = p.sem("sEV")
    sAC = p.sem("sAC")
    sOT = p.sem("sOT")
    sLD = p.sem("sLD")
    sGP = p.sem("sGP")

    from contextlib import ExitStack
    with ExitStack() as ctx:
        epk = ctx.enter_context(nc.sbuf_tensor(_nm("t_epk"), [128, 32], F32))
        enk = ctx.enter_context(nc.sbuf_tensor(_nm("t_enk"), [128, 32], F32))
        Eneg = ctx.enter_context(nc.sbuf_tensor(_nm("t_Eneg"), [128, NT], BF16))
        Epos = ctx.enter_context(nc.sbuf_tensor(_nm("t_Epos"), [128, NT], BF16))
        Ff = ctx.enter_context(nc.sbuf_tensor(_nm("t_Ff"), [128, 4 * 512], F32))
        pSw = [ctx.enter_context(nc.psum_tensor(_nm(f"t_pSw{i}"), [128, 1024], F32))
               for i in range(2)]
        po = [ctx.enter_context(nc.psum_tensor(_nm(f"t_po{i}"), [128, 512], F32))
              for i in range(4)]
        pS0, pS1 = pSw[0][:, 0:512], pSw[0][:, 512:1024]
        A = alloc_epi(nc, ctx, final=final)

        with nc.sbuf_tensor(_nm("t_iot"), [128, NT], F32) as iot:
            p.do("gpsimd", lambda g: g.iota(
                epk[:, :], [[128, 32]], base=0, channel_multiplier=1,
                allow_small_or_imprecise_dtypes=True), inc=sGP)
            m1 = sGP.mark()
            p.do("scalar", lambda s: s.activation(enk[:, :], epk[:, :], AF.Exp,
                                                  scale=-CDEC),
                 after=[m1], inc=sAC)
            p.do("scalar", lambda s: s.activation(epk[:, :], epk[:, :], AF.Exp,
                                                  scale=CDEC),
                 after=[m1, sAC.mark()], inc=sAC)
            p.do("gpsimd", lambda g: g.iota(
                iot[:, :], [[1, NT]], base=0, channel_multiplier=0,
                allow_small_or_imprecise_dtypes=True), inc=sGP)
            m2 = sGP.mark()
            p.do("scalar", lambda s: s.activation(Eneg[:, :], iot[:, :], AF.Exp,
                                                  scale=-CDEC),
                 after=[m2, sAC.mark()], inc=sAC)
            p.do("vector", lambda v: v.tensor_scalar_mul(Eneg[:, :], Eneg[:, :],
                                                         float(L1SCALE)),
                 after=[sAC.mark()], inc=sEV)
            p.do("scalar", lambda s: s.activation(Epos[:, :], iot[:, :], AF.Exp,
                                                  scale=CDEC),
                 after=[m2, sAC.mark()], inc=sAC)
            p.do("vector", lambda v: v.tensor_scalar_mul(Epos[:, :], Epos[:, :],
                                                         float(L1SCALE)),
                 after=[sAC.mark()], inc=sEV)
            for o in range(4):
                p.do("gpsimd", (lambda o=o: (lambda g: g.iota(
                    Ff[:, o * 512:(o + 1) * 512], [[-1, 512]], base=o * 128,
                    channel_multiplier=1,
                    allow_small_or_imprecise_dtypes=True)))(), inc=sGP)
            m3 = sGP.mark()
            p.do("scalar", lambda s: s.activation(Ff[:, :], Ff[:, :], AF.Relu),
                 after=[m3, sAC.mark()], inc=sAC)
            p.do("scalar", lambda s: s.activation(Ff[:, :], Ff[:, :], AF.Exp,
                                                  scale=-2.0 * CDEC),
                 after=[sAC.mark()], inc=sAC)
            tables_done_ac = sAC.mark()
            tables_done_ev = sEV.mark()

            with ExitStack() as ctxA:
                xh = ctxA.enter_context(nc.sbuf_tensor(_nm("t_xh"), [128, NCH * NT], BF16))
                stg = ctxA.enter_context(nc.sbuf_tensor(_nm("t_stg"), [128, NT], BF16))
                for cc in range(NCH):
                    p.dma("sync", xh[:, cc * NT:(cc + 1) * NT], io["XLN"][cc, :, :],
                          inc=sXL)
                xh_done = sXL.mark()

                with nc.sbuf_tensor(_nm("t_wq"), [128, 8192], BF16) as wq:
                    p.dma("gpsimd", wq.rearrange("p (cc j) -> p cc j", cc=NCH),
                          io["L1_attn_Wq"].rearrange("(cc pp) j -> pp cc j", pp=128),
                          inc=sWT)
                    wq_rdy = (sWT, sWT.count)
                    stg_free = NONE_MARK
                    for jc in range(NCH):
                        for tbq in range(NTB):
                            bank = [pS0, pS1][tbq % 2]
                            for cc in range(NCH):
                                p.do("tensor", (lambda jc=jc, tbq=tbq, bank=bank, wq=wq, cc=cc: (lambda t: t.matmul(
                                    bank[:, :],
                                    wq[:, cc * 1024 + jc * 128: cc * 1024 + (jc + 1) * 128],
                                    xh[:, cc * NT + tbq * TB: cc * NT + (tbq + 1) * TB],
                                    start=(cc == 0), stop=(cc == NCH - 1))))(),
                                    after=[wq_rdy, xh_done, (sEV, sEV.count - 1)],
                                    inc=(sMM if cc == NCH - 1 else None))
                            p.do("vector", (lambda tbq=tbq, bank=bank: (lambda v: v.tensor_copy(
                                stg[:, tbq * TB:(tbq + 1) * TB], bank[:, :])))(),
                                after=[sMM.mark(), stg_free], inc=sEV)
                        p.dma("sync", io["QTd"][jc, :, :], stg[:, :],
                              after=[sEV.mark()], inc=sOT)
                        stg_free = sOT.mark()
                    qt_mm_end = sMM.mark()

                with (
                    nc.sbuf_tensor(_nm("t_wk"), [128, 8192], BF16) as wk,
                    nc.sbuf_tensor(_nm("t_wv"), [128, 8192], BF16) as wv,
                ):
                    p.dma("gpsimd", wk.rearrange("p (cc j) -> p cc j", cc=NCH),
                          io["L1_attn_Wk"].rearrange("(cc pp) j -> pp cc j", pp=128),
                          after=[qt_mm_end], inc=sWT)
                    p.dma("gpsimd", wv.rearrange("p (cc j) -> p cc j", cc=NCH),
                          io["L1_attn_Wv"].rearrange("(cc pp) j -> pp cc j", pp=128),
                          inc=sWT)
                    wkv_rdy = (sWT, sWT.count)
                    for jc in range(NCH):
                        for tbq in range(NTB):
                            bank = [pS0, pS1][tbq % 2]
                            for i in range(16):
                                wbuf = wk if i < 8 else wv
                                cc = i % 8
                                p.do("tensor", (lambda jc=jc, tbq=tbq, bank=bank, wbuf=wbuf, cc=cc, i=i: (lambda t: t.matmul(
                                    bank[:, :],
                                    wbuf[:, cc * 1024 + jc * 128: cc * 1024 + (jc + 1) * 128],
                                    xh[:, cc * NT + tbq * TB: cc * NT + (tbq + 1) * TB],
                                    start=(i == 0), stop=(i == 15))))(),
                                    after=[wkv_rdy, xh_done, (sEV, sEV.count - 1)],
                                    inc=(sMM if i == 15 else None))
                            p.do("vector", (lambda tbq=tbq, bank=bank: (lambda v: v.tensor_copy(
                                stg[:, tbq * TB:(tbq + 1) * TB], bank[:, :])))(),
                                after=[sMM.mark(), stg_free], inc=sEV)
                        p.dma("sync", io["KVSd"][jc, :, :], stg[:, :],
                              after=[sEV.mark()], inc=sOT)
                        stg_free = sOT.mark()
                    kvs_mm_end = sMM.mark()
                    kvs_st = sOT.mark()

                latT = ctxA.enter_context(nc.sbuf_tensor(_nm("t_latT"), [128, 2 * NT], BF16))
                kvstb = ctxA.enter_context(nc.sbuf_tensor(_nm("t_kvstb"), [128, 2 * 4096], BF16))
                kvstb_free = [NONE_MARK, NONE_MARK]
                with nc.sbuf_tensor(_nm("t_wd"), [128, 2048], BF16) as wd:
                    p.dma("gpsimd", wd.rearrange("p (cc j) -> p cc j", cc=NCH),
                          io["L1_attn_W_down_kv"].rearrange("(cc pp) j -> pp cc j", pp=128),
                          after=[kvs_mm_end], inc=sWT)
                    wd_rdy = (sWT, sWT.count)
                    for tbq in range(NTB):
                        sl = tbq % 2
                        for cc in range(NCH):
                            p.dma("sync", kvstb[:, sl * 4096 + cc * TB: sl * 4096 + (cc + 1) * TB],
                                  io["KVSd"][cc, :, tbq * TB:(tbq + 1) * TB],
                                  after=[kvstb_free[sl], kvs_st], inc=sXL)
                        kv_tb = sXL.mark()
                        for jc in range(2):
                            bank = [pS0, pS1][jc]
                            for cc in range(NCH):
                                p.do("tensor", (lambda jc=jc, tbq=tbq, sl=sl, bank=bank, wd=wd, cc=cc: (lambda t: t.matmul(
                                    bank[:, :],
                                    wd[:, cc * 256 + jc * 128: cc * 256 + (jc + 1) * 128],
                                    kvstb[:, sl * 4096 + cc * TB: sl * 4096 + (cc + 1) * TB],
                                    start=(cc == 0), stop=(cc == NCH - 1))))(),
                                    after=[wd_rdy, kv_tb, (sEV, sEV.count - 2)],
                                    inc=(sMM if cc == NCH - 1 else None))
                            p.do("vector", (lambda jc=jc, tbq=tbq, bank=bank: (lambda v: v.tensor_copy(
                                latT[:, jc * NT + tbq * TB: jc * NT + (tbq + 1) * TB],
                                bank[:, :])))(),
                                after=[sMM.mark()], inc=sEV)
                        kvstb_free[sl] = sMM.mark()
                    lat_done = sEV.mark()
                    lat_mm_end = sMM.mark()

                with (
                    nc.sbuf_tensor(_nm("t_wuk"), [128, 2 * 1024], BF16) as wuk,
                    nc.sbuf_tensor(_nm("t_wuv"), [128, 2 * 1024], BF16) as wuv,
                ):
                    p.dma("gpsimd", wuk.rearrange("p (cc j) -> p cc j", cc=2),
                          io["L1_attn_W_up_k"].rearrange("(cc pp) j -> pp cc j", pp=128),
                          after=[lat_mm_end], inc=sWT)
                    p.dma("gpsimd", wuv.rearrange("p (cc j) -> p cc j", cc=2),
                          io["L1_attn_W_up_v"].rearrange("(cc pp) j -> pp cc j", pp=128),
                          inc=sWT)
                    wu_rdy = (sWT, sWT.count)
                    for wub, dstd in ((wuk, io["KRd"]), (wuv, io["VRd"])):
                        for jc in range(NCH):
                            for tbq in range(NTB):
                                bank = [pS0, pS1][tbq % 2]
                                for cc in range(2):
                                    p.do("tensor", (lambda jc=jc, tbq=tbq, bank=bank, wub=wub, cc=cc: (lambda t: t.matmul(
                                        bank[:, :],
                                        wub[:, cc * 1024 + jc * 128: cc * 1024 + (jc + 1) * 128],
                                        latT[:, cc * NT + tbq * TB: cc * NT + (tbq + 1) * TB],
                                        start=(cc == 0), stop=(cc == 1))))(),
                                        after=[wu_rdy, lat_done, (sEV, sEV.count - 1)],
                                        inc=(sMM if cc == 1 else None))
                                p.do("vector", (lambda tbq=tbq, bank=bank: (lambda v: v.tensor_copy(
                                    stg[:, tbq * TB:(tbq + 1) * TB], bank[:, :])))(),
                                    after=[sMM.mark(), stg_free], inc=sEV)
                            p.dma("sync", dstd[jc, :, :], stg[:, :],
                                  after=[sEV.mark()], inc=sOT)
                            stg_free = sOT.mark()
                    proj_all_mm = sMM.mark()
                    proj_all_st = sOT.mark()
                p.flush("l1attn_proj")

            with ExitStack() as ctxB:
                vtmall = ctxB.enter_context(nc.sbuf_tensor(_nm("t_vtmall"), [128, NHEADS * 32 * 65], BF16))

                ctxV = ctxB.enter_context(ExitStack())
                vrh2 = [ctxV.enter_context(nc.sbuf_tensor(_nm("t_vrh"), [64, NT], F32)) for _ in range(2)]
                vt_free = [NONE_MARK, NONE_MARK]
                pS_free2 = [NONE_MARK, NONE_MARK]
                for h in range(NHEADS):
                    par = h % 2
                    hc, hr = h // 2, (h % 2) * 64
                    p.dma("gpsimd", vrh2[par][0:64, :], io["VRd"][hc, hr:hr + 64, :],
                          after=[vt_free[par], proj_all_st], inc=sXL)
                    vld = sXL.mark()
                    for g2 in range(2):
                        for tl in range(16):
                            kc = g2 * 16 + tl
                            p.do("tensor", (lambda kc=kc, tl=tl, par=par, g2=g2: (lambda t: t.transpose(
                                pSw[g2][:, tl * 64:(tl + 1) * 64],
                                vrh2[par][0:64, kc * 128:(kc + 1) * 128],
                                c.ident_f32[0:64, 0:64])))(),
                                after=[vld, c.ready, pS_free2[g2 % 2]],
                                inc=(sMM if tl == 15 else None))
                        vv = vtmall.rearrange("p (hh kc x) -> p hh kc x", hh=NHEADS, x=65)
                        p.do("vector", (lambda h=h, g2=g2, vv=vv: (lambda v: v.tensor_copy(
                            vv[:, h, g2 * 16:(g2 + 1) * 16, 0:64],
                            pSw[g2 % 2].rearrange("p (tl x) -> p tl x", x=64)[:, :, :])))(),
                            after=[sMM.mark()], inc=sEV)
                        pS_free2[g2 % 2] = sEV.mark()
                    vt_free[par] = sMM.mark()
                vvall = vtmall.rearrange("p (hh kc x) -> p hh kc x", hh=NHEADS, x=65)
                p.do("gpsimd", lambda g: g.memset(vvall[:, :, :, 64:65], 1.0),
                     after=[sEV.mark()], inc=sGP)
                vph_done = sGP.mark()
                if "DBGV" in io:
                    p.dma("sync", io["DBGV"][:, :], vtmall[:, :],
                          after=[vph_done], inc=sOT)
                p.flush("l1attn_vpre")
                ctxV.close()
                krh2 = [ctxB.enter_context(nc.sbuf_tensor(_nm("t_krh"), [128, NT], BF16)) for _ in range(2)]
                qh1 = ctxB.enter_context(nc.sbuf_tensor(_nm("t_qh"), [128, NT], BF16))
                qmin2 = [ctxB.enter_context(nc.sbuf_tensor(_nm("t_qmin"), [128, NT], BF16)) for _ in range(2)]
                qpl2 = [ctxB.enter_context(nc.sbuf_tensor(_nm("t_qpl"), [128, NT], BF16)) for _ in range(2)]
                ptile = ctxB.enter_context(nc.sbuf_tensor(_nm("t_ptile"), [128, 2 * 1024], BF16))
                oth = ctxB.enter_context(nc.sbuf_tensor(_nm("t_oth"), [128, NT], BF16))
                rrow = ctxB.enter_context(nc.sbuf_tensor(_nm("t_rrow"), [128, 512], F32))
                otmp = ctxB.enter_context(nc.sbuf_tensor(_nm("t_otmp"), [64, 512], F32))

                h_in_free = [NONE_MARK, NONE_MARK]
                q_free = [NONE_MARK, NONE_MARK]
                oth_free = NONE_MARK
                pt_free = [NONE_MARK] * 2
                pSw_free = [pS_free2[0], pS_free2[1]]
                po_free = [NONE_MARK] * 4
                pn_free = NONE_MARK
                otmp_free = NONE_MARK
                qh_free = [NONE_MARK]
                h_dma = {}
                hq_done = {}

                def head_loads(hc):
                    par = hc % 2
                    p.dma("sync", krh2[par][:, :], io["KRd"][hc, :, :],
                          after=[h_in_free[par], proj_all_st], inc=sXL)
                    p.dma("sync", qh1[:, :], io["QTd"][hc, :, :],
                          after=[h_in_free[par], qh_free[0]], inc=sXL)
                    h_dma[hc] = sXL.mark()
                    p.do("vector", (lambda par=par: (lambda v: v.tensor_mul(
                        qmin2[par][:, :], qh1[:, :], Eneg[:, :])))(),
                        after=[h_dma[hc], q_free[par], tables_done_ev], inc=sEV)
                    p.do("vector", (lambda par=par: (lambda v: v.tensor_mul(
                        qpl2[par][:, :], qh1[:, :], Epos[:, :])))(),
                        after=[h_dma[hc], q_free[par]], inc=sEV)
                    hq_done[hc] = sEV.mark()
                    qh_free[0] = sEV.mark()

                head_loads(0)
                head_loads(1)

                def tile_kind(kc, qb):
                    qlo, qhi = qb * 512, qb * 512 + 511
                    klo, khi = kc * 128, kc * 128 + 127
                    if qlo > khi:
                        return "up"
                    if qhi < klo:
                        return "lo"
                    return "st"

                for hc in range(NCH):
                    par = hc % 2
                    krh = krh2[par]
                    qmin, qpl = qmin2[par], qpl2[par]
                    h_in = h_dma[hc]
                    q_done = hq_done[hc]
                    vbA = (2 * hc) * 32 * 65
                    vbB = (2 * hc + 1) * 32 * 65
                    exp_mark = {}

                    for qb in range(8):
                        pop = [po[(qb % 2) * 2], po[(qb % 2) * 2 + 1]]
                        kind_of = [tile_kind(kc, qb) for kc in range(32)]

                        def emit_S(kc, qb=qb, kind_of=kind_of, krh=krh,
                                   qmin=qmin, qpl=qpl, h_in=h_in, q_done=q_done):
                            w = pSw[kc % 2]
                            kind = kind_of[kc]
                            rhs = qmin if kind in ("up", "st") else qpl
                            marks = []
                            for hh in range(2):
                                p.do("tensor", (lambda kc=kc, qb=qb, hh=hh, rhs=rhs, krh=krh, w=w: (lambda t: t.matmul(
                                    w[:, hh * 512:(hh + 1) * 512],
                                    krh[hh * 64:(hh + 1) * 64, kc * 128:(kc + 1) * 128],
                                    rhs[hh * 64:(hh + 1) * 64, qb * TB:(qb + 1) * TB],
                                    start=True, stop=True)))(),
                                    after=[h_in, q_done, pSw_free[kc % 2]],
                                    inc=sMM)
                                marks.append(sMM.mark())
                            return marks

                        def emit_exp(kc, marks, qb=qb, kind_of=kind_of):
                            w = pSw[kc % 2]
                            kind = kind_of[kc]
                            if kind == "st":
                                for hh in range(2):
                                    p.do("vector", (lambda hh=hh, kc=kc, w=w: (lambda v: v.tensor_mul(
                                        w[:, hh * 512:(hh + 1) * 512],
                                        w[:, hh * 512:(hh + 1) * 512],
                                        Ff[:, (kc % 4) * 512:(kc % 4 + 1) * 512])))(),
                                        after=[marks[hh], tables_done_ac], inc=sEV)
                                    marks[hh] = sEV.mark()
                            slot = kc % 2
                            scol = epk if kind in ("up", "st") else enk
                            p.do("scalar", (lambda kc=kc, slot=slot, scol=scol, w=w: (lambda s: s.activation(
                                ptile[:, slot * 1024:(slot + 1) * 1024],
                                w[:, :], AF.Exp, scale=scol[:, kc:kc + 1])))(),
                                after=marks + [pt_free[slot], tables_done_ac],
                                inc=sAC)
                            pSw_free[kc % 2] = sAC.mark()
                            exp_mark[kc] = sAC.mark()

                        def emit_PV(kc, qb=qb, pop=pop, vbA=vbA, vbB=vbB):
                            slot = kc % 2
                            for hh, vb in ((0, vbA), (1, vbB)):
                                p.do("tensor", (lambda kc=kc, hh=hh, slot=slot, pop=pop, vb=vb: (lambda t: t.matmul(
                                    pop[hh][0:65, :],
                                    vtmall[:, vb + kc * 65: vb + (kc + 1) * 65],
                                    ptile[:, slot * 1024 + hh * 512: slot * 1024 + (hh + 1) * 512],
                                    start=(kc == 0), stop=(kc == 31))))(),
                                    after=[exp_mark[kc], vph_done,
                                           (po_free[(qb % 2) * 2 + hh] if kc == 0 else None)],
                                    inc=sMM)
                                if hh == 1:
                                    pt_free[slot] = sMM.mark()

                        mk = emit_S(0)
                        emit_exp(0, mk)
                        for kc in range(32):
                            if kc + 1 < 32:
                                mk = emit_S(kc + 1)
                                emit_exp(kc + 1, mk)
                            emit_PV(kc)
                        pv_done = sMM.mark()
                        for hh in range(2):
                            bi = (qb % 2) * 2 + hh
                            p.do("scalar", (lambda bi=bi: (lambda s: s.activation(
                                rrow[64:65, 0:512], po[bi][64:65, :], AF.Ln)))(),
                                after=[pv_done, pn_free], inc=sAC)
                            p.do("scalar", (lambda: (lambda s: s.activation(
                                rrow[64:65, 0:512], rrow[64:65, 0:512], AF.Exp,
                                scale=-1.0)))(),
                                after=[sAC.mark()], inc=sAC)
                            rr_done = sAC.mark()
                            p.do("vector", (lambda bi=bi: (lambda v: v.tensor_copy(
                                otmp[0:64, :], po[bi][0:64, :])))(),
                                after=[pv_done, otmp_free], inc=sEV)
                            ev_cp = sEV.mark()
                            p.do("tensor", (lambda bi=bi: (lambda t: t.matmul(
                                po[bi][0:64, :], c.ones_blk[64:65, 0:64],
                                rrow[64:65, 0:512], start=True, stop=True)))(),
                                after=[rr_done, ev_cp], inc=sMM)
                            p.do("vector", (lambda qb=qb, bi=bi, hh=hh: (lambda v: v.tensor_mul(
                                oth[hh * 64:(hh + 1) * 64, qb * TB:(qb + 1) * TB],
                                otmp[0:64, :],
                                po[bi][0:64, :])))(),
                                after=[sMM.mark(), oth_free], inc=sEV)
                            po_free[bi] = sEV.mark()
                            otmp_free = sEV.mark()
                            pn_free = sEV.mark()
                    oth_done = sEV.mark()
                    p.dma("sync", io["OT"][hc, :, :], oth[:, :],
                          after=[oth_done], inc=sOT)
                    oth_free = sOT.mark()
                    h_in_free[par] = sMM.mark()
                    q_free[par] = sMM.mark()
                    if hc + 2 < NCH:
                        head_loads(hc + 2)
                ot_all = sOT.mark()
                p.flush("l1attn_heads")

            with ExitStack() as ctxC:
                woB = ctxC.enter_context(nc.sbuf_tensor(_nm("t_woB"), [128, 8192], BF16))
                otb = ctxC.enter_context(nc.sbuf_tensor(_nm("t_otb"), [128, 2 * 4096], BF16))
                ybuf = ctxC.enter_context(nc.sbuf_tensor(_nm("t_ybuf"), [128, 4096], BF16))
                p.dma("gpsimd", woB.rearrange("p (cc j) -> p cc j", cc=NCH),
                      io["L1_attn_Wo"].rearrange("(cc pp) j -> pp cc j", pp=128),
                      inc=sWT)
                wo_rdy = (sWT, sWT.count)
                otb_free = [NONE_MARK, NONE_MARK]
                ybuf_free = NONE_MARK
                ev_free = {"pS0": NONE_MARK, "pS1": NONE_MARK}
                for tb in range(NTB):
                    sl = tb % 2
                    for cc in range(NCH):
                        p.dma("sync", otb[:, sl * 4096 + cc * TB: sl * 4096 + (cc + 1) * TB],
                              io["OT"][cc, :, tb * TB:(tb + 1) * TB],
                              after=[otb_free[sl], ot_all], inc=sXL)
                    ot_tb = sXL.mark()
                    for jc in range(NCH):
                        bank = [pS0, pS1][jc % 2]
                        bkey = "pS0" if jc % 2 == 0 else "pS1"
                        for cc in range(NCH):
                            p.do("tensor", (lambda jc=jc, cc=cc, sl=sl, bank=bank: (lambda t: t.matmul(
                                bank[:, :],
                                woB[:, cc * 1024 + jc * 128: cc * 1024 + (jc + 1) * 128],
                                otb[:, sl * 4096 + cc * TB: sl * 4096 + (cc + 1) * TB],
                                start=(cc == 0), stop=(cc == NCH - 1))))(),
                                after=[wo_rdy, ot_tb, ev_free[bkey], ybuf_free],
                                inc=(sMM if cc == NCH - 1 else None))
                        p.do("vector", (lambda jc=jc, bank=bank: (lambda v: v.tensor_copy(
                            ybuf[:, jc * TB:(jc + 1) * TB], bank[:, :])))(),
                            after=[sMM.mark()], inc=sEV)
                        ev_free[bkey] = sEV.mark()
                    y_done = sEV.mark()
                    otb_free[sl] = sMM.mark()

                    if not final:
                        def add_src(jc, dst, xr):
                            def f(v):
                                return v.tensor_add(
                                    dst, ybuf[:, jc * TB:(jc + 1) * TB], xr)
                            return f
                        m = emit_epilogue_tb(p, io, c, A, po[0], po[1], tb,
                                             add_src, src_after=[y_done])
                    else:
                        def add_src4(jc, t4, dst, xr):
                            def f(v):
                                return v.tensor_add(
                                    dst,
                                    ybuf[:, jc * TB + t4 * 128: jc * TB + (t4 + 1) * 128],
                                    xr)
                            return f
                        m = emit_final_tb(p, io, c, A, po[0], po[1], tb,
                                          add_src4, src_after=[y_done])
                    ybuf_free = m
                p.fence()
                p.flush("l1attn")



def emit_l1ffn(p, io, c, final=False):
    nc = p.nc
    sXL = p.sem("sXL")
    sWT = p.sem("sWT")
    sYP = p.sem("sYP")
    sGT = p.sem("sGT")
    sGE = p.sem("sGE")
    sHM = p.sem("sHM")
    sLD = p.sem("sLD")

    from contextlib import ExitStack
    with ExitStack() as ctx:
        xln = ctx.enter_context(nc.sbuf_tensor(_nm("n_xln"), [128, NCH * TB], BF16))
        wa = ctx.enter_context(nc.sbuf_tensor(_nm("n_wa"), [128, 8192], BF16))
        wb = ctx.enter_context(nc.sbuf_tensor(_nm("n_wb"), [128, 8192], BF16))
        xta = ctx.enter_context(nc.sbuf_tensor(_nm("n_xta"), [128, NCH * TB], BF16))
        xtb = ctx.enter_context(nc.sbuf_tensor(_nm("n_xtb"), [128, NCH * TB], BF16))
        mbc8 = ctx.enter_context(nc.sbuf_tensor(_nm("n_mbc8"), [128, NE * TB], BF16))
        rwt = ctx.enter_context(nc.sbuf_tensor(_nm("n_rwt"), [128, NCH * NE], BF16))
        rwin = ctx.enter_context(nc.sbuf_tensor(_nm("n_rwin"), [NE, DM], F32))
        rbsb = ctx.enter_context(nc.sbuf_tensor(_nm("n_rb"), [NE, 1], F32))
        ebsb = ctx.enter_context(nc.sbuf_tensor(_nm("n_eb"), [NE, DM], BF16))
        glog = ctx.enter_context(nc.sbuf_tensor(_nm("n_glog"), [NE, TB], F32))
        gt = ctx.enter_context(nc.sbuf_tensor(_nm("n_gt"), [128, 4 * NE], F32))
        gms = ctx.enter_context(nc.sbuf_tensor(_nm("n_gms"), [128, NE], F32))
        grows = ctx.enter_context(nc.sbuf_tensor(_nm("n_grows"), [128, 8], F32))
        mt = ctx.enter_context(nc.sbuf_tensor(_nm("n_mt"), [128, 4 * NE], F32))
        mT = ctx.enter_context(nc.sbuf_tensor(_nm("n_mT"), [NE, TB], BF16))
        pb = [ctx.enter_context(nc.psum_tensor(_nm(f"n_pb{i}"), [128, 512], F32))
              for i in range(NE)]
        A = alloc_epi(nc, ctx, final=final)

        bank_add = [NONE_MARK] * NE
        p.dma("sync", rwin[:, :], io["L1_ffn_router_w"][:, :], inc=sLD)
        p.dma("gpsimd", rbsb[:, :],
              io["L1_ffn_router_b"].rearrange("(e one) -> e one", one=1), inc=sLD)
        p.dma("gpsimd", ebsb[:, :], io["L1_ffn_expert_b"][:, :], inc=sLD)
        ld1 = sLD.mark()
        for cc in range(NCH):
            p.do("tensor", (lambda cc=cc: (lambda t: t.transpose(
                pb[0][:, cc * NE:(cc + 1) * NE],
                rwin[:, cc * 128:(cc + 1) * 128],
                c.ident_f32[0:NE, 0:NE])))(),
                after=[ld1, c.ready], inc=(sGT if cc == NCH - 1 else None))
        p.do("vector", lambda v: v.tensor_copy(rwt[:, :], pb[0][:, 0:NCH * NE]),
             after=[sGT.mark()], inc=sGE)
        rwt_done = sGE.mark()
        bank_add[0] = sGE.mark()

        wt_ready = {}
        xln_free = NONE_MARK
        glog_free = NONE_MARK
        mbc_free = NONE_MARK
        mT_free = NONE_MARK
        xt_free = [NONE_MARK, NONE_MARK]

        def load_w(kk, after):
            e = kk % NE
            buf = wa if kk % 2 == 0 else wb
            src = io["W1T_l1"][e].rearrange("cc p j -> p cc j")
            p.dma("gpsimd", buf.rearrange("p (cc j) -> p cc j", cc=NCH),
                  src, after=after, inc=sWT)
            return sWT.count

        wt_ready[0] = load_w(0, [])
        wt_ready[1] = load_w(1, [])

        for tb in range(NTB):
            ts, te = tb * TB, (tb + 1) * TB
            for cc in range(NCH):
                p.dma("sync", xln[:, cc * TB:(cc + 1) * TB],
                      io["XLN"][cc, :, ts:te], after=[xln_free], inc=sXL)
            xh_done = sXL.mark()

            for cc in range(NCH):
                p.do("tensor", (lambda cc=cc: (lambda t: t.matmul(
                    pb[7][0:NE, :], rwt[:, cc * NE:(cc + 1) * NE],
                    xln[:, cc * TB:(cc + 1) * TB],
                    start=(cc == 0), stop=(cc == NCH - 1))))(),
                    after=[xh_done, rwt_done, bank_add[7]],
                    inc=(sGT if cc == NCH - 1 else None))
            p.do("vector", lambda v: v.tensor_scalar_add(
                glog[:, :], pb[7][0:NE, :], rbsb[:, 0:1]),
                after=[sGT.mark(), glog_free], inc=sGE)
            gl_done = sGE.mark()
            tp_free = sGE.mark()
            for tl in range(4):
                p.do("tensor", (lambda tl=tl: (lambda t: t.transpose(
                    pb[7][:, tl * NE:(tl + 1) * NE],
                    glog[:, tl * 128:(tl + 1) * 128],
                    c.ident_f32[0:NE, 0:NE])))(),
                    after=[gl_done, tp_free], inc=sGT)
                p.do("vector", (lambda tl=tl: (lambda v: v.tensor_copy(
                    gt[:, tl * NE:(tl + 1) * NE],
                    pb[7][:, tl * NE:(tl + 1) * NE])))(),
                    after=[sGT.mark()], inc=sGE)
                tp_free = sGE.mark()
            gt_all = sGE.mark()
            for tl in range(4):
                gtc = gt[:, tl * NE:(tl + 1) * NE]
                mx = grows[:, 0:1]; nmx = grows[:, 1:2]
                ss = grows[:, 2:3]; rs = grows[:, 3:4]
                p.do("vector", (lambda gtc=gtc, mx=mx: (lambda v: v.reduce_max(
                    mx, gtc, AXX)))(), after=[sGE.mark()], inc=sGE)
                p.do("vector", (lambda mx=mx, nmx=nmx: (lambda v: v.tensor_scalar_mul(
                    nmx, mx, -1.0)))(), after=[sGE.mark()], inc=sGE)
                p.do("scalar", (lambda gtc=gtc, nmx=nmx: (lambda s: s.activation(
                    gms[:, :], gtc, AF.Exp, bias=nmx)))(),
                    after=[sGE.mark()], inc=sGE)
                p.do("vector", (lambda ss=ss: (lambda v: v.reduce_sum(
                    ss, gms[:, :], AXX)))(), after=[sGE.mark()], inc=sGE)
                p.do("vector", (lambda ss=ss, rs=rs: (lambda v: v.reciprocal(
                    rs, ss)))(), after=[sGE.mark()], inc=sGE)
                p.do("vector", (lambda gtc=gtc, mx=mx: (lambda v: v.tensor_single_scalar(
                    gms[:, :], gtc, mx, ALU.is_ge)))(),
                    after=[sGE.mark()], inc=sGE)
                p.do("vector", (lambda tl=tl, rs=rs: (lambda v: v.tensor_scalar(
                    mt[:, tl * NE:(tl + 1) * NE], gms[:, :], rs, 0.5,
                    ALU.mult, ALU.mult)))(),
                    after=[sGE.mark()], inc=sGE)
                p.do("tensor", (lambda tl=tl: (lambda t: t.transpose(
                    pb[7][0:NE, tl * 128:(tl + 1) * 128],
                    mt[:, tl * NE:(tl + 1) * NE], c.ident_f32[:, :])))(),
                    after=[sGE.mark(), gt_all, mT_free], inc=sGT)
            glog_free = sGT.mark()
            p.do("vector", lambda v: v.tensor_copy(mT[0:NE, :], pb[7][0:NE, 0:TB]),
                 after=[sGT.mark()], inc=sGE)
            bank_add[7] = sGE.mark()
            mT_done = sGE.mark()

            mbc_bank_free = [bank_add[0], bank_add[1]]
            for e in range(NE):
                bk = pb[e % 2]
                p.do("tensor", (lambda e=e, bk=bk: (lambda t: t.matmul(
                    bk[:, :], c.sel_bf[:, e * 128:(e + 1) * 128],
                    mT[0:NE, :], start=True, stop=True)))(),
                    after=[mT_done, mbc_bank_free[e % 2], mbc_free],
                    inc=sGT)
                p.do("vector", (lambda e=e, bk=bk: (lambda v: v.tensor_copy(
                    mbc8[:, e * TB:(e + 1) * TB], bk[:, :])))(),
                    after=[sGT.mark()], inc=sGE)
                mbc_bank_free[e % 2] = sGE.mark()
            bank_add[0] = mbc_bank_free[0]
            bank_add[1] = mbc_bank_free[1]
            mbc_done = sGE.mark()

            for e in range(NE):
                kk = tb * NE + e
                buf = wa if kk % 2 == 0 else wb
                wready = (sWT, wt_ready[kk])
                xt = xta if e % 2 == 0 else xtb
                for cc in range(NCH):
                    p.do("vector", (lambda cc=cc, e=e, xt=xt: (lambda v: v.tensor_mul(
                        xt[:, cc * TB:(cc + 1) * TB],
                        xln[:, cc * TB:(cc + 1) * TB],
                        mbc8[:, e * TB:(e + 1) * TB])))(),
                        after=[mbc_done, xh_done, xt_free[e % 2]],
                        inc=sHM)
                xt_done = sHM.mark()
                for jc in range(NCH):
                    for dc in range(NCH):
                        p.do("tensor", (lambda jc=jc, dc=dc, e=e, buf=buf, xt=xt: (lambda t: t.matmul(
                            pb[jc][:, :],
                            buf[:, dc * 1024 + jc * 128: dc * 1024 + (jc + 1) * 128],
                            xt[:, dc * TB:(dc + 1) * TB],
                            start=(e == 0 and dc == 0),
                            stop=(e == NE - 1 and dc == NCH - 1))))(),
                            after=[xt_done, wready,
                                   (bank_add[jc] if e == 0 else None)],
                            inc=(sYP if (jc == NCH - 1 and dc == NCH - 1) else None))
                    if e == 0:
                        p.do("tensor", (lambda jc=jc: (lambda t: t.matmul(
                            pb[jc][:, :], ebsb[:, jc * 128:(jc + 1) * 128],
                            mT[0:NE, :], start=False, stop=False)))(),
                            after=[mT_done, ld1], inc=None)
                xt_free[e % 2] = sYP.mark()
                if kk + 2 < NTB * NE:
                    wt_ready[kk + 2] = load_w(kk + 2, [sYP.mark()])
            y_done = sYP.mark()
            xt_free = [y_done, y_done]
            xln_free = y_done
            mT_free = y_done
            mbc_free = y_done

            if not final:
                def add_src(jc, dst, xr):
                    def f(v):
                        return v.tensor_add(dst, pb[jc][:, :], xr)
                    return f
                m = emit_epilogue_tb(p, io, c, A, pb[0], pb[1], tb, add_src,
                                     src_after=[y_done])
            else:
                def add_src4(jc, t4, dst, xr):
                    def f(v):
                        return v.tensor_add(
                            dst, pb[jc][:, t4 * 128:(t4 + 1) * 128], xr)
                    return f
                m = emit_final_tb(p, io, c, A, pb[0], pb[1], tb, add_src4,
                                  src_after=[y_done])
            for jc in range(NE):
                bank_add[jc] = m
    p.fence()
    p.flush("l1ffn")



def emit_final(p, io, c):
    nc = p.nc
    sXL = p.sem("sXL")
    sMM = p.sem("sMM")
    sEV = p.sem("sEV")
    sST = p.sem("sST")
    with (
        nc.sbuf_tensor(_nm("z_in"), [128, 2 * 4096], F32) as zin,
        nc.sbuf_tensor(_nm("z_out"), [128, 1024], F32) as zout,
        nc.psum_tensor(_nm("z_p0"), [128, 512], F32) as zp0,
        nc.psum_tensor(_nm("z_p1"), [128, 512], F32) as zp1,
    ):
        in_free = [NONE_MARK, NONE_MARK]
        bank_free = [NONE_MARK, NONE_MARK]
        out_free = NONE_MARK
        for tb in range(NTB):
            sl = tb % 2
            for cc in range(NCH):
                p.dma("sync", zin[:, sl * 4096 + cc * TB: sl * 4096 + (cc + 1) * TB],
                      io["XRES"][cc, :, tb * TB:(tb + 1) * TB],
                      after=[in_free[sl]], inc=sXL)
            ld = sXL.mark()
            for t4 in range(4):
                for cc in range(NCH):
                    bk = [zp0, zp1][cc % 2]
                    p.do("tensor", (lambda cc=cc, t4=t4, sl=sl, bk=bk: (lambda t: t.transpose(
                        bk[:, 0:128],
                        zin[:, sl * 4096 + cc * TB + t4 * 128: sl * 4096 + cc * TB + (t4 + 1) * 128],
                        c.ident_f32[:, :])))(),
                        after=[ld, bank_free[cc % 2], c.ready], inc=sMM)
                    p.do("vector", (lambda cc=cc, bk=bk: (lambda v: v.tensor_copy(
                        zout[:, cc * 128:(cc + 1) * 128], bk[:, 0:128])))(),
                        after=[sMM.mark(), out_free], inc=sEV)
                    bank_free[cc % 2] = sEV.mark()
                p.dma("sync", io["out"][tb * TB + t4 * 128: tb * TB + (t4 + 1) * 128, :],
                      zout[:, :], after=[sEV.mark()], inc=sST)
                out_free = sST.mark()
            in_free[sl] = sMM.mark()
    p.fence()
    p.flush("final")



def build_program(parts, xio="internal"):
    nc = bass.Bass("TRN2", target_bir_lowering=False)
    io = declare_io(nc, parts, xio=xio)
    p = Prog(nc)
    from contextlib import ExitStack
    with ExitStack() as ctx:
        c = emit_consts(p, io, ctx, need_proj=any(
            pt.startswith("l0attn") for pt in parts))
        p.flush("consts")
        for part in parts:
            if part == "prologue":
                emit_prologue(p, io, c)
            elif part == "l0ffn":
                emit_l0ffn(p, io, c, final=False)
            elif part == "l0ffn_final":
                emit_l0ffn(p, io, c, final=True)
            elif part == "l1prep":
                emit_l1prep(p, io, c)
            elif part == "l0aprep":
                emit_l0aprep(p, io, c)
            elif part == "l0attn":
                emit_l0attn(p, io, c, final=False)
            elif part == "l1attn":
                emit_l1attn(p, io, c, final=False)
            elif part == "l1ffn":
                emit_l1ffn(p, io, c, final=False)
            elif part == "l1ffn_final":
                emit_l1ffn(p, io, c, final=True)
            elif part == "final":
                emit_final(p, io, c)
        p.fence()
        p.flush("tail")
    return nc, io


def _in_maps(parts, inputs, extra=None):
    need = set()
    for part in parts:
        need |= PART_INPUTS[part.replace("_final", "")]
    xs = np.ascontiguousarray(inputs["L0_attn_x"]) if "L0_attn_x" in need else None
    maps = []
    for core in range(8):
        m = {}
        for k in sorted(need):
            if k == "L0_attn_x":
                m[k] = xs[core]
            else:
                m[k] = np.ascontiguousarray(inputs[k])
        if extra:
            for k, v in extra.items():
                m[k] = v[core]
        maps.append(m)
    return maps


DEFAULT_PARTS = ["prologue", "l0ffn", "l0aprep", "l0attn", "l0ffn",
                 "l1prep", "l1ffn", "l1attn", "l1ffn", "final"]


def kernel(**inputs):
    parts = DEFAULT_PARTS
    nc, io = build_program(parts, xio="internal")
    maps = _in_maps(parts, inputs)
    res = run_bass_kernel_spmd(nc, maps, core_ids=list(range(8)))
    return np.stack([r["out"] for r in res.results], axis=0)
